# version 02
# baseline (speedup 1.0000x reference)
import contextlib
import numpy as np
import ml_dtypes
import concourse.bass as bass
import concourse.mybir as mybir
from concourse.bass_utils import run_bass_kernel_spmd

F32 = mybir.dt.float32
BF16 = mybir.dt.bfloat16
I32 = mybir.dt.int32
AF = mybir.ActivationFunctionType
ALU = mybir.AluOpType
AX = mybir.AxisListType

L_FULL, D, E, H = 16384, 1024, 2048, 4096
EPS = 1e-5
ENGINES = ("tensor", "vector", "scalar", "gpsimd", "sync")


class Chan:
    __slots__ = ("name", "sem", "n")

    def __init__(self, name):
        self.name = name
        self.sem = None
        self.n = 0


class Inst:
    __slots__ = ("h",)

    def __init__(self, h):
        self.h = h

    def inc(self, chan, k=1):
        if k <= 0:
            return self
        if self.h is not None:
            self.h.then_inc(chan.sem, k)
        chan.n += k
        return self


class Eng:
    def __init__(self, prog, name):
        object.__setattr__(self, "_p", prog)
        object.__setattr__(self, "_n", name)

    def wait(self, chan, val=None):
        v = chan.n if val is None else val
        if self._p.cur == self._n:
            self._p.handles[self._n].wait_ge(chan.sem, v)

    def __getattr__(self, op):
        p = self._p
        nm = self._n

        def call(*a, **kw):
            if p.cur == nm:
                return Inst(getattr(p.handles[nm], op)(*a, **kw))
            return Inst(None)

        return call


class Prog:
    def __init__(self):
        self.cur = None
        self.handles = {}
        self.t = Eng(self, "tensor")
        self.v = Eng(self, "vector")
        self.s = Eng(self, "scalar")
        self.g = Eng(self, "gpsimd")
        self.y = Eng(self, "sync")


def build_nc(n_cores, tpc, dbg=None):
    nch = tpc // 128
    nst = tpc // 512
    nc = bass.Bass("TRN2", target_bir_lowering=False,
                   detect_race_conditions=False)

    def din(name, shape, dt=F32):
        return nc.dram_tensor(name, list(shape), dt, kind="ExternalInput")

    x_own = din("x_own", [tpc, D])
    x_halo = din("x_halo", [128, D])
    WiT = din("WiT", [D, 2 * E])
    WdtT = din("WdtT", [E, E])
    WBT = din("WBT", [E, E])
    WCT = din("WCT", [E, E])
    WoT = din("WoT", [E, D])
    w1T = din("w1T", [D, H])
    w2T = din("w2T", [H, D])
    ln1w_r = din("ln1w_r", [1, D])
    ln1b_r = din("ln1b_r", [1, D])
    b1_r = din("b1_r", [1, H])
    ln2w_r = din("ln2w_r", [1, H])
    ln2b_r = din("ln2b_r", [1, H])
    b2_r = din("b2_r", [1, D])
    bdt_c = din("bdt_c", [128, 16])
    alog_c = din("alog_c", [128, 16])
    bconv_c = din("bconv_c", [128, 16])
    wconv_c = din("wconv_c", [128, 64])
    ones_r = din("ones_r", [1, 128])
    ones_c = din("ones_c", [128, 1])
    idt_b = din("idt_b", [128, 128], BF16)
    idf_f = din("idf_f", [128, 128])
    halo_m = din("halo_m", [128, 1])
    prefm = din("prefm", [n_cores, 1])
    mm8 = din("mm8", [n_cores, n_cores])

    out_t = nc.dram_tensor("out", [tpc, D], F32, kind="ExternalOutput")

    y1_s = nc.dram_tensor("y1_s", [nch + 1, 128, D], F32)
    q1_s = nc.dram_tensor("q1_s", [nch + 1, 128, H], BF16)
    q2_s = nc.dram_tensor("q2_s", [nch, 128, H], BF16)
    w1q_s = nc.dram_tensor("w1q_s", [8, 128, H], BF16)
    w2q_s = nc.dram_tensor("w2q_s", [32, 128, D], BF16)
    xcb_s = nc.dram_tensor("xcb_s", [nst, 128, 16 * 512], BF16)
    slz_s = nc.dram_tensor("slz_s", [nst, 128, 16 * 512], BF16)
    alp_s = nc.dram_tensor("alp_s", [nst, 16, 128, 512], F32)
    bet_s = nc.dram_tensor("bet_s", [nst, 16, 128, 512], F32)
    x2_s = nc.dram_tensor("x2_s", [nch, 128, D], F32)
    cc_in = nc.dram_tensor("cc_in", [2, E], F32)
    cc_out = nc.dram_tensor("cc_out", [2 * n_cores, E], F32)

    dbg_t = None
    if dbg == "y1":
        dbg_t = nc.dram_tensor("dbg", [(nch + 1) * 128, D], F32, kind="ExternalOutput")
    elif dbg == "m1":
        dbg_t = nc.dram_tensor("dbg", [2 * nst * 128, 16 * 512], F32, kind="ExternalOutput")
    elif dbg == "m2":
        dbg_t = nc.dram_tensor("dbg", [2 * nst * 16 * 128 + 3 * 128, 512], F32, kind="ExternalOutput")
    elif dbg == "x2":
        dbg_t = nc.dram_tensor("dbg", [nch * 128, D], F32, kind="ExternalOutput")
    elif dbg == "q1":
        dbg_t = nc.dram_tensor("dbg", [43 * 128, H], F32, kind="ExternalOutput")
    elif dbg == "f1a":
        dbg_t = nc.dram_tensor("dbg", [8 * 128, H], F32, kind="ExternalOutput")
    elif dbg == "wdt":
        dbg_t = nc.dram_tensor("dbg", [6 * 128, 2048], F32, kind="ExternalOutput")
    elif dbg == "m2g":
        dbg_t = nc.dram_tensor("dbg", [24 * 128, 512], F32, kind="ExternalOutput")

    chans = {n: Chan(n) for n in ("LD", "ST", "PE", "DV", "AC", "CC", "WL", "WG")}

    es = contextlib.ExitStack()
    with es:
        def sbt(name, shape, dt=F32):
            return es.enter_context(nc.sbuf_tensor(name, list(shape), dt))

        WA = sbt("WA", [128, 32768], BF16)
        WB_ = sbt("WB_", [128, 32768], BF16)
        AR = sbt("AR", [128, 11776], F32)
        BCln2w = sbt("BCln2w", [128, H], BF16)
        BCln2b = sbt("BCln2b", [128, H], BF16)
        BCb1 = sbt("BCb1", [128, H], BF16)
        BCb2h = sbt("BCb2h", [128, D], BF16)
        BCln1w = sbt("BCln1w", [128, D], BF16)
        BCln1b = sbt("BCln1b", [128, D], BF16)
        SMALL = sbt("SMALL", [128, 480], F32)
        IDT = sbt("IDT", [128, 128], BF16)
        IDF = sbt("IDF", [128, 128], F32)
        PRM = sbt("PRM", [128, 128], F32)
        PS = [es.enter_context(nc.psum_tensor(f"ps{i}", [128, 512], F32))
              for i in range(4)]
        PSX = es.enter_context(nc.psum_tensor("psx", [128, 512], F32))
        PSY = es.enter_context(nc.psum_tensor("psy", [128, 512], F32))
        PT = [es.enter_context(nc.psum_tensor(f"pt{i}", [128, 1024], BF16))
              for i in range(2)]

        for cn in chans:
            chans[cn].sem = es.enter_context(nc.semaphore(cn))

        block = es.enter_context(nc.Block())
        P = Prog()

        def arf(o, w):
            return AR[:, o:o + w]

        def arb(o, w):
            return AR[:, o:o + (w + 1) // 2].bitcast(BF16)[:, :w]

        def ari(o, w):
            return AR[:, o:o + w].bitcast(I32)[:, :w]

        def smc(o, w=1):
            return SMALL[:, o:o + w]

        C = chans

        ONES_ROW = SMALL[0:1, 0:128]

        def emit(P):
            LD, ST, PE, DV, AC, CCh, WL, WG = (C["LD"], C["ST"], C["PE"], C["DV"],
                                               C["AC"], C["CC"], C["WL"], C["WG"])
            t, v, s, g, y = P.t, P.v, P.s, P.g, P.y

            def vfence(inst):
                inst.inc(DV, 1)
                v.wait(DV)

            def ln_rstd(src, width, cb):
                nj = width // 512
                last = None
                for j in range(nj):
                    last = v.bn_stats(smc(cb + j * 6, 6), src(j))
                vfence(last)
                vfence(v.bn_aggr(smc(cb + 48, 2), smc(cb, 6 * nj)))
                vfence(v.tensor_scalar(smc(cb + 50), smc(cb + 49), EPS, None, ALU.add))
                vfence(v.tensor_scalar(smc(cb + 55), smc(cb + 50), -0.5, None, ALU.mult))
                iv = smc(cb + 50).bitcast(I32)
                iy = smc(cb + 51).bitcast(I32)
                vfence(v.tensor_scalar(iy, iv, 1, None, ALU.arith_shift_right))
                vfence(v.tensor_scalar(iy, iy, -1, None, ALU.bitwise_xor))
                vfence(v.tensor_scalar(iy, iy, 0x5F3759E0, None, ALU.add))
                for _ in range(3):
                    s.wait(DV)
                    s.activation(smc(cb + 53), smc(cb + 51), AF.Square).inc(AC, 1)
                    v.wait(AC)
                    vfence(v.tensor_scalar(smc(cb + 54), smc(cb + 53), smc(cb + 55),
                                           1.5, ALU.mult, ALU.add))
                    vfence(v.tensor_scalar(smc(cb + 51), smc(cb + 54), smc(cb + 51),
                                           None, ALU.mult))
                vfence(v.tensor_scalar(smc(cb + 52), smc(cb + 51), smc(cb + 48),
                                       -1.0, ALU.mult, ALU.mult))
                return smc(cb + 51), smc(cb + 52)

            def slab2(ten, i):
                return ten.ap()[i]

            y.dma_start(out=IDT[:, :], in_=idt_b.ap()).inc(LD, 16)
            y.dma_start(out=IDF[:, :], in_=idf_f.ap()).inc(LD, 16)
            y.dma_start(out=PRM[:, 0:16], in_=bdt_c.ap()).inc(LD, 16)
            y.dma_start(out=PRM[:, 16:32], in_=alog_c.ap()).inc(LD, 16)
            y.dma_start(out=PRM[:, 32:48], in_=bconv_c.ap()).inc(LD, 16)
            y.dma_start(out=PRM[:, 48:112], in_=wconv_c.ap()).inc(LD, 16)
            y.dma_start(out=PRM[:, 112:113], in_=halo_m.ap()).inc(LD, 16)
            y.dma_start(out=PRM[:, 113:114], in_=ones_c.ap()).inc(LD, 16)
            y.dma_start(out=SMALL[0:1, 0:128], in_=ones_r.ap()).inc(LD, 16)
            ld_p0 = LD.n

            o_ln1w, o_ln1b, o_ln2w, o_ln2b, o_b2, o_b1 = 0, 1024, 2048, 6144, 10240, 11264 - 4096
            y.dma_start(out=AR[0:1, o_ln1w:o_ln1w + D], in_=ln1w_r.ap()).inc(LD, 16)
            y.dma_start(out=AR[0:1, o_ln1b:o_ln1b + D], in_=ln1b_r.ap()).inc(LD, 16)
            y.dma_start(out=AR[0:1, o_b2:o_b2 + D], in_=b2_r.ap()).inc(LD, 16)
            y.dma_start(out=AR[0:1, o_ln2w:o_ln2w + H], in_=b1_r.ap()).inc(LD, 16)
            ld_rows1 = LD.n

            def bcast_row(dst_bf, src_off, w, scale, ldv):
                for j in range(w // 512):
                    t.wait(LD, ldv)
                    t.matmul(PSX[:, :], ONES_ROW,
                             AR[0:1, src_off + j * 512:src_off + (j + 1) * 512],
                             start=True, stop=True).inc(PE, 1)
                    v.wait(PE)
                    if scale != 1.0:
                        v.tensor_scalar(dst_bf[:, j * 512:(j + 1) * 512],
                                        PSX[:, :], scale, None, ALU.mult).inc(DV, 1)
                    else:
                        v.tensor_copy(dst_bf[:, j * 512:(j + 1) * 512],
                                      PSX[:, :]).inc(DV, 1)
                    t.wait(DV)

            bcast_row(BCln1w, o_ln1w, D, 1.0, ld_rows1)
            bcast_row(BCln1b, o_ln1b, D, 1.0, ld_rows1)
            bcast_row(BCb2h, o_b2, D, 0.5, ld_rows1)
            bcast_row(BCb1, o_ln2w, H, 1.0, ld_rows1)
            y.wait(PE, C["PE"].n)
            y.dma_start(out=AR[0:1, o_ln2w:o_ln2w + H], in_=ln2w_r.ap()).inc(LD, 16)
            y.dma_start(out=AR[0:1, o_ln2b:o_ln2b + H], in_=ln2b_r.ap()).inc(LD, 16)
            ld_rows2 = LD.n
            bcast_row(BCln2w, o_ln2w, H, 1.0, ld_rows2)
            bcast_row(BCln2b, o_ln2b, H, 1.0, ld_rows2)
            pe_bcast_done = PE.n
            dv_bcast_done = DV.n

            s.wait(LD, ld_p0)
            s.activation(PRM[:, 16:32], PRM[:, 16:32], AF.Exp).inc(AC, 1)
            v.wait(AC)
            v.tensor_scalar(PRM[:, 16:32], PRM[:, 16:32], -1.0, None, ALU.mult).inc(DV, 1)

            def quant_weight(wT, nchunks, wcols, img, pair_off):
                first_ld_gate = max(pe_bcast_done, 0)
                dv_gate = []
                for ch in range(nchunks):
                    buf = (ch % 2) * 4096
                    if ch < 2:
                        y.wait(PE, pe_bcast_done)
                        y.wait(DV, dv_bcast_done)
                    else:
                        y.wait(DV, dv_gate[ch - 2])
                    y.dma_start(out=arf(buf, wcols),
                                in_=wT.ap()[ch * 128:(ch + 1) * 128, :]).inc(LD, 16)
                    v.wait(LD)
                    v.tensor_reduce(smc(140 + ch), arf(buf, wcols), AX.X, ALU.add,
                                    apply_absolute_value=True).inc(DV, 1)
                    dv_gate.append(DV.n)
                vfence(v.tensor_copy(arf(8192, 512), arf(0, 512)))
                v.tensor_reduce(smc(139), SMALL[:, 140:140 + nchunks], AX.X, ALU.add).inc(DV, 1)
                t.wait(DV)
                t.matmul(PSX[0:1, 0:1], smc(139), PRM[:, 113:114],
                         start=True, stop=True).inc(PE, 1)
                v.wait(PE)
                sc_c = SMALL[0:1, 138:139]
                sc_s = SMALL[0:1, 137:138]
                vfence(v.tensor_scalar(sc_c, PSX[0:1, 0:1],
                                       1.0 / (nchunks * 128 * wcols), None, ALU.mult))
                vfence(v.tensor_scalar(sc_c, sc_c, 1e-5, None, ALU.max))
                v.reciprocal(sc_s, sc_c).inc(DV, 1)
                t.wait(DV)
                t.matmul(PSX[:, 0:2], ONES_ROW, SMALL[0:1, 137:139],
                         start=True, stop=True).inc(PE, 1)
                v.wait(PE)
                v.tensor_copy(smc(pair_off, 2), PSX[:, 0:2]).inc(DV, 1)
                v.wait(DV)
                t.wait(DV)
                scale_bc = smc(pair_off)
                nh = max(1, wcols // 2048)
                hw = wcols // nh
                st_gate = []
                dv_pass1 = DV.n
                half_st = 0
                for ch in range(nchunks):
                    buf = (ch % 2) * 4096
                    if ch >= 2:
                        y.wait(ST, st_gate[ch - 2])
                    else:
                        y.wait(DV, dv_pass1)
                    y.dma_start(out=arf(buf, wcols),
                                in_=wT.ap()[ch * 128:(ch + 1) * 128, :]).inc(LD, 16)
                    v.wait(LD)
                    for hhh in range(nh):
                        srcw = arf(buf + hhh * hw, hw)
                        v.tensor_scalar(ari(8192, hw), srcw, scale_bc, None, ALU.mult)
                        v.tensor_copy(arf(buf + hhh * hw, hw), ari(8192, hw))
                        v.wait(ST, half_st)
                        v.tensor_scalar(arb(10240, hw), arf(buf + hhh * hw, hw),
                                        -1.0, 1.0, ALU.max, ALU.min).inc(DV, 1)
                        g.wait(DV)
                        g.dma_start(out=img.ap()[ch][:, hhh * hw:(hhh + 1) * hw],
                                    in_=arb(10240, hw)).inc(ST, 16)
                        half_st = ST.n
                    st_gate.append(ST.n)
                y.wait(ST, ST.n)

            quant_weight(w1T, 8, H, w1q_s, 172)
            if dbg == "q1":
                g.wait(DV, DV.n)
                g.dma_start(out=dbg_t.ap()[42 * 128:43 * 128, 0:64],
                            in_=SMALL[:, 128:192]).inc(ST, 16)
                v.wait(ST, ST.n)
            quant_weight(w2T, 32, D, w2q_s, 174)
            c1_col = smc(173)
            c2_col = smc(175)

            for ch in range(8):
                y.dma_start(out=WA[:, ch * H:(ch + 1) * H], in_=w1q_s.ap()[ch]).inc(WL, 16)
            wl_f1a = WL.n
            for ch in range(32):
                y.dma_start(out=WB_[:, ch * D:(ch + 1) * D], in_=w2q_s.ap()[ch]).inc(WL, 16)
            wl_f1b = WL.n

            def ffn_A(src_kind, q_img, nchunks, sout2_base, wl_gate):
                xfree = [None, None]
                ptfree = [0, 0]
                bankfree = [0, 0]
                q2free = 0
                for c in range(nchunks):
                    xbuf = (c % 2) * 1024
                    if xfree[c % 2] is not None:
                        y.wait(AC, xfree[c % 2])
                    if src_kind == "x":
                        src = x_halo.ap() if c == 0 else x_own.ap()[(c - 1) * 128:c * 128, :]
                    else:
                        src = slab2(x2_s, c)
                    y.dma_start(out=arf(xbuf, D), in_=src).inc(LD, 16)
                    v.wait(LD)
                    rstd1, nmb1 = ln_rstd(lambda j: arf(xbuf + j * 512, 512), D, 200)
                    s.wait(DV)
                    s.activation(arb(2048, D), arf(xbuf, D), AF.Identity,
                                 bias=nmb1, scale=rstd1).inc(AC, 1)
                    xfree[c % 2] = AC.n
                    v.wait(AC)
                    v.tensor_tensor(arb(2560, D), arb(2048, D), BCln1w[:, :], ALU.mult)
                    v.tensor_tensor(arb(3072, D), arb(2560, D), BCln1b[:, :], ALU.add)
                    vfence(v.tensor_reduce(smc(229), arb(3072, D), AX.X, ALU.max,
                                           apply_absolute_value=True))
                    vfence(v.tensor_scalar(smc(229), smc(229), EPS, None, ALU.max))
                    vfence(v.reciprocal(smc(230), smc(229)))
                    v.tensor_scalar(smc(231), smc(229), c1_col, 1.0 / 127.0,
                                    ALU.mult, ALU.mult).inc(DV, 1)
                    v.tensor_scalar(ari(3584, D), arb(3072, D), 127.0, smc(230),
                                    ALU.mult, ALU.mult)
                    v.tensor_copy(arb(4608, D), ari(3584, D)).inc(DV, 1)
                    for k in range(8):
                        bk = k % 2
                        t.wait(DV) if k == 0 else None
                        t.wait(DV, ptfree[bk]) if k >= 2 else None
                        t.transpose(PT[bk][:, 0:128],
                                    arb(4608, D)[:, k * 128:(k + 1) * 128],
                                    IDT[:, :]).inc(PE, 1)
                        v.wait(PE)
                        v.tensor_copy(arb(5120, D)[:, k * 128:(k + 1) * 128],
                                      PT[bk][:, 0:128]).inc(DV, 1)
                        ptfree[bk] = DV.n
                    t.wait(WL, wl_gate)
                    s.wait(ST, q2free)
                    s.wait(DV)
                    for n in range(8):
                        bank = n % 2
                        t.wait(AC, bankfree[bank])
                        for k in range(8):
                            t.matmul(PS[bank][:, :],
                                     arb(5120, D)[:, k * 128:(k + 1) * 128],
                                     WA[:, k * H + n * 512:k * H + (n + 1) * 512],
                                     start=(k == 0), stop=(k == 7)).inc(PE, 1 if k == 7 else 0)
                        s.wait(PE)
                        s.activation(arf(10752, 512), PS[bank][:, :], AF.Identity,
                                     scale=smc(231)).inc(AC, 1)
                        bankfree[bank] = AC.n
                        v.wait(AC)
                        v.tensor_tensor(arf(11264, 512), arf(10752, 512),
                                        BCb1[:, n * 512:(n + 1) * 512], ALU.add).inc(DV, 1)
                        s.wait(DV)
                        s.activation(arb(5632, H)[:, n * 512:(n + 1) * 512],
                                     arf(11264, 512), AF.Gelu).inc(AC, 1)
                    v.wait(AC)
                    rstd2, nmb2 = ln_rstd(lambda j: arb(5632, H)[:, j * 512:(j + 1) * 512],
                                          H, 340)
                    s.wait(DV)
                    s.activation(arb(5632, H), arb(5632, H), AF.Identity,
                                 bias=nmb2, scale=rstd2).inc(AC, 1)
                    v.wait(AC)
                    v.tensor_tensor(arb(7680, H), arb(5632, H), BCln2w[:, :], ALU.mult)
                    v.tensor_tensor(arb(5632, H), arb(7680, H), BCln2b[:, :], ALU.add)
                    vfence(v.tensor_reduce(smc(298), arb(5632, H), AX.X, ALU.max,
                                           apply_absolute_value=True))
                    vfence(v.tensor_scalar(smc(298), smc(298), EPS, None, ALU.max))
                    vfence(v.reciprocal(smc(299), smc(298)))
                    v.tensor_scalar(SMALL[:, sout2_base + c:sout2_base + c + 1],
                                    smc(298), c2_col, 1.0 / 127.0,
                                    ALU.mult, ALU.mult).inc(DV, 1)
                    last = None
                    for qq in range(4):
                        v.tensor_scalar(ari(9728, 1024),
                                        arb(5632, H)[:, qq * 1024:(qq + 1) * 1024],
                                        127.0, smc(299), ALU.mult, ALU.mult)
                        last = v.tensor_copy(arb(3584, H)[:, qq * 1024:(qq + 1) * 1024],
                                             ari(9728, 1024))
                    last.inc(DV, 1)
                    for kk in range(8):
                        bk = kk % 2
                        t.wait(DV) if kk == 0 else None
                        t.wait(DV, ptfree[bk]) if kk >= 2 else None
                        for j in range(4):
                            k = kk * 4 + j
                            t.transpose(PT[bk][:, j * 128:(j + 1) * 128],
                                        arb(3584, H)[:, k * 128:(k + 1) * 128],
                                        IDT[:, :]).inc(PE, 1 if j == 3 else 0)
                        v.wait(PE)
                        v.tensor_copy(arb(5632, H)[:, kk * 512:(kk + 1) * 512],
                                      PT[bk][:, 0:512]).inc(DV, 1)
                        ptfree[bk] = DV.n
                    g.wait(DV)
                    g.dma_start(out=slab2(q_img, c), in_=arb(5632, H)).inc(ST, 16)
                    q2free = ST.n
                y.wait(ST, ST.n)

            def ffn_B(src_kind, q_img, nchunks, sout2_base, dst_kind, wl_gate):
                qfree = [0, 0]
                xfree = [0, 0]
                mmfree = [0, 0]
                for c in range(nchunks):
                    qbuf = 3584 if c % 2 == 0 else 5632
                    xbuf = (c % 2) * 1024
                    y.wait(PE, qfree[c % 2])
                    y.dma_start(out=arb(qbuf, H), in_=slab2(q_img, c)).inc(LD, 16)
                    if src_kind == "x":
                        src = x_halo.ap() if c == 0 else x_own.ap()[(c - 1) * 128:c * 128, :]
                    else:
                        src = slab2(x2_s, c)
                    y.wait(DV, xfree[c % 2])
                    y.dma_start(out=arf(xbuf, D), in_=src).inc(LD, 16)
                    v.wait(LD)
                    v.tensor_scalar(smc(273), SMALL[:, sout2_base + c:sout2_base + c + 1],
                                    0.5, None, ALU.mult).inc(DV, 1)
                    t.wait(WL, wl_gate)
                    t.wait(DV)
                    for n in range(2):
                        bank = 2 + n
                        t.wait(DV, mmfree[n])
                        for k in range(32):
                            t.matmul(PS[bank][:, :],
                                     arb(qbuf, H)[:, k * 128:(k + 1) * 128],
                                     WB_[:, k * D + n * 512:k * D + (n + 1) * 512],
                                     start=(k == 0), stop=(k == 31)).inc(PE, 1 if k == 31 else 0)
                        s.wait(PE)
                        s.activation(arf(2048, 512), PS[bank][:, :], AF.Identity,
                                     scale=smc(273)).inc(AC, 1)
                        v.wait(AC)
                        v.tensor_tensor(arf(10752, 512), arf(2048, 512),
                                        BCb2h[:, n * 512:(n + 1) * 512], ALU.add)
                        v.tensor_tensor(arf(2560 + n * 512, 512), arf(10752, 512),
                                        arf(xbuf + n * 512, 512), ALU.add).inc(DV, 1)
                        mmfree[n] = DV.n
                    qfree[c % 2] = PE.n
                    xfree[c % 2] = DV.n
                    g.wait(DV)
                    if dst_kind == "y1":
                        g.dma_start(out=slab2(y1_s, c), in_=arf(2560, D)).inc(ST, 16)
                    else:
                        g.dma_start(out=out_t.ap()[c * 128:(c + 1) * 128, :],
                                    in_=arf(2560, D)).inc(ST, 16)
                y.wait(ST, ST.n)

            ffn_A("x", q1_s, 1 if dbg == "f1a" else nch + 1, 300, wl_f1a)
            if dbg == "f1a":
                g.wait(DV, DV.n)
                g.wait(AC, AC.n)
                g.dma_start(out=dbg_t.ap()[0:128, 0:D], in_=arb(2048, D)).inc(ST, 16)
                g.dma_start(out=dbg_t.ap()[128:256, 0:D], in_=arb(3072, D)).inc(ST, 16)
                g.dma_start(out=dbg_t.ap()[256:384, 0:D], in_=arb(4608, D)).inc(ST, 16)
                g.dma_start(out=dbg_t.ap()[384:512, :], in_=arb(5632, H)).inc(ST, 16)
                g.dma_start(out=dbg_t.ap()[512:640, :], in_=arb(7680, H)).inc(ST, 16)
                g.dma_start(out=dbg_t.ap()[640:768, :], in_=arb(3584, H)).inc(ST, 16)
                g.dma_start(out=dbg_t.ap()[768:896, 0:120], in_=SMALL[:, 200:320]).inc(ST, 16)
                g.dma_start(out=dbg_t.ap()[896:1024, 0:512], in_=arf(10752, 512)).inc(ST, 16)
                y.wait(ST, ST.n)
                return
            if dbg == "q1":
                for ch in range(8):
                    g.dma_start(out=dbg_t.ap()[ch * 128:(ch + 1) * 128, :],
                                in_=w1q_s.ap()[ch]).inc(ST, 16)
                for ch in range(32):
                    g.dma_start(out=dbg_t.ap()[(8 + ch) * 128:(9 + ch) * 128, 0:D],
                                in_=w2q_s.ap()[ch]).inc(ST, 16)
                g.dma_start(out=dbg_t.ap()[40 * 128:41 * 128, :],
                            in_=slab2(q1_s, 3)).inc(ST, 16)
                g.wait(DV, DV.n)
                g.dma_start(out=dbg_t.ap()[41 * 128:42 * 128, 0:33],
                            in_=SMALL[:, 300:333]).inc(ST, 16)
                y.wait(ST, ST.n)
                return
            ffn_B("x", q1_s, nch + 1, 300, "y1", wl_f1b)

            if dbg == "y1":
                for c in range(nch + 1):
                    g.dma_start(out=dbg_t.ap()[c * 128:(c + 1) * 128, :],
                                in_=slab2(y1_s, c)).inc(ST, 16)
                y.wait(ST, ST.n)
                return

            pe_after_f1 = PE.n
            g.wait(PE, pe_after_f1)
            for ch in range(16):
                g.dma_start(out=WB_[:, ch * E:(ch + 1) * E],
                            in_=WdtT.ap()[ch * 128:(ch + 1) * 128, :]).inc(WG, 16)
            wl_wdt = WG.n
            for ch in range(8):
                g.dma_start(out=WA[:, ch * 4096:(ch + 1) * 4096],
                            in_=WiT.ap()[ch * 128:(ch + 1) * 128, :]).inc(WG, 16)
            wl_wi = WG.n

            m1_ptfree = [0, 0]
            m1_ybuf_gate = [0, 0]
            m1_xhb_gate = [0]

            def m1_ln_chunk(slab_idx, parity, lny_col):
                ybuf = 6208 + parity * 1024
                y.wait(AC, m1_ybuf_gate[parity])
                y.dma_start(out=arf(ybuf, D), in_=slab2(y1_s, slab_idx)).inc(LD, 16)
                v.wait(LD)
                rstdm, nmbm = ln_rstd(lambda j: arf(ybuf + j * 512, 512), D, 200)
                s.wait(DV)
                s.wait(PE, m1_xhb_gate[0])
                s.activation(arb(10304, D), arf(ybuf, D), AF.Identity,
                             bias=nmbm, scale=rstdm).inc(AC, 1)
                m1_ybuf_gate[parity] = AC.n
                t.wait(AC)
                for k in range(8):
                    bk = k % 2
                    if k >= 2:
                        t.wait(DV, m1_ptfree[bk])
                    t.transpose(PT[bk][:, 0:128],
                                arb(10304, D)[:, k * 128:(k + 1) * 128],
                                IDT[:, :]).inc(PE, 1)
                    v.wait(PE)
                    v.tensor_copy(arb(4160, 4096)[:, k * 512 + lny_col:k * 512 + lny_col + 128],
                                  PT[bk][:, 0:128]).inc(DV, 1)
                    m1_ptfree[bk] = DV.n
                m1_xhb_gate[0] = PE.n

            X1R = arb(0, 16 * 520)

            m1_ln_chunk(0, 0, 0)
            t.wait(WG, wl_wi)
            m1_bankfree = [("DV", 0), ("DV", 0), ("DV", 0), ("DV", 0)]

            def bank_wait(eng, bank):
                cname, val = m1_bankfree[bank]
                eng.wait(C[cname], val)

            for gi in range(16):
                bank = gi % 2
                base = gi * 520
                bank_wait(t, bank)
                for k in range(8):
                    t.matmul(PS[bank][:, 0:4],
                             WA[:, k * 4096 + gi * 128:k * 4096 + gi * 128 + 128],
                             arb(4160, 4096)[:, k * 512 + 124:k * 512 + 128],
                             start=(k == 0), stop=(k == 7)).inc(PE, 1 if k == 7 else 0)
                v.wait(PE)
                v.tensor_scalar(X1R[:, base:base + 4], PS[bank][:, 0:4],
                                PRM[:, 112:113], None, ALU.mult).inc(DV, 1)
                m1_bankfree[bank] = ("DV", DV.n)
            last = None
            for gi in range(16):
                base = gi * 520
                for k in range(4):
                    last = v.tensor_scalar(arf(10816 + gi * 4 + k, 1),
                                           X1R[:, base + k:base + k + 1],
                                           PRM[:, 48 + gi * 4 + k:49 + gi * 4 + k],
                                           None, ALU.mult)
            vfence(last)
            for gi in range(16):
                last = v.tensor_reduce(arf(10880 + gi, 1), arf(10816 + gi * 4, 4),
                                       AX.X, ALU.add)
            vfence(last)
            vfence(v.tensor_tensor(arf(10896, 16), arf(10880, 16),
                                   PRM[:, 32:48], ALU.add))
            s.wait(DV)
            s.activation(arf(10912, 16), arf(10896, 16), AF.Sigmoid).inc(AC, 1)
            v.wait(AC)
            vfence(v.tensor_tensor(arf(10928, 16), arf(10912, 16), arf(10896, 16),
                                   ALU.mult))
            XCH = SMALL[:, 386:394].bitcast(BF16)[:, 0:16]
            v.tensor_scalar(XCH, arf(10928, 16), PRM[:, 112:113], None,
                            ALU.mult).inc(DV, 1)

            m1_xco_gate = [0, 0]
            m1_szo_gate = [0, 0]
            for st_i in range(nst):
                for i in range(4):
                    m1_ln_chunk(1 + st_i * 4 + i, i % 2, i * 128)
                for gi in range(16):
                    bank = gi % 2
                    base = gi * 520
                    bank_wait(t, bank)
                    for k in range(8):
                        t.matmul(PS[bank][:, :],
                                 WA[:, k * 4096 + gi * 128:k * 4096 + gi * 128 + 128],
                                 arb(4160, 4096)[:, k * 512:(k + 1) * 512],
                                 start=(k == 0), stop=(k == 7)).inc(PE, 1 if k == 7 else 0)
                    v.wait(PE)
                    v.tensor_copy(X1R[:, base + 4:base + 516], PS[bank][:, :]).inc(DV, 1)
                    m1_bankfree[bank] = ("DV", DV.n)
                    v.tensor_scalar(arf(9280, 512), X1R[:, base + 1:base + 513],
                                    PRM[:, 48 + gi * 4:49 + gi * 4], None, ALU.mult)
                    for k in range(1, 4):
                        dsto = 9792 if k % 2 == 1 else 9280
                        srco = 9280 if k % 2 == 1 else 9792
                        v.scalar_tensor_tensor(arf(dsto, 512),
                                               X1R[:, base + 1 + k:base + 513 + k],
                                               PRM[:, 48 + gi * 4 + k:49 + gi * 4 + k],
                                               arf(srco, 512), ALU.mult, ALU.add)
                    v.tensor_scalar(arf(9280, 512), arf(9792, 512),
                                    PRM[:, 32 + gi:33 + gi], None, ALU.add).inc(DV, 1)
                    v.tensor_copy(X1R[:, base:base + 4],
                                  X1R[:, base + 512:base + 516])
                    s.wait(DV)
                    s.activation(arf(9792, 512), arf(9280, 512), AF.Sigmoid).inc(AC, 1)
                    v.wait(AC)
                    xco = 8256 + (gi % 2) * 256
                    v.wait(ST, m1_xco_gate[gi % 2])
                    v.tensor_tensor(arb(xco, 512), arf(9280, 512), arf(9792, 512),
                                    ALU.mult).inc(DV, 1)
                    g.wait(DV)
                    g.dma_start(out=xcb_s.ap()[st_i][:, gi * 512:(gi + 1) * 512],
                                in_=arb(xco, 512)).inc(ST, 16)
                    m1_xco_gate[gi % 2] = ST.n
                    bank_wait(t, bank)
                    for k in range(8):
                        t.matmul(PS[bank][:, :],
                                 WA[:, k * 4096 + 2048 + gi * 128:k * 4096 + 2048 + gi * 128 + 128],
                                 arb(4160, 4096)[:, k * 512:(k + 1) * 512],
                                 start=(k == 0), stop=(k == 7)).inc(PE, 1 if k == 7 else 0)
                    s.wait(PE)
                    s.wait(DV)
                    s.activation(arf(9792, 512), PS[bank][:, :], AF.Sigmoid).inc(AC, 1)
                    v.wait(AC)
                    szo = 8768 + (gi % 2) * 256
                    v.wait(ST, m1_szo_gate[gi % 2])
                    v.tensor_tensor(arb(szo, 512), PS[bank][:, :], arf(9792, 512),
                                    ALU.mult).inc(DV, 1)
                    m1_bankfree[bank] = ("DV", DV.n)
                    g.wait(DV)
                    g.dma_start(out=slz_s.ap()[st_i][:, gi * 512:(gi + 1) * 512],
                                in_=arb(szo, 512)).inc(ST, 16)
                    m1_szo_gate[gi % 2] = ST.n
            y.wait(ST, ST.n)

            if dbg == "m1":
                for st_i in range(nst):
                    g.dma_start(out=dbg_t.ap()[st_i * 128:(st_i + 1) * 128, :],
                                in_=xcb_s.ap()[st_i]).inc(ST, 16)
                    g.dma_start(out=dbg_t.ap()[(nst + st_i) * 128:(nst + st_i + 1) * 128, :],
                                in_=slz_s.ap()[st_i]).inc(ST, 16)
                y.wait(ST, ST.n)
                return

            pe_after_m1 = PE.n
            g.wait(PE, pe_after_m1)
            for ch in range(16):
                g.dma_start(out=WA[:, ch * E:(ch + 1) * E],
                            in_=WBT.ap()[ch * 128:(ch + 1) * 128, :]).inc(WG, 16)
            wl_wb = WG.n
            UC = [SMALL[:, 402 + i:403 + i] for i in range(16)]
            HC = [SMALL[:, 418 + i:419 + i] for i in range(16)]

            t.wait(WG, wl_wb)
            for gi in range(16):
                bank = gi % 2
                bank_wait(t, bank)
                for ke in range(16):
                    t.matmul(PS[bank][:, 0:1],
                             WA[:, ke * E + gi * 128:ke * E + gi * 128 + 128],
                             XCH[:, ke:ke + 1],
                             start=(ke == 0), stop=(ke == 15)).inc(PE, 1 if ke == 15 else 0)
                t.matmul(PSY[0:128, 0:128], ONES_ROW, ONES_ROW,
                         start=True, stop=True).inc(PE, 1)
                s.wait(PE)
                s.activation(arf(10816 + gi, 1), PS[bank][:, 0:1], AF.Exp,
                             scale=-1.0).inc(AC, 1)
                m1_bankfree[bank] = ("AC", AC.n)
            v.wait(AC)
            vfence(v.tensor_scalar(arf(10832, 16), arf(10816, 16), 1.0, None, ALU.add))
            vfence(v.reciprocal(arf(10848, 16), arf(10832, 16)))
            vfence(v.tensor_tensor(arf(10864, 16), arf(10848, 16), XCH[:, 0:16],
                                   ALU.mult))
            v.tensor_scalar(SMALL[:, 402:418], arf(10864, 16), PRM[:, 112:113],
                            None, ALU.mult)
            v.memset(SMALL[:, 418:434], 0.0).inc(DV, 1)

            if dbg == "wdt":
                g.wait(DV, DV.n)
                g.dma_start(out=dbg_t.ap()[0:128, :], in_=WB_[:, 0:E]).inc(ST, 16)
                g.dma_start(out=dbg_t.ap()[128:256, :], in_=WB_[:, 5 * E:6 * E]).inc(ST, 16)
                g.dma_start(out=dbg_t.ap()[256:384, :], in_=WB_[:, 15 * E:16 * E]).inc(ST, 16)
                g.dma_start(out=dbg_t.ap()[384:512, :], in_=WA[:, 0:E]).inc(ST, 16)
                g.dma_start(out=dbg_t.ap()[512:640, 0:128], in_=PRM[:, :]).inc(ST, 16)
                y.wait(ST, ST.n)
                return

            m2_afree = [0, 0]
            m2_bfree = [0, 0]
            m2_xcb_gate = 0
            m2_bside_dv = [0]
            for st_i in range(nst):
                y.wait(PE, m2_xcb_gate)
                y.wait(DV, m2_bside_dv[0])
                y.dma_start(out=arb(0, 8192), in_=xcb_s.ap()[st_i]).inc(LD, 16)
                for gi in range(16):
                    bank = gi % 2
                    base_a = 8200 + (gi % 2) * 512
                    base_b = 7176 + (gi % 2) * 512
                    t.wait(LD)
                    bank_wait(t, bank)
                    for ke in range(16):
                        t.matmul(PS[bank][:, :],
                                 WB_[:, ke * E + gi * 128:ke * E + gi * 128 + 128],
                                 arb(0, 8192)[:, ke * 512:(ke + 1) * 512],
                                 start=(ke == 0), stop=(ke == 15)).inc(PE, 1 if ke == 15 else 0)
                    t.matmul(PSY[0:128, 0:128], ONES_ROW, ONES_ROW,
                             start=True, stop=True).inc(PE, 1)
                    s.wait(PE)
                    s.activation(arf(4096, 512), PS[bank][:, :], AF.Identity,
                                 bias=PRM[:, 0 + gi:1 + gi]).inc(AC, 1)
                    m1_bankfree[bank] = ("AC", AC.n)
                    s.wait(DV, m2_bside_dv[0])
                    s.activation(arf(4608, 512), arf(4096, 512), AF.Abs)
                    s.activation(arf(5120, 512), arf(4608, 512), AF.Exp, scale=-1.0)
                    s.activation(arf(4608, 512), arf(5120, 512), AF.Ln, bias=1.0)
                    s.activation(arf(5120, 512), arf(4096, 512), AF.Relu).inc(AC, 1)
                    if dbg == "m2g" and st_i == 0 and gi == 3:
                        g.wait(AC, AC.n)
                        g.dma_start(out=dbg_t.ap()[20 * 128:21 * 128, :],
                                    in_=arf(4096, 512)).inc(ST, 16)
                        g.dma_start(out=dbg_t.ap()[21 * 128:22 * 128, :],
                                    in_=arf(4608, 512)).inc(ST, 16)
                        g.dma_start(out=dbg_t.ap()[22 * 128:23 * 128, :],
                                    in_=arf(5120, 512)).inc(ST, 16)
                    v.wait(AC)
                    dtinst = v.tensor_tensor(arf(5632, 512), arf(4608, 512), arf(5120, 512),
                                             ALU.add)
                    if dbg == "m2g" and st_i == 0 and gi == 3:
                        dtinst.inc(DV, 1)
                        g.wait(DV, DV.n)
                        g.dma_start(out=dbg_t.ap()[23 * 128:24 * 128, :],
                                    in_=arf(5632, 512)).inc(ST, 16)
                        v.wait(ST, ST.n)
                    v.tensor_scalar(arf(4096, 512), arf(5632, 512),
                                    PRM[:, 16 + gi:17 + gi], -1e-4, ALU.mult, ALU.min)
                    v.tensor_scalar(arf(4096, 512), arf(4096, 512), -10.0, None,
                                    ALU.max)
                    v.tensor_reduce(arf(9736 + gi * 4 + st_i, 1), arf(4096, 512),
                                    AX.X, ALU.add).inc(DV, 1)
                    s.wait(DV)
                    s.wait(ST, m2_afree[gi % 2])
                    s.activation(arf(base_a, 512), arf(4096, 512), AF.Exp).inc(AC, 1)
                    if dbg == "m2g" and st_i == 0 and gi == 3:
                        v.wait(AC, AC.n)
                        v.tensor_copy(arf(9224, 512), PS[bank][:, :]).inc(DV, 1)
                        g.wait(DV, DV.n)
                        g.wait(AC, AC.n)
                        g.dma_start(out=dbg_t.ap()[0:128, :],
                                    in_=arf(9224, 512)).inc(ST, 16)
                        g.dma_start(out=dbg_t.ap()[128:256, :],
                                    in_=arf(4096, 512)).inc(ST, 16)
                        for rr in range(16):
                            g.dma_start(out=dbg_t.ap()[(2 + rr) * 128:(3 + rr) * 128, :],
                                        in_=arb(0, 8192)[:, rr * 512:(rr + 1) * 512]).inc(ST, 16)
                        g.dma_start(out=dbg_t.ap()[18 * 128:19 * 128, 0:128],
                                    in_=WB_[:, gi * 128:gi * 128 + 128]).inc(ST, 16)
                        g.dma_start(out=dbg_t.ap()[19 * 128:20 * 128, 0:128],
                                    in_=WB_[:, 15 * E + gi * 128:15 * E + gi * 128 + 128]).inc(ST, 16)
                        t.wait(ST, ST.n)
                        v.wait(ST, ST.n)
                        s.wait(ST, ST.n)
                    bank_wait(t, bank)
                    for ke in range(16):
                        t.matmul(PS[bank][:, :],
                                 WA[:, ke * E + gi * 128:ke * E + gi * 128 + 128],
                                 arb(0, 8192)[:, ke * 512:(ke + 1) * 512],
                                 start=(ke == 0), stop=(ke == 15)).inc(PE, 1 if ke == 15 else 0)
                    t.matmul(PSY[0:128, 0:128], ONES_ROW, ONES_ROW,
                             start=True, stop=True).inc(PE, 1)
                    s.wait(PE)
                    s.activation(arf(4608, 512), PS[bank][:, :], AF.Exp,
                                 scale=-1.0).inc(AC, 1)
                    m1_bankfree[bank] = ("AC", AC.n)
                    v.wait(AC)
                    v.tensor_scalar(arf(4608, 512), arf(4608, 512), 1.0, None, ALU.add)
                    v.reciprocal(arf(5120, 512), arf(4608, 512))
                    v.tensor_copy(arf(6144, 1), UC[gi])
                    vfence(v.tensor_tensor(arf(6145, 512),
                                           arb(0, 8192)[:, gi * 512:(gi + 1) * 512],
                                           arf(5120, 512), ALU.mult))
                    v.tensor_copy(UC[gi], arf(6656, 1))
                    v.tensor_tensor(arf(6664, 512), arf(6144, 512), arf(6145, 512),
                                    ALU.add)
                    s.wait(ST, m2_bfree[gi % 2]) if False else None
                    v.wait(ST, m2_bfree[gi % 2])
                    v.scalar_tensor_tensor(arf(base_b, 512), arf(6664, 512), 0.5,
                                           arf(5632, 512), ALU.mult, ALU.mult).inc(DV, 1)
                    v.wait(AC)
                    vfence(v.tensor_tensor_scan(arf(9224, 512), arf(base_a, 512),
                                                arf(base_b, 512), HC[gi],
                                                ALU.mult, ALU.add))
                    v.tensor_copy(HC[gi], arf(9735, 1)).inc(DV, 1)
                    m2_bside_dv[0] = DV.n
                    g.wait(AC)
                    g.wait(DV)
                    g.dma_start(out=alp_s.ap()[st_i, gi], in_=arf(base_a, 512)).inc(ST, 16)
                    g.dma_start(out=bet_s.ap()[st_i, gi], in_=arf(base_b, 512)).inc(ST, 16)
                    m2_afree[gi % 2] = ST.n
                    m2_bfree[gi % 2] = ST.n
                m2_xcb_gate = PE.n
            y.wait(ST, ST.n)

            last = None
            for gi in range(16):
                last = v.tensor_reduce(arf(9800 + gi, 1), arf(9736 + gi * 4, nst),
                                       AX.X, ALU.add)
            vfence(last)
            s.wait(DV)
            s.activation(arf(9816, 16), arf(9800, 16), AF.Exp).inc(AC, 1)
            last = None
            for gi in range(16):
                last = v.tensor_copy(arf(9832 + gi, 1), HC[gi])
            vfence(last)
            t.wait(DV)
            t.wait(AC)
            t.transpose(PSX[0:16, 0:128], arf(9832, 16), IDF[:, :]).inc(PE, 1)
            v.wait(PE)
            v.tensor_copy(AR[0:16, 9856:9984], PSX[0:16, 0:128]).inc(DV, 1)
            t.transpose(PSY[0:16, 0:128], arf(9816, 16), IDF[:, :]).inc(PE, 1)
            v.wait(PE)
            v.tensor_copy(AR[0:16, 9984:10112], PSY[0:16, 0:128]).inc(DV, 1)
            g.wait(DV)
            g.dma_start(out=cc_in.ap()[0:1, :].rearrange("a (r c) -> (a r) c", c=128),
                        in_=AR[0:16, 9856:9984]).inc(ST, 16)
            g.dma_start(out=cc_in.ap()[1:2, :].rearrange("a (r c) -> (a r) c", c=128),
                        in_=AR[0:16, 9984:10112]).inc(ST, 16)
            g.wait(ST, ST.n)
            g.collective_compute(
                "AllGather", ALU.bypass,
                replica_groups=[list(range(n_cores))],
                ins=[cc_in.ap().opt()],
                outs=[cc_out.ap().opt()],
            ).inc(CCh, 1)
            g.wait(CCh, 1)
            NCr = n_cores
            cco = cc_out.ap().rearrange("(r two) e -> two r e", two=2)
            g.dma_start(out=AR[0:NCr, 0:E], in_=cco[0]).inc(ST, 16)
            g.dma_start(out=AR[0:NCr, 2048:2048 + E], in_=cco[1]).inc(ST, 16)
            y.dma_start(out=AR[0:NCr, 4096:4096 + NCr], in_=mm8.ap()).inc(LD, 16)
            y.dma_start(out=AR[0:NCr, 4200:4201], in_=prefm.ap()).inc(LD, 16)
            s.wait(ST, ST.n)
            s.wait(LD, LD.n)
            s.activation(AR[0:NCr, 6144:6144 + E], AR[0:NCr, 2048:2048 + E],
                         AF.Ln).inc(AC, 1)
            v.wait(AC)
            v.tensor_scalar(AR[0:NCr, 6144:6144 + E], AR[0:NCr, 6144:6144 + E],
                            -1e30, None, ALU.max).inc(DV, 1)
            t.wait(DV)
            for j4 in range(4):
                t.matmul(PSX[0:NCr, 0:512], AR[0:NCr, 4096:4096 + NCr],
                         AR[0:NCr, 6144 + j4 * 512:6144 + (j4 + 1) * 512],
                         start=True, stop=True).inc(PE, 1)
                s.wait(PE)
                s.activation(AR[0:NCr, 8192 + j4 * 512:8192 + (j4 + 1) * 512],
                             PSX[0:NCr, 0:512], AF.Exp).inc(AC, 1)
                t.wait(AC)
            v.wait(AC)
            v.tensor_tensor(AR[0:NCr, 8192:8192 + E], AR[0:NCr, 8192:8192 + E],
                            AR[0:NCr, 0:E], ALU.mult).inc(DV, 1)
            t.wait(DV)
            for j4 in range(4):
                t.matmul(PSX[0:1, 0:512], AR[0:NCr, 4200:4201],
                         AR[0:NCr, 8192 + j4 * 512:8192 + (j4 + 1) * 512],
                         start=True, stop=True).inc(PE, 1)
                v.wait(PE)
                v.tensor_copy(AR[0:1, 6144 + j4 * 512:6144 + (j4 + 1) * 512],
                              PSX[0:1, 0:512]).inc(DV, 1)
                t.wait(DV)
            for gi in range(16):
                bank = gi % 2
                pso = PSX if bank == 0 else PSY
                t.transpose(pso[:, 0:1],
                            AR[0:1, 6144 + gi * 128:6144 + (gi + 1) * 128],
                            IDF[0:1, 0:1]).inc(PE, 1)
                v.wait(PE)
                v.tensor_copy(SMALL[:, 448 + gi:449 + gi], pso[:, 0:1]).inc(DV, 1)
                t.wait(DV) if gi >= 1 else None
            HSC = [SMALL[:, 448 + i:449 + i] for i in range(16)]
            dv_after_combine = DV.n

            if dbg == "m2":
                for st_i in range(nst):
                    for gi in range(16):
                        g.dma_start(out=dbg_t.ap()[(st_i * 16 + gi) * 128:(st_i * 16 + gi + 1) * 128, :],
                                    in_=alp_s.ap()[st_i, gi]).inc(ST, 16)
                        g.dma_start(out=dbg_t.ap()[(nst * 16 + st_i * 16 + gi) * 128:(nst * 16 + st_i * 16 + gi + 1) * 128, :],
                                    in_=bet_s.ap()[st_i, gi]).inc(ST, 16)
                base_r = 2 * nst * 16 * 128
                g.wait(DV, DV.n)
                g.dma_start(out=dbg_t.ap()[base_r:base_r + 128, 0:16], in_=arf(9832, 16)).inc(ST, 16)
                g.dma_start(out=dbg_t.ap()[base_r + 128:base_r + 256, 0:16], in_=arf(9816, 16)).inc(ST, 16)
                g.dma_start(out=dbg_t.ap()[base_r + 256:base_r + 384, 0:16], in_=SMALL[:, 448:464]).inc(ST, 16)
                y.wait(ST, ST.n)
                return

            pe_after_m2 = PE.n
            g.wait(PE, pe_after_m2)
            for ch in range(16):
                g.dma_start(out=WB_[:, ch * E:(ch + 1) * E],
                            in_=WCT.ap()[ch * 128:(ch + 1) * 128, :]).inc(WG, 16)
            wl_wc = WG.n
            for ch in range(16):
                g.dma_start(out=WA[:, ch * 1024:(ch + 1) * 1024],
                            in_=WoT.ap()[ch * 128:(ch + 1) * 128, :]).inc(WG, 16)
            wl_wo = WG.n

            HC2 = HC
            m3_xcb_gate = None
            m3_ain_gate = [dv_after_combine, dv_after_combine]
            m3_slz_gate = [dv_after_combine, dv_after_combine]
            m3_st_gate = ST.n
            t.wait(WG, wl_wc)
            for st_i in range(nst):
                if m3_xcb_gate is None:
                    y.wait(DV, dv_after_combine)
                else:
                    y.wait(PE, m3_xcb_gate)
                y.dma_start(out=arb(0, 8192), in_=xcb_s.ap()[st_i]).inc(LD, 16)
                for gi in range(16):
                    bank = gi % 2
                    abuf = 8192 + (gi % 2) * 512
                    bbuf = 9216 + (gi % 2) * 512
                    zbuf = 10240 + (gi % 2) * 256
                    y.wait(DV, m3_ain_gate[gi % 2])
                    y.wait(ST, m3_st_gate)
                    y.dma_start(out=arf(abuf, 512), in_=alp_s.ap()[st_i, gi]).inc(LD, 16)
                    y.dma_start(out=arf(bbuf, 512), in_=bet_s.ap()[st_i, gi]).inc(LD, 16)
                    y.wait(DV, m3_slz_gate[gi % 2])
                    y.dma_start(out=arb(zbuf, 512),
                                in_=slz_s.ap()[st_i][:, gi * 512:(gi + 1) * 512]).inc(LD, 16)
                    t.wait(LD)
                    bank_wait(t, bank)
                    for ke in range(16):
                        t.matmul(PS[bank][:, :],
                                 WB_[:, ke * E + gi * 128:ke * E + gi * 128 + 128],
                                 arb(0, 8192)[:, ke * 512:(ke + 1) * 512],
                                 start=(ke == 0), stop=(ke == 15)).inc(PE, 1 if ke == 15 else 0)
                    s.wait(PE)
                    s.activation(arf(11264, 512), PS[bank][:, :], AF.Tanh).inc(AC, 1)
                    m1_bankfree[bank] = ("AC", AC.n)
                    init = HSC[gi] if st_i == 0 else HC2[gi]
                    v.wait(LD)
                    vfence(v.tensor_tensor_scan(arf(10752, 512), arf(abuf, 512),
                                                arf(bbuf, 512), init,
                                                ALU.mult, ALU.add))
                    v.tensor_copy(HC2[gi], arf(11263, 1))
                    v.wait(AC)
                    v.tensor_tensor(arf(bbuf, 512), arf(10752, 512), arf(11264, 512),
                                    ALU.mult)
                    v.tensor_tensor(arb(4096, 8192)[:, gi * 512:(gi + 1) * 512],
                                    arf(bbuf, 512), arb(zbuf, 512), ALU.mult).inc(DV, 1)
                    m3_ain_gate[gi % 2] = DV.n
                    m3_slz_gate[gi % 2] = DV.n
                t.wait(WG, wl_wo)
                t.wait(DV)
                for j in range(4):
                    bank = 2 + (j % 2)
                    y.wait(DV, DV.n)
                    y.dma_start(out=arf(8192, D),
                                in_=slab2(y1_s, 1 + st_i * 4 + j)).inc(LD, 16)
                    for n in range(2):
                        bank_wait(t, bank)
                        for ke in range(16):
                            t.matmul(PS[bank][:, :],
                                     arb(4096, 8192)[:, ke * 512 + j * 128:ke * 512 + j * 128 + 128],
                                     WA[:, ke * 1024 + n * 512:ke * 1024 + (n + 1) * 512],
                                     start=(ke == 0), stop=(ke == 15)).inc(PE, 1 if ke == 15 else 0)
                        v.wait(PE)
                        v.wait(LD)
                        v.tensor_tensor(arf(9216 + n * 512, 512), PS[bank][:, :],
                                        arf(8192 + n * 512, 512), ALU.add).inc(DV, 1)
                        m1_bankfree[bank] = ("DV", DV.n)
                    g.wait(DV)
                    g.dma_start(out=slab2(x2_s, st_i * 4 + j), in_=arf(9216, D)).inc(ST, 16)
                m3_xcb_gate = PE.n
                m3_ain_gate = [DV.n, DV.n]
                m3_st_gate = ST.n
            y.wait(ST, ST.n)

            if dbg == "x2":
                for c in range(nch):
                    g.dma_start(out=dbg_t.ap()[c * 128:(c + 1) * 128, :],
                                in_=slab2(x2_s, c)).inc(ST, 16)
                y.wait(ST, ST.n)
                return

            pe_after_m3 = PE.n
            first = True
            for ch in range(8):
                if first:
                    y.wait(PE, pe_after_m3)
                    first = False
                y.dma_start(out=WA[:, ch * H:(ch + 1) * H], in_=w1q_s.ap()[ch]).inc(WL, 16)
            wl_f2a = WL.n
            for ch in range(32):
                y.dma_start(out=WB_[:, ch * D:(ch + 1) * D], in_=w2q_s.ap()[ch]).inc(WL, 16)
            wl_f2b = WL.n
            ffn_A("x2", q2_s, nch, 318, wl_f2a)
            ffn_B("x2", q2_s, nch, 318, "out", wl_f2b)
            y.wait(ST, ST.n)

        def run_walk(eng):
            def f(handle):
                P.cur = eng
                P.handles[eng] = handle
                for cc_ in chans.values():
                    cc_.n = 0
                emit(P)
            return f

        for eng in ENGINES:
            getattr(block, eng)(run_walk(eng))

    return nc


def _prep_inputs(inputs, n_cores, tpc):
    f32 = np.float32
    x = np.ascontiguousarray(np.asarray(inputs["L0_attn_x"]).reshape(-1, D).astype(f32))
    common = {
        "WiT": np.ascontiguousarray(np.asarray(inputs["L0_attn_Wi"]).T.astype(f32)),
        "WdtT": np.ascontiguousarray(np.asarray(inputs["L0_attn_Wdt"]).T.astype(f32)),
        "WBT": np.ascontiguousarray(np.asarray(inputs["L0_attn_WB"]).T.astype(f32)),
        "WCT": np.ascontiguousarray(np.asarray(inputs["L0_attn_WC"]).T.astype(f32)),
        "WoT": np.ascontiguousarray(np.asarray(inputs["L0_attn_Wo"]).T.astype(f32)),
        "w1T": np.ascontiguousarray(np.asarray(inputs["L0_ffn_w1"]).T.astype(f32)),
        "w2T": np.ascontiguousarray(np.asarray(inputs["L0_ffn_w2"]).T.astype(f32)),
        "ln1w_r": np.asarray(inputs["L0_ffn_ln1_w"]).reshape(1, D).astype(f32),
        "ln1b_r": np.asarray(inputs["L0_ffn_ln1_b"]).reshape(1, D).astype(f32),
        "b1_r": np.asarray(inputs["L0_ffn_b1"]).reshape(1, H).astype(f32),
        "ln2w_r": np.asarray(inputs["L0_ffn_ln2_w"]).reshape(1, H).astype(f32),
        "ln2b_r": np.asarray(inputs["L0_ffn_ln2_b"]).reshape(1, H).astype(f32),
        "b2_r": np.asarray(inputs["L0_ffn_b2"]).reshape(1, D).astype(f32),
        "bdt_c": np.ascontiguousarray(
            np.asarray(inputs["L0_attn_bdt"]).reshape(16, 128).T.astype(f32)),
        "alog_c": np.ascontiguousarray(
            np.asarray(inputs["L0_attn_A_log"]).reshape(16, 128).T.astype(f32)),
        "bconv_c": np.ascontiguousarray(
            np.asarray(inputs["L0_attn_bconv"]).reshape(16, 128).T.astype(f32)),
        "wconv_c": np.ascontiguousarray(
            np.asarray(inputs["L0_attn_Wconv"]).reshape(16, 128, 4)
            .transpose(1, 0, 2).reshape(128, 64).astype(f32)),
        "ones_r": np.ones((1, 128), f32),
        "ones_c": np.ones((128, 1), f32),
        "idt_b": np.eye(128, dtype=f32).astype(ml_dtypes.bfloat16),
        "idf_f": np.eye(128, dtype=f32),
    }
    ims = []
    for i in range(n_cores):
        st = i * tpc
        im = dict(common)
        im["x_own"] = x[st:st + tpc]
        im["x_halo"] = x[st - 128:st] if i > 0 else np.ascontiguousarray(x[0:128])
        im["halo_m"] = np.full((128, 1), 0.0 if i == 0 else 1.0, f32)
        pm = np.zeros((n_cores, 1), f32)
        pm[:i] = 1.0
        im["prefm"] = pm
        mm = np.zeros((n_cores, n_cores), f32)
        for j in range(i):
            for k in range(j + 1, i):
                mm[k, j] = 1.0
        im["mm8"] = mm
        ims.append(im)
    return ims


def kernel(**inputs):
    n_cores = 8
    tpc = L_FULL // n_cores
    nc = build_nc(n_cores, tpc)
    ims = _prep_inputs(inputs, n_cores, tpc)
    res = run_bass_kernel_spmd(nc, ims, core_ids=list(range(n_cores)))
    out = np.concatenate([res.results[i]["out"] for i in range(n_cores)], axis=0)
    return out.reshape(1, L_FULL, D)


# version 04
# speedup vs baseline: 1.0378x; 1.0378x over previous
import contextlib
import numpy as np
import ml_dtypes
import concourse.bass as bass
import concourse.mybir as mybir
from concourse.bass_utils import run_bass_kernel_spmd

F32 = mybir.dt.float32
BF16 = mybir.dt.bfloat16
I32 = mybir.dt.int32
AF = mybir.ActivationFunctionType
ALU = mybir.AluOpType
AX = mybir.AxisListType

L_FULL, D, E, H = 16384, 1024, 2048, 4096
EPS = 1e-5
ENGINES = ("tensor", "vector", "scalar", "gpsimd", "sync")


class Chan:
    __slots__ = ("name", "sem", "n")

    def __init__(self, name):
        self.name = name
        self.sem = None
        self.n = 0


class Inst:
    __slots__ = ("h",)

    def __init__(self, h):
        self.h = h

    def inc(self, chan, k=1):
        if k <= 0:
            return self
        if self.h is not None:
            self.h.then_inc(chan.sem, k)
        chan.n += k
        return self


class Eng:
    def __init__(self, prog, name):
        object.__setattr__(self, "_p", prog)
        object.__setattr__(self, "_n", name)

    def wait(self, chan, val=None):
        v = chan.n if val is None else val
        if self._p.cur == self._n:
            self._p.handles[self._n].wait_ge(chan.sem, v)

    def __getattr__(self, op):
        p = self._p
        nm = self._n

        def call(*a, **kw):
            if p.cur == nm:
                return Inst(getattr(p.handles[nm], op)(*a, **kw))
            return Inst(None)

        return call


class Prog:
    def __init__(self):
        self.cur = None
        self.handles = {}
        self.t = Eng(self, "tensor")
        self.v = Eng(self, "vector")
        self.s = Eng(self, "scalar")
        self.g = Eng(self, "gpsimd")
        self.y = Eng(self, "sync")


def build_nc(n_cores, tpc, dbg=None):
    nch = tpc // 128
    nst = tpc // 512
    nc = bass.Bass("TRN2", target_bir_lowering=False,
                   detect_race_conditions=False)

    def din(name, shape, dt=F32):
        return nc.dram_tensor(name, list(shape), dt, kind="ExternalInput")

    x_own = din("x_own", [tpc, D])
    x_halo = din("x_halo", [128, D])
    WiT = din("WiT", [D, 2 * E])
    WdtT = din("WdtT", [E, E])
    WBT = din("WBT", [E, E])
    WCT = din("WCT", [E, E])
    WoT = din("WoT", [E, D])
    w1T = din("w1T", [D, H])
    w2T = din("w2T", [H, D])
    ln1w_r = din("ln1w_r", [1, D])
    ln1b_r = din("ln1b_r", [1, D])
    b1_r = din("b1_r", [1, H])
    ln2w_r = din("ln2w_r", [1, H])
    ln2b_r = din("ln2b_r", [1, H])
    b2_r = din("b2_r", [1, D])
    bdt_c = din("bdt_c", [128, 16])
    alog_c = din("alog_c", [128, 16])
    bconv_c = din("bconv_c", [128, 16])
    wconv_c = din("wconv_c", [128, 64])
    ones_r = din("ones_r", [1, 128])
    ones_c = din("ones_c", [128, 1])
    idt_b = din("idt_b", [128, 128], BF16)
    idf_f = din("idf_f", [128, 128])
    halo_m = din("halo_m", [128, 1])
    prefm = din("prefm", [n_cores, 1])
    mm8 = din("mm8", [n_cores, n_cores])

    out_t = nc.dram_tensor("out", [tpc, D], F32, kind="ExternalOutput")

    y1_s = nc.dram_tensor("y1_s", [nch + 1, 128, D], F32)
    q1_s = nc.dram_tensor("q1_s", [nch + 1, 128, H], BF16)
    q2_s = nc.dram_tensor("q2_s", [nch, 128, H], BF16)
    w1q_s = nc.dram_tensor("w1q_s", [8, 128, H], BF16)
    w2q_s = nc.dram_tensor("w2q_s", [32, 128, D], BF16)
    xcb_s = nc.dram_tensor("xcb_s", [nst, 128, 16 * 512], BF16)
    slz_s = nc.dram_tensor("slz_s", [nst, 128, 16 * 512], BF16)
    alp_s = nc.dram_tensor("alp_s", [nst, 16, 128, 512], F32)
    bet_s = nc.dram_tensor("bet_s", [nst, 16, 128, 512], F32)
    x2_s = nc.dram_tensor("x2_s", [nch, 128, D], F32)
    cc_in = nc.dram_tensor("cc_in", [2, E], F32)
    cc_out = nc.dram_tensor("cc_out", [2 * n_cores, E], F32)

    dbg_t = None
    if dbg == "y1":
        dbg_t = nc.dram_tensor("dbg", [(nch + 1) * 128, D], F32, kind="ExternalOutput")
    elif dbg == "m1":
        dbg_t = nc.dram_tensor("dbg", [2 * nst * 128, 16 * 512], F32, kind="ExternalOutput")
    elif dbg == "m2":
        dbg_t = nc.dram_tensor("dbg", [2 * nst * 16 * 128 + 3 * 128, 512], F32, kind="ExternalOutput")
    elif dbg == "x2":
        dbg_t = nc.dram_tensor("dbg", [nch * 128, D], F32, kind="ExternalOutput")
    elif dbg == "q1":
        dbg_t = nc.dram_tensor("dbg", [43 * 128, H], F32, kind="ExternalOutput")
    elif dbg == "f1a":
        dbg_t = nc.dram_tensor("dbg", [8 * 128, H], F32, kind="ExternalOutput")
    elif dbg == "wdt":
        dbg_t = nc.dram_tensor("dbg", [6 * 128, 2048], F32, kind="ExternalOutput")
    elif dbg == "m2g":
        dbg_t = nc.dram_tensor("dbg", [24 * 128, 512], F32, kind="ExternalOutput")

    chans = {n: Chan(n) for n in ("LD", "ST", "PE", "DV", "AC", "CC", "WL", "WG",
                                  "QA", "QB", "QC", "QD")}

    es = contextlib.ExitStack()
    with es:
        def sbt(name, shape, dt=F32):
            return es.enter_context(nc.sbuf_tensor(name, list(shape), dt))

        WA = sbt("WA", [128, 32768], BF16)
        WB_ = sbt("WB_", [128, 32768], BF16)
        AR = sbt("AR", [128, 11776], F32)
        BCln2w = sbt("BCln2w", [128, H], BF16)
        BCln2b = sbt("BCln2b", [128, H], BF16)
        BCb1 = sbt("BCb1", [128, H], BF16)
        BCb2h = sbt("BCb2h", [128, D], BF16)
        BCln1w = sbt("BCln1w", [128, D], BF16)
        BCln1b = sbt("BCln1b", [128, D], BF16)
        SMALL = sbt("SMALL", [128, 480], F32)
        IDT = sbt("IDT", [128, 128], BF16)
        IDF = sbt("IDF", [128, 128], F32)
        PRM = sbt("PRM", [128, 128], F32)
        PS = [es.enter_context(nc.psum_tensor(f"ps{i}", [128, 512], F32))
              for i in range(4)]
        PSX = es.enter_context(nc.psum_tensor("psx", [128, 512], F32))
        PSY = es.enter_context(nc.psum_tensor("psy", [128, 512], F32))
        PT = [es.enter_context(nc.psum_tensor(f"pt{i}", [128, 1024], BF16))
              for i in range(2)]

        for cn in chans:
            chans[cn].sem = es.enter_context(nc.semaphore(cn))

        block = es.enter_context(nc.Block())
        P = Prog()

        def arf(o, w):
            return AR[:, o:o + w]

        def arb(o, w):
            return AR[:, o:o + (w + 1) // 2].bitcast(BF16)[:, :w]

        def ari(o, w):
            return AR[:, o:o + w].bitcast(I32)[:, :w]

        def smc(o, w=1):
            return SMALL[:, o:o + w]

        C = chans

        ONES_ROW = SMALL[0:1, 0:128]

        def emit(P):
            LD, ST, PE, DV, AC, CCh, WL, WG = (C["LD"], C["ST"], C["PE"], C["DV"],
                                               C["AC"], C["CC"], C["WL"], C["WG"])
            t, v, s, g, y = P.t, P.v, P.s, P.g, P.y

            def vfence(inst):
                inst.inc(DV, 1)
                v.wait(DV)

            def ln_rstd(src, width, cb):
                nj = width // 512
                last = None
                for j in range(nj):
                    last = v.bn_stats(smc(cb + j * 6, 6), src(j))
                vfence(last)
                vfence(v.bn_aggr(smc(cb + 48, 2), smc(cb, 6 * nj)))
                vfence(v.tensor_scalar(smc(cb + 50), smc(cb + 49), EPS, None, ALU.add))
                vfence(v.tensor_scalar(smc(cb + 55), smc(cb + 50), -0.5, None, ALU.mult))
                iv = smc(cb + 50).bitcast(I32)
                iy = smc(cb + 51).bitcast(I32)
                vfence(v.tensor_scalar(iy, iv, 1, None, ALU.arith_shift_right))
                vfence(v.tensor_scalar(iy, iy, -1, None, ALU.bitwise_xor))
                vfence(v.tensor_scalar(iy, iy, 0x5F3759E0, None, ALU.add))
                for _ in range(3):
                    s.wait(DV)
                    s.activation(smc(cb + 53), smc(cb + 51), AF.Square).inc(AC, 1)
                    v.wait(AC)
                    vfence(v.tensor_scalar(smc(cb + 54), smc(cb + 53), smc(cb + 55),
                                           1.5, ALU.mult, ALU.add))
                    vfence(v.tensor_scalar(smc(cb + 51), smc(cb + 54), smc(cb + 51),
                                           None, ALU.mult))
                vfence(v.tensor_scalar(smc(cb + 52), smc(cb + 51), smc(cb + 48),
                                       -1.0, ALU.mult, ALU.mult))
                return smc(cb + 51), smc(cb + 52)

            def slab2(ten, i):
                return ten.ap()[i]

            y.dma_start(out=IDT[:, :], in_=idt_b.ap()).inc(LD, 16)
            y.dma_start(out=IDF[:, :], in_=idf_f.ap()).inc(LD, 16)
            y.dma_start(out=PRM[:, 0:16], in_=bdt_c.ap()).inc(LD, 16)
            y.dma_start(out=PRM[:, 16:32], in_=alog_c.ap()).inc(LD, 16)
            y.dma_start(out=PRM[:, 32:48], in_=bconv_c.ap()).inc(LD, 16)
            y.dma_start(out=PRM[:, 48:112], in_=wconv_c.ap()).inc(LD, 16)
            y.dma_start(out=PRM[:, 112:113], in_=halo_m.ap()).inc(LD, 16)
            y.dma_start(out=PRM[:, 113:114], in_=ones_c.ap()).inc(LD, 16)
            y.dma_start(out=SMALL[0:1, 0:128], in_=ones_r.ap()).inc(LD, 16)
            ld_p0 = LD.n

            o_ln1w, o_ln1b, o_ln2w, o_ln2b, o_b2, o_b1 = 0, 1024, 2048, 6144, 10240, 11264 - 4096
            y.dma_start(out=AR[0:1, o_ln1w:o_ln1w + D], in_=ln1w_r.ap()).inc(LD, 16)
            y.dma_start(out=AR[0:1, o_ln1b:o_ln1b + D], in_=ln1b_r.ap()).inc(LD, 16)
            y.dma_start(out=AR[0:1, o_b2:o_b2 + D], in_=b2_r.ap()).inc(LD, 16)
            y.dma_start(out=AR[0:1, o_ln2w:o_ln2w + H], in_=b1_r.ap()).inc(LD, 16)
            ld_rows1 = LD.n

            def bcast_row(dst_bf, src_off, w, scale, ldv):
                for j in range(w // 512):
                    t.wait(LD, ldv)
                    t.matmul(PSX[:, :], ONES_ROW,
                             AR[0:1, src_off + j * 512:src_off + (j + 1) * 512],
                             start=True, stop=True).inc(PE, 1)
                    v.wait(PE)
                    if scale != 1.0:
                        v.tensor_scalar(dst_bf[:, j * 512:(j + 1) * 512],
                                        PSX[:, :], scale, None, ALU.mult).inc(DV, 1)
                    else:
                        v.tensor_copy(dst_bf[:, j * 512:(j + 1) * 512],
                                      PSX[:, :]).inc(DV, 1)
                    t.wait(DV)

            bcast_row(BCln1w, o_ln1w, D, 1.0, ld_rows1)
            bcast_row(BCln1b, o_ln1b, D, 1.0, ld_rows1)
            bcast_row(BCb2h, o_b2, D, 0.5, ld_rows1)
            bcast_row(BCb1, o_ln2w, H, 1.0, ld_rows1)
            y.wait(PE, C["PE"].n)
            y.dma_start(out=AR[0:1, o_ln2w:o_ln2w + H], in_=ln2w_r.ap()).inc(LD, 16)
            y.dma_start(out=AR[0:1, o_ln2b:o_ln2b + H], in_=ln2b_r.ap()).inc(LD, 16)
            ld_rows2 = LD.n
            bcast_row(BCln2w, o_ln2w, H, 1.0, ld_rows2)
            bcast_row(BCln2b, o_ln2b, H, 1.0, ld_rows2)
            pe_bcast_done = PE.n
            dv_bcast_done = DV.n

            s.wait(LD, ld_p0)
            s.activation(PRM[:, 16:32], PRM[:, 16:32], AF.Exp).inc(AC, 1)
            v.wait(AC)
            v.tensor_scalar(PRM[:, 16:32], PRM[:, 16:32], -1.0, None, ALU.mult).inc(DV, 1)

            def wbf(i):
                return WB_[:, i * 8192:(i + 1) * 8192].bitcast(F32)

            def quant_weight(wT, nchunks, wcols, img, pair_off, dst, gate0):
                if wcols == 4096:
                    bufs = [lambda: arf(0, 4096), lambda: wbf(0),
                            lambda: arf(4096, 4096), lambda: wbf(1)]
                else:
                    bufs = [lambda: arf(0, 1024), lambda: arf(1024, 1024),
                            lambda: arf(2048, 1024), lambda: arf(3072, 1024)]
                qeng = [y, g, y, g]
                qch = [C["QA"], C["QB"], C["QC"], C["QD"]]

                def issue(ch, gates):
                    b = ch % 4
                    e = qeng[b]
                    for gc, gv in gates:
                        e.wait(gc, gv)
                    e.dma_start(out=bufs[b](),
                                in_=wT.ap()[ch * 128:(ch + 1) * 128, :]).inc(qch[b], 16)
                    return qch[b].n

                ldv = {}
                dvred = []
                for ch in range(min(4, nchunks)):
                    ldv[ch] = issue(ch, gate0)
                for ch in range(nchunks):
                    v.wait(qch[ch % 4], ldv[ch])
                    v.tensor_reduce(smc(140 + ch), bufs[ch % 4](), AX.X, ALU.add,
                                    apply_absolute_value=True).inc(DV, 1)
                    dvred.append(DV.n)
                    if ch + 4 < nchunks:
                        ldv[ch + 4] = issue(ch + 4, [(DV, dvred[ch])])
                vfence(v.tensor_copy(arf(8192, 512), arf(0, 512)))
                v.tensor_reduce(smc(139), SMALL[:, 140:140 + nchunks], AX.X, ALU.add).inc(DV, 1)
                t.wait(DV)
                t.matmul(PSX[0:1, 0:1], smc(139), PRM[:, 113:114],
                         start=True, stop=True).inc(PE, 1)
                v.wait(PE)
                sc_c = SMALL[0:1, 138:139]
                sc_s = SMALL[0:1, 137:138]
                vfence(v.tensor_scalar(sc_c, PSX[0:1, 0:1],
                                       1.0 / (nchunks * 128 * wcols), None, ALU.mult))
                vfence(v.tensor_scalar(sc_c, sc_c, 1e-5, None, ALU.max))
                v.reciprocal(sc_s, sc_c).inc(DV, 1)
                t.wait(DV)
                t.matmul(PSX[:, 0:2], ONES_ROW, SMALL[0:1, 137:139],
                         start=True, stop=True).inc(PE, 1)
                v.wait(PE)
                v.tensor_copy(smc(pair_off, 2), PSX[:, 0:2]).inc(DV, 1)
                v.wait(DV)
                t.wait(DV)
                pe_comb = PE.n
                scale_bc = smc(pair_off)
                nh = max(1, wcols // 2048)
                hw = wcols // nh
                dvq = []
                for ch in range(min(4, nchunks)):
                    ldv[ch] = issue(ch, [(DV, dvred[max(ch, nchunks - 4 + ch)])])
                for ch in range(nchunks):
                    v.wait(qch[ch % 4], ldv[ch])
                    for hhh in range(nh):
                        srcw = bufs[ch % 4]()[:, hhh * hw:(hhh + 1) * hw]
                        v.tensor_scalar(ari(8192, hw), srcw, scale_bc, None, ALU.mult)
                        v.tensor_copy(srcw, ari(8192, hw))
                        dslice = dst[:, ch * wcols + hhh * hw:ch * wcols + (hhh + 1) * hw]
                        v.tensor_scalar(dslice, srcw,
                                        -1.0, 1.0, ALU.max, ALU.min).inc(DV, 1)
                        s.wait(DV)
                        s.dma_start(out=img.ap()[ch][:, hhh * hw:(hhh + 1) * hw],
                                    in_=dslice).inc(WL, 16)
                    dvq.append(DV.n)
                    if ch + 4 < nchunks:
                        ldv[ch + 4] = issue(ch + 4, [(DV, dvq[ch])])
                return DV.n, pe_comb

            dv_w1q, pe_comb1 = quant_weight(w1T, 8, H, w1q_s, 172, WA,
                                            [(PE, pe_bcast_done), (DV, dv_bcast_done)])
            if dbg == "q1":
                g.wait(DV, DV.n)
                g.dma_start(out=dbg_t.ap()[42 * 128:43 * 128, 0:64],
                            in_=SMALL[:, 128:192]).inc(ST, 16)
                v.wait(ST, ST.n)
            v.wait(PE, pe_comb1)
            dv_w2q, _ = quant_weight(w2T, 32, D, w2q_s, 174, WB_, [(DV, dv_w1q)])
            c1_col = smc(173)
            c2_col = smc(175)
            wl_imgs = WL.n

            def ffn_A(src_kind, q_img, nchunks, sout2_base, w_gate):
                gch, gval = w_gate
                xfree = [None, None]
                ptfree = [0, 0]
                bankfree = [0, 0]
                q2free = 0
                for c in range(nchunks):
                    xbuf = (c % 2) * 1024
                    if xfree[c % 2] is not None:
                        y.wait(AC, xfree[c % 2])
                    if src_kind == "x":
                        src = x_halo.ap() if c == 0 else x_own.ap()[(c - 1) * 128:c * 128, :]
                    else:
                        src = slab2(x2_s, c)
                    y.dma_start(out=arf(xbuf, D), in_=src).inc(LD, 16)
                    v.wait(LD)
                    rstd1, nmb1 = ln_rstd(lambda j: arf(xbuf + j * 512, 512), D, 200)
                    s.wait(DV)
                    s.activation(arb(2048, D), arf(xbuf, D), AF.Identity,
                                 bias=nmb1, scale=rstd1).inc(AC, 1)
                    xfree[c % 2] = AC.n
                    v.wait(AC)
                    v.tensor_tensor(arb(2560, D), arb(2048, D), BCln1w[:, :], ALU.mult)
                    v.tensor_tensor(arb(3072, D), arb(2560, D), BCln1b[:, :], ALU.add)
                    vfence(v.tensor_reduce(smc(229), arb(3072, D), AX.X, ALU.max,
                                           apply_absolute_value=True))
                    vfence(v.tensor_scalar(smc(229), smc(229), EPS, None, ALU.max))
                    vfence(v.reciprocal(smc(230), smc(229)))
                    v.tensor_scalar(smc(231), smc(229), c1_col, 1.0 / 127.0,
                                    ALU.mult, ALU.mult).inc(DV, 1)
                    v.tensor_scalar(ari(3584, D), arb(3072, D), 127.0, smc(230),
                                    ALU.mult, ALU.mult)
                    v.tensor_copy(arb(4608, D), ari(3584, D)).inc(DV, 1)
                    for k in range(8):
                        bk = k % 2
                        t.wait(DV) if k == 0 else None
                        t.wait(DV, ptfree[bk]) if k >= 2 else None
                        t.transpose(PT[bk][:, 0:128],
                                    arb(4608, D)[:, k * 128:(k + 1) * 128],
                                    IDT[:, :]).inc(PE, 1)
                        v.wait(PE)
                        v.tensor_copy(arb(5120, D)[:, k * 128:(k + 1) * 128],
                                      PT[bk][:, 0:128]).inc(DV, 1)
                        ptfree[bk] = DV.n
                    t.wait(gch, gval)
                    s.wait(ST, q2free)
                    s.wait(DV)
                    for n in range(8):
                        bank = n % 2
                        t.wait(AC, bankfree[bank])
                        for k in range(8):
                            t.matmul(PS[bank][:, :],
                                     arb(5120, D)[:, k * 128:(k + 1) * 128],
                                     WA[:, k * H + n * 512:k * H + (n + 1) * 512],
                                     start=(k == 0), stop=(k == 7)).inc(PE, 1 if k == 7 else 0)
                        s.wait(PE)
                        s.activation(arf(10752, 512), PS[bank][:, :], AF.Identity,
                                     scale=smc(231)).inc(AC, 1)
                        bankfree[bank] = AC.n
                        v.wait(AC)
                        v.tensor_tensor(arf(11264, 512), arf(10752, 512),
                                        BCb1[:, n * 512:(n + 1) * 512], ALU.add).inc(DV, 1)
                        s.wait(DV)
                        s.activation(arb(5632, H)[:, n * 512:(n + 1) * 512],
                                     arf(11264, 512), AF.Gelu).inc(AC, 1)
                    v.wait(AC)
                    rstd2, nmb2 = ln_rstd(lambda j: arb(5632, H)[:, j * 512:(j + 1) * 512],
                                          H, 340)
                    s.wait(DV)
                    s.activation(arb(5632, H), arb(5632, H), AF.Identity,
                                 bias=nmb2, scale=rstd2).inc(AC, 1)
                    v.wait(AC)
                    v.tensor_tensor(arb(7680, H), arb(5632, H), BCln2w[:, :], ALU.mult)
                    v.tensor_tensor(arb(5632, H), arb(7680, H), BCln2b[:, :], ALU.add)
                    vfence(v.tensor_reduce(smc(298), arb(5632, H), AX.X, ALU.max,
                                           apply_absolute_value=True))
                    vfence(v.tensor_scalar(smc(298), smc(298), EPS, None, ALU.max))
                    vfence(v.reciprocal(smc(299), smc(298)))
                    v.tensor_scalar(SMALL[:, sout2_base + c:sout2_base + c + 1],
                                    smc(298), c2_col, 1.0 / 127.0,
                                    ALU.mult, ALU.mult).inc(DV, 1)
                    last = None
                    for qq in range(4):
                        v.tensor_scalar(ari(9728, 1024),
                                        arb(5632, H)[:, qq * 1024:(qq + 1) * 1024],
                                        127.0, smc(299), ALU.mult, ALU.mult)
                        last = v.tensor_copy(arb(3584, H)[:, qq * 1024:(qq + 1) * 1024],
                                             ari(9728, 1024))
                    last.inc(DV, 1)
                    for kk in range(8):
                        bk = kk % 2
                        t.wait(DV) if kk == 0 else None
                        t.wait(DV, ptfree[bk]) if kk >= 2 else None
                        for j in range(4):
                            k = kk * 4 + j
                            t.transpose(PT[bk][:, j * 128:(j + 1) * 128],
                                        arb(3584, H)[:, k * 128:(k + 1) * 128],
                                        IDT[:, :]).inc(PE, 1 if j == 3 else 0)
                        v.wait(PE)
                        v.tensor_copy(arb(5632, H)[:, kk * 512:(kk + 1) * 512],
                                      PT[bk][:, 0:512]).inc(DV, 1)
                        ptfree[bk] = DV.n
                    g.wait(DV)
                    g.dma_start(out=slab2(q_img, c), in_=arb(5632, H)).inc(ST, 16)
                    q2free = ST.n
                y.wait(ST, ST.n)

            def ffn_B(src_kind, q_img, nchunks, sout2_base, dst_kind, w_gate):
                gch, gval = w_gate
                qfree = [0, 0]
                xfree = [0, 0]
                mmfree = [0, 0]
                for c in range(nchunks):
                    qbuf = 3584 if c % 2 == 0 else 5632
                    xbuf = (c % 2) * 1024
                    y.wait(PE, qfree[c % 2])
                    y.dma_start(out=arb(qbuf, H), in_=slab2(q_img, c)).inc(LD, 16)
                    if src_kind == "x":
                        src = x_halo.ap() if c == 0 else x_own.ap()[(c - 1) * 128:c * 128, :]
                    else:
                        src = slab2(x2_s, c)
                    y.wait(DV, xfree[c % 2])
                    y.dma_start(out=arf(xbuf, D), in_=src).inc(LD, 16)
                    v.wait(LD)
                    v.tensor_scalar(smc(273), SMALL[:, sout2_base + c:sout2_base + c + 1],
                                    0.5, None, ALU.mult).inc(DV, 1)
                    t.wait(gch, gval)
                    t.wait(DV)
                    for n in range(2):
                        bank = 2 + n
                        t.wait(DV, mmfree[n])
                        for k in range(32):
                            t.matmul(PS[bank][:, :],
                                     arb(qbuf, H)[:, k * 128:(k + 1) * 128],
                                     WB_[:, k * D + n * 512:k * D + (n + 1) * 512],
                                     start=(k == 0), stop=(k == 31)).inc(PE, 1 if k == 31 else 0)
                        s.wait(PE)
                        s.activation(arf(2048, 512), PS[bank][:, :], AF.Identity,
                                     scale=smc(273)).inc(AC, 1)
                        v.wait(AC)
                        v.tensor_tensor(arf(10752, 512), arf(2048, 512),
                                        BCb2h[:, n * 512:(n + 1) * 512], ALU.add)
                        v.tensor_tensor(arf(2560 + n * 512, 512), arf(10752, 512),
                                        arf(xbuf + n * 512, 512), ALU.add).inc(DV, 1)
                        mmfree[n] = DV.n
                    qfree[c % 2] = PE.n
                    xfree[c % 2] = DV.n
                    g.wait(DV)
                    if dst_kind == "y1":
                        g.dma_start(out=slab2(y1_s, c), in_=arf(2560, D)).inc(ST, 16)
                    else:
                        g.dma_start(out=out_t.ap()[c * 128:(c + 1) * 128, :],
                                    in_=arf(2560, D)).inc(ST, 16)
                y.wait(ST, ST.n)

            y.wait(DV, dv_w2q)
            ffn_A("x", q1_s, 1 if dbg == "f1a" else nch + 1, 300, (DV, dv_w1q))
            if dbg == "f1a":
                g.wait(DV, DV.n)
                g.wait(AC, AC.n)
                g.dma_start(out=dbg_t.ap()[0:128, 0:D], in_=arb(2048, D)).inc(ST, 16)
                g.dma_start(out=dbg_t.ap()[128:256, 0:D], in_=arb(3072, D)).inc(ST, 16)
                g.dma_start(out=dbg_t.ap()[256:384, 0:D], in_=arb(4608, D)).inc(ST, 16)
                g.dma_start(out=dbg_t.ap()[384:512, :], in_=arb(5632, H)).inc(ST, 16)
                g.dma_start(out=dbg_t.ap()[512:640, :], in_=arb(7680, H)).inc(ST, 16)
                g.dma_start(out=dbg_t.ap()[640:768, :], in_=arb(3584, H)).inc(ST, 16)
                g.dma_start(out=dbg_t.ap()[768:896, 0:120], in_=SMALL[:, 200:320]).inc(ST, 16)
                g.dma_start(out=dbg_t.ap()[896:1024, 0:512], in_=arf(10752, 512)).inc(ST, 16)
                y.wait(ST, ST.n)
                return
            if dbg == "q1":
                for ch in range(8):
                    g.dma_start(out=dbg_t.ap()[ch * 128:(ch + 1) * 128, :],
                                in_=w1q_s.ap()[ch]).inc(ST, 16)
                for ch in range(32):
                    g.dma_start(out=dbg_t.ap()[(8 + ch) * 128:(9 + ch) * 128, 0:D],
                                in_=w2q_s.ap()[ch]).inc(ST, 16)
                g.dma_start(out=dbg_t.ap()[40 * 128:41 * 128, :],
                            in_=slab2(q1_s, 3)).inc(ST, 16)
                g.wait(DV, DV.n)
                g.dma_start(out=dbg_t.ap()[41 * 128:42 * 128, 0:33],
                            in_=SMALL[:, 300:333]).inc(ST, 16)
                y.wait(ST, ST.n)
                return
            ffn_B("x", q1_s, nch + 1, 300, "y1", (DV, dv_w2q))

            if dbg == "y1":
                for c in range(nch + 1):
                    g.dma_start(out=dbg_t.ap()[c * 128:(c + 1) * 128, :],
                                in_=slab2(y1_s, c)).inc(ST, 16)
                y.wait(ST, ST.n)
                return

            pe_after_f1 = PE.n
            g.wait(PE, pe_after_f1)
            g.wait(WL, wl_imgs)
            for ch in range(16):
                g.dma_start(out=WB_[:, ch * E:(ch + 1) * E],
                            in_=WdtT.ap()[ch * 128:(ch + 1) * 128, :]).inc(WG, 16)
            wl_wdt = WG.n
            for ch in range(8):
                g.dma_start(out=WA[:, ch * 4096:(ch + 1) * 4096],
                            in_=WiT.ap()[ch * 128:(ch + 1) * 128, :]).inc(WG, 16)
            wl_wi = WG.n

            m1_ptfree = [0, 0]
            m1_ybuf_gate = [0, 0]
            m1_xhb_gate = [0]

            def m1_ln_chunk(slab_idx, parity, lny_col):
                ybuf = 6208 + parity * 1024
                y.wait(AC, m1_ybuf_gate[parity])
                y.dma_start(out=arf(ybuf, D), in_=slab2(y1_s, slab_idx)).inc(LD, 16)
                v.wait(LD)
                rstdm, nmbm = ln_rstd(lambda j: arf(ybuf + j * 512, 512), D, 200)
                s.wait(DV)
                s.wait(PE, m1_xhb_gate[0])
                s.activation(arb(10304, D), arf(ybuf, D), AF.Identity,
                             bias=nmbm, scale=rstdm).inc(AC, 1)
                m1_ybuf_gate[parity] = AC.n
                t.wait(AC)
                for k in range(8):
                    bk = k % 2
                    if k >= 2:
                        t.wait(DV, m1_ptfree[bk])
                    t.transpose(PT[bk][:, 0:128],
                                arb(10304, D)[:, k * 128:(k + 1) * 128],
                                IDT[:, :]).inc(PE, 1)
                    v.wait(PE)
                    v.tensor_copy(arb(4160, 4096)[:, k * 512 + lny_col:k * 512 + lny_col + 128],
                                  PT[bk][:, 0:128]).inc(DV, 1)
                    m1_ptfree[bk] = DV.n
                m1_xhb_gate[0] = PE.n

            X1R = arb(0, 16 * 520)

            m1_ln_chunk(0, 0, 0)
            t.wait(WG, wl_wi)
            m1_bankfree = [("DV", 0), ("DV", 0), ("DV", 0), ("DV", 0)]

            def bank_wait(eng, bank):
                cname, val = m1_bankfree[bank]
                eng.wait(C[cname], val)

            for gi in range(16):
                bank = gi % 2
                base = gi * 520
                bank_wait(t, bank)
                for k in range(8):
                    t.matmul(PS[bank][:, 0:4],
                             WA[:, k * 4096 + gi * 128:k * 4096 + gi * 128 + 128],
                             arb(4160, 4096)[:, k * 512 + 124:k * 512 + 128],
                             start=(k == 0), stop=(k == 7)).inc(PE, 1 if k == 7 else 0)
                v.wait(PE)
                v.tensor_scalar(X1R[:, base:base + 4], PS[bank][:, 0:4],
                                PRM[:, 112:113], None, ALU.mult).inc(DV, 1)
                m1_bankfree[bank] = ("DV", DV.n)
            last = None
            for gi in range(16):
                base = gi * 520
                for k in range(4):
                    last = v.tensor_scalar(arf(10816 + gi * 4 + k, 1),
                                           X1R[:, base + k:base + k + 1],
                                           PRM[:, 48 + gi * 4 + k:49 + gi * 4 + k],
                                           None, ALU.mult)
            vfence(last)
            for gi in range(16):
                last = v.tensor_reduce(arf(10880 + gi, 1), arf(10816 + gi * 4, 4),
                                       AX.X, ALU.add)
            vfence(last)
            vfence(v.tensor_tensor(arf(10896, 16), arf(10880, 16),
                                   PRM[:, 32:48], ALU.add))
            s.wait(DV)
            s.activation(arf(10912, 16), arf(10896, 16), AF.Sigmoid).inc(AC, 1)
            v.wait(AC)
            vfence(v.tensor_tensor(arf(10928, 16), arf(10912, 16), arf(10896, 16),
                                   ALU.mult))
            XCH = SMALL[:, 386:394].bitcast(BF16)[:, 0:16]
            v.tensor_scalar(XCH, arf(10928, 16), PRM[:, 112:113], None,
                            ALU.mult).inc(DV, 1)

            m1_xco_gate = [0, 0]
            m1_szo_gate = [0, 0]
            for st_i in range(nst):
                for i in range(4):
                    m1_ln_chunk(1 + st_i * 4 + i, i % 2, i * 128)
                for gi in range(16):
                    bank = gi % 2
                    base = gi * 520
                    bank_wait(t, bank)
                    for k in range(8):
                        t.matmul(PS[bank][:, :],
                                 WA[:, k * 4096 + gi * 128:k * 4096 + gi * 128 + 128],
                                 arb(4160, 4096)[:, k * 512:(k + 1) * 512],
                                 start=(k == 0), stop=(k == 7)).inc(PE, 1 if k == 7 else 0)
                    v.wait(PE)
                    v.tensor_copy(X1R[:, base + 4:base + 516], PS[bank][:, :]).inc(DV, 1)
                    m1_bankfree[bank] = ("DV", DV.n)
                    v.tensor_scalar(arf(9280, 512), X1R[:, base + 1:base + 513],
                                    PRM[:, 48 + gi * 4:49 + gi * 4], None, ALU.mult)
                    for k in range(1, 4):
                        dsto = 9792 if k % 2 == 1 else 9280
                        srco = 9280 if k % 2 == 1 else 9792
                        v.scalar_tensor_tensor(arf(dsto, 512),
                                               X1R[:, base + 1 + k:base + 513 + k],
                                               PRM[:, 48 + gi * 4 + k:49 + gi * 4 + k],
                                               arf(srco, 512), ALU.mult, ALU.add)
                    v.tensor_scalar(arf(9280, 512), arf(9792, 512),
                                    PRM[:, 32 + gi:33 + gi], None, ALU.add).inc(DV, 1)
                    v.tensor_copy(X1R[:, base:base + 4],
                                  X1R[:, base + 512:base + 516])
                    s.wait(DV)
                    s.activation(arf(9792, 512), arf(9280, 512), AF.Sigmoid).inc(AC, 1)
                    v.wait(AC)
                    xco = 8256 + (gi % 2) * 256
                    v.wait(ST, m1_xco_gate[gi % 2])
                    v.tensor_tensor(arb(xco, 512), arf(9280, 512), arf(9792, 512),
                                    ALU.mult).inc(DV, 1)
                    g.wait(DV)
                    g.dma_start(out=xcb_s.ap()[st_i][:, gi * 512:(gi + 1) * 512],
                                in_=arb(xco, 512)).inc(ST, 16)
                    m1_xco_gate[gi % 2] = ST.n
                    bank_wait(t, bank)
                    for k in range(8):
                        t.matmul(PS[bank][:, :],
                                 WA[:, k * 4096 + 2048 + gi * 128:k * 4096 + 2048 + gi * 128 + 128],
                                 arb(4160, 4096)[:, k * 512:(k + 1) * 512],
                                 start=(k == 0), stop=(k == 7)).inc(PE, 1 if k == 7 else 0)
                    s.wait(PE)
                    s.wait(DV)
                    s.activation(arf(9792, 512), PS[bank][:, :], AF.Sigmoid).inc(AC, 1)
                    v.wait(AC)
                    szo = 8768 + (gi % 2) * 256
                    v.wait(ST, m1_szo_gate[gi % 2])
                    v.tensor_tensor(arb(szo, 512), PS[bank][:, :], arf(9792, 512),
                                    ALU.mult).inc(DV, 1)
                    m1_bankfree[bank] = ("DV", DV.n)
                    g.wait(DV)
                    g.dma_start(out=slz_s.ap()[st_i][:, gi * 512:(gi + 1) * 512],
                                in_=arb(szo, 512)).inc(ST, 16)
                    m1_szo_gate[gi % 2] = ST.n
            y.wait(ST, ST.n)

            if dbg == "m1":
                for st_i in range(nst):
                    g.dma_start(out=dbg_t.ap()[st_i * 128:(st_i + 1) * 128, :],
                                in_=xcb_s.ap()[st_i]).inc(ST, 16)
                    g.dma_start(out=dbg_t.ap()[(nst + st_i) * 128:(nst + st_i + 1) * 128, :],
                                in_=slz_s.ap()[st_i]).inc(ST, 16)
                y.wait(ST, ST.n)
                return

            pe_after_m1 = PE.n
            g.wait(PE, pe_after_m1)
            for ch in range(16):
                g.dma_start(out=WA[:, ch * E:(ch + 1) * E],
                            in_=WBT.ap()[ch * 128:(ch + 1) * 128, :]).inc(WG, 16)
            wl_wb = WG.n
            UC = [SMALL[:, 402 + i:403 + i] for i in range(16)]
            HC = [SMALL[:, 418 + i:419 + i] for i in range(16)]

            t.wait(WG, wl_wb)
            for gi in range(16):
                bank = gi % 2
                bank_wait(t, bank)
                for ke in range(16):
                    t.matmul(PS[bank][:, 0:1],
                             WA[:, ke * E + gi * 128:ke * E + gi * 128 + 128],
                             XCH[:, ke:ke + 1],
                             start=(ke == 0), stop=(ke == 15)).inc(PE, 1 if ke == 15 else 0)
                t.matmul(PSY[0:128, 0:128], ONES_ROW, ONES_ROW,
                         start=True, stop=True).inc(PE, 1)
                s.wait(PE)
                s.activation(arf(10816 + gi, 1), PS[bank][:, 0:1], AF.Exp,
                             scale=-1.0).inc(AC, 1)
                m1_bankfree[bank] = ("AC", AC.n)
            v.wait(AC)
            vfence(v.tensor_scalar(arf(10832, 16), arf(10816, 16), 1.0, None, ALU.add))
            vfence(v.reciprocal(arf(10848, 16), arf(10832, 16)))
            vfence(v.tensor_tensor(arf(10864, 16), arf(10848, 16), XCH[:, 0:16],
                                   ALU.mult))
            v.tensor_scalar(SMALL[:, 402:418], arf(10864, 16), PRM[:, 112:113],
                            None, ALU.mult)
            v.memset(SMALL[:, 418:434], 0.0).inc(DV, 1)

            if dbg == "wdt":
                g.wait(DV, DV.n)
                g.dma_start(out=dbg_t.ap()[0:128, :], in_=WB_[:, 0:E]).inc(ST, 16)
                g.dma_start(out=dbg_t.ap()[128:256, :], in_=WB_[:, 5 * E:6 * E]).inc(ST, 16)
                g.dma_start(out=dbg_t.ap()[256:384, :], in_=WB_[:, 15 * E:16 * E]).inc(ST, 16)
                g.dma_start(out=dbg_t.ap()[384:512, :], in_=WA[:, 0:E]).inc(ST, 16)
                g.dma_start(out=dbg_t.ap()[512:640, 0:128], in_=PRM[:, :]).inc(ST, 16)
                y.wait(ST, ST.n)
                return

            m2_afree = [0, 0]
            m2_bfree = [0, 0]
            m2_xcb_gate = 0
            m2_bside_dv = [0]
            for st_i in range(nst):
                y.wait(PE, m2_xcb_gate)
                y.wait(DV, m2_bside_dv[0])
                y.dma_start(out=arb(0, 8192), in_=xcb_s.ap()[st_i]).inc(LD, 16)
                for gi in range(16):
                    bank = gi % 2
                    base_a = 8200 + (gi % 2) * 512
                    base_b = 7176 + (gi % 2) * 512
                    t.wait(LD)
                    bank_wait(t, bank)
                    for ke in range(16):
                        t.matmul(PS[bank][:, :],
                                 WB_[:, ke * E + gi * 128:ke * E + gi * 128 + 128],
                                 arb(0, 8192)[:, ke * 512:(ke + 1) * 512],
                                 start=(ke == 0), stop=(ke == 15)).inc(PE, 1 if ke == 15 else 0)
                    t.matmul(PSY[0:128, 0:128], ONES_ROW, ONES_ROW,
                             start=True, stop=True).inc(PE, 1)
                    s.wait(PE)
                    s.activation(arf(4096, 512), PS[bank][:, :], AF.Identity,
                                 bias=PRM[:, 0 + gi:1 + gi]).inc(AC, 1)
                    m1_bankfree[bank] = ("AC", AC.n)
                    s.wait(DV, m2_bside_dv[0])
                    s.activation(arf(4608, 512), arf(4096, 512), AF.Abs)
                    s.activation(arf(5120, 512), arf(4608, 512), AF.Exp, scale=-1.0)
                    s.activation(arf(4608, 512), arf(5120, 512), AF.Ln, bias=1.0)
                    s.activation(arf(5120, 512), arf(4096, 512), AF.Relu).inc(AC, 1)
                    if dbg == "m2g" and st_i == 0 and gi == 3:
                        g.wait(AC, AC.n)
                        g.dma_start(out=dbg_t.ap()[20 * 128:21 * 128, :],
                                    in_=arf(4096, 512)).inc(ST, 16)
                        g.dma_start(out=dbg_t.ap()[21 * 128:22 * 128, :],
                                    in_=arf(4608, 512)).inc(ST, 16)
                        g.dma_start(out=dbg_t.ap()[22 * 128:23 * 128, :],
                                    in_=arf(5120, 512)).inc(ST, 16)
                    v.wait(AC)
                    dtinst = v.tensor_tensor(arf(5632, 512), arf(4608, 512), arf(5120, 512),
                                             ALU.add)
                    if dbg == "m2g" and st_i == 0 and gi == 3:
                        dtinst.inc(DV, 1)
                        g.wait(DV, DV.n)
                        g.dma_start(out=dbg_t.ap()[23 * 128:24 * 128, :],
                                    in_=arf(5632, 512)).inc(ST, 16)
                        v.wait(ST, ST.n)
                    v.tensor_scalar(arf(4096, 512), arf(5632, 512),
                                    PRM[:, 16 + gi:17 + gi], -1e-4, ALU.mult, ALU.min)
                    v.tensor_scalar(arf(4096, 512), arf(4096, 512), -10.0, None,
                                    ALU.max)
                    v.tensor_reduce(arf(9736 + gi * 4 + st_i, 1), arf(4096, 512),
                                    AX.X, ALU.add).inc(DV, 1)
                    s.wait(DV)
                    s.wait(ST, m2_afree[gi % 2])
                    s.activation(arf(base_a, 512), arf(4096, 512), AF.Exp).inc(AC, 1)
                    if dbg == "m2g" and st_i == 0 and gi == 3:
                        v.wait(AC, AC.n)
                        v.tensor_copy(arf(9224, 512), PS[bank][:, :]).inc(DV, 1)
                        g.wait(DV, DV.n)
                        g.wait(AC, AC.n)
                        g.dma_start(out=dbg_t.ap()[0:128, :],
                                    in_=arf(9224, 512)).inc(ST, 16)
                        g.dma_start(out=dbg_t.ap()[128:256, :],
                                    in_=arf(4096, 512)).inc(ST, 16)
                        for rr in range(16):
                            g.dma_start(out=dbg_t.ap()[(2 + rr) * 128:(3 + rr) * 128, :],
                                        in_=arb(0, 8192)[:, rr * 512:(rr + 1) * 512]).inc(ST, 16)
                        g.dma_start(out=dbg_t.ap()[18 * 128:19 * 128, 0:128],
                                    in_=WB_[:, gi * 128:gi * 128 + 128]).inc(ST, 16)
                        g.dma_start(out=dbg_t.ap()[19 * 128:20 * 128, 0:128],
                                    in_=WB_[:, 15 * E + gi * 128:15 * E + gi * 128 + 128]).inc(ST, 16)
                        t.wait(ST, ST.n)
                        v.wait(ST, ST.n)
                        s.wait(ST, ST.n)
                    bank_wait(t, bank)
                    for ke in range(16):
                        t.matmul(PS[bank][:, :],
                                 WA[:, ke * E + gi * 128:ke * E + gi * 128 + 128],
                                 arb(0, 8192)[:, ke * 512:(ke + 1) * 512],
                                 start=(ke == 0), stop=(ke == 15)).inc(PE, 1 if ke == 15 else 0)
                    t.matmul(PSY[0:128, 0:128], ONES_ROW, ONES_ROW,
                             start=True, stop=True).inc(PE, 1)
                    s.wait(PE)
                    s.activation(arf(4608, 512), PS[bank][:, :], AF.Exp,
                                 scale=-1.0).inc(AC, 1)
                    m1_bankfree[bank] = ("AC", AC.n)
                    v.wait(AC)
                    v.tensor_scalar(arf(4608, 512), arf(4608, 512), 1.0, None, ALU.add)
                    v.reciprocal(arf(5120, 512), arf(4608, 512))
                    v.tensor_copy(arf(6144, 1), UC[gi])
                    vfence(v.tensor_tensor(arf(6145, 512),
                                           arb(0, 8192)[:, gi * 512:(gi + 1) * 512],
                                           arf(5120, 512), ALU.mult))
                    v.tensor_copy(UC[gi], arf(6656, 1))
                    v.tensor_tensor(arf(6664, 512), arf(6144, 512), arf(6145, 512),
                                    ALU.add)
                    s.wait(ST, m2_bfree[gi % 2]) if False else None
                    v.wait(ST, m2_bfree[gi % 2])
                    v.scalar_tensor_tensor(arf(base_b, 512), arf(6664, 512), 0.5,
                                           arf(5632, 512), ALU.mult, ALU.mult).inc(DV, 1)
                    v.wait(AC)
                    vfence(v.tensor_tensor_scan(arf(9224, 512), arf(base_a, 512),
                                                arf(base_b, 512), HC[gi],
                                                ALU.mult, ALU.add))
                    v.tensor_copy(HC[gi], arf(9735, 1)).inc(DV, 1)
                    m2_bside_dv[0] = DV.n
                    g.wait(AC)
                    g.wait(DV)
                    g.dma_start(out=alp_s.ap()[st_i, gi], in_=arf(base_a, 512)).inc(ST, 16)
                    g.dma_start(out=bet_s.ap()[st_i, gi], in_=arf(base_b, 512)).inc(ST, 16)
                    m2_afree[gi % 2] = ST.n
                    m2_bfree[gi % 2] = ST.n
                m2_xcb_gate = PE.n
            y.wait(ST, ST.n)

            last = None
            for gi in range(16):
                last = v.tensor_reduce(arf(9800 + gi, 1), arf(9736 + gi * 4, nst),
                                       AX.X, ALU.add)
            vfence(last)
            s.wait(DV)
            s.activation(arf(9816, 16), arf(9800, 16), AF.Exp).inc(AC, 1)
            last = None
            for gi in range(16):
                last = v.tensor_copy(arf(9832 + gi, 1), HC[gi])
            vfence(last)
            t.wait(DV)
            t.wait(AC)
            t.transpose(PSX[0:16, 0:128], arf(9832, 16), IDF[:, :]).inc(PE, 1)
            v.wait(PE)
            v.tensor_copy(AR[0:16, 9856:9984], PSX[0:16, 0:128]).inc(DV, 1)
            t.transpose(PSY[0:16, 0:128], arf(9816, 16), IDF[:, :]).inc(PE, 1)
            v.wait(PE)
            v.tensor_copy(AR[0:16, 9984:10112], PSY[0:16, 0:128]).inc(DV, 1)
            g.wait(DV)
            g.dma_start(out=cc_in.ap()[0:1, :].rearrange("a (r c) -> (a r) c", c=128),
                        in_=AR[0:16, 9856:9984]).inc(ST, 16)
            g.dma_start(out=cc_in.ap()[1:2, :].rearrange("a (r c) -> (a r) c", c=128),
                        in_=AR[0:16, 9984:10112]).inc(ST, 16)
            g.wait(ST, ST.n)
            g.collective_compute(
                "AllGather", ALU.bypass,
                replica_groups=[list(range(n_cores))],
                ins=[cc_in.ap().opt()],
                outs=[cc_out.ap().opt()],
            ).inc(CCh, 1)
            g.wait(CCh, 1)
            NCr = n_cores
            cco = cc_out.ap().rearrange("(r two) e -> two r e", two=2)
            g.dma_start(out=AR[0:NCr, 0:E], in_=cco[0]).inc(ST, 16)
            g.dma_start(out=AR[0:NCr, 2048:2048 + E], in_=cco[1]).inc(ST, 16)
            y.dma_start(out=AR[0:NCr, 4096:4096 + NCr], in_=mm8.ap()).inc(LD, 16)
            y.dma_start(out=AR[0:NCr, 4200:4201], in_=prefm.ap()).inc(LD, 16)
            s.wait(ST, ST.n)
            s.wait(LD, LD.n)
            s.activation(AR[0:NCr, 6144:6144 + E], AR[0:NCr, 2048:2048 + E],
                         AF.Ln).inc(AC, 1)
            v.wait(AC)
            v.tensor_scalar(AR[0:NCr, 6144:6144 + E], AR[0:NCr, 6144:6144 + E],
                            -1e30, None, ALU.max).inc(DV, 1)
            t.wait(DV)
            for j4 in range(4):
                t.matmul(PSX[0:NCr, 0:512], AR[0:NCr, 4096:4096 + NCr],
                         AR[0:NCr, 6144 + j4 * 512:6144 + (j4 + 1) * 512],
                         start=True, stop=True).inc(PE, 1)
                s.wait(PE)
                s.activation(AR[0:NCr, 8192 + j4 * 512:8192 + (j4 + 1) * 512],
                             PSX[0:NCr, 0:512], AF.Exp).inc(AC, 1)
                t.wait(AC)
            v.wait(AC)
            v.tensor_tensor(AR[0:NCr, 8192:8192 + E], AR[0:NCr, 8192:8192 + E],
                            AR[0:NCr, 0:E], ALU.mult).inc(DV, 1)
            t.wait(DV)
            for j4 in range(4):
                t.matmul(PSX[0:1, 0:512], AR[0:NCr, 4200:4201],
                         AR[0:NCr, 8192 + j4 * 512:8192 + (j4 + 1) * 512],
                         start=True, stop=True).inc(PE, 1)
                v.wait(PE)
                v.tensor_copy(AR[0:1, 6144 + j4 * 512:6144 + (j4 + 1) * 512],
                              PSX[0:1, 0:512]).inc(DV, 1)
                t.wait(DV)
            for gi in range(16):
                bank = gi % 2
                pso = PSX if bank == 0 else PSY
                t.transpose(pso[:, 0:1],
                            AR[0:1, 6144 + gi * 128:6144 + (gi + 1) * 128],
                            IDF[0:1, 0:1]).inc(PE, 1)
                v.wait(PE)
                v.tensor_copy(SMALL[:, 448 + gi:449 + gi], pso[:, 0:1]).inc(DV, 1)
                t.wait(DV) if gi >= 1 else None
            HSC = [SMALL[:, 448 + i:449 + i] for i in range(16)]
            dv_after_combine = DV.n

            if dbg == "m2":
                for st_i in range(nst):
                    for gi in range(16):
                        g.dma_start(out=dbg_t.ap()[(st_i * 16 + gi) * 128:(st_i * 16 + gi + 1) * 128, :],
                                    in_=alp_s.ap()[st_i, gi]).inc(ST, 16)
                        g.dma_start(out=dbg_t.ap()[(nst * 16 + st_i * 16 + gi) * 128:(nst * 16 + st_i * 16 + gi + 1) * 128, :],
                                    in_=bet_s.ap()[st_i, gi]).inc(ST, 16)
                base_r = 2 * nst * 16 * 128
                g.wait(DV, DV.n)
                g.dma_start(out=dbg_t.ap()[base_r:base_r + 128, 0:16], in_=arf(9832, 16)).inc(ST, 16)
                g.dma_start(out=dbg_t.ap()[base_r + 128:base_r + 256, 0:16], in_=arf(9816, 16)).inc(ST, 16)
                g.dma_start(out=dbg_t.ap()[base_r + 256:base_r + 384, 0:16], in_=SMALL[:, 448:464]).inc(ST, 16)
                y.wait(ST, ST.n)
                return

            pe_after_m2 = PE.n
            g.wait(PE, pe_after_m2)
            for ch in range(16):
                g.dma_start(out=WB_[:, ch * E:(ch + 1) * E],
                            in_=WCT.ap()[ch * 128:(ch + 1) * 128, :]).inc(WG, 16)
            wl_wc = WG.n
            for ch in range(16):
                g.dma_start(out=WA[:, ch * 1024:(ch + 1) * 1024],
                            in_=WoT.ap()[ch * 128:(ch + 1) * 128, :]).inc(WG, 16)
            wl_wo = WG.n

            HC2 = HC
            m3_xcb_gate = None
            m3_ain_gate = [dv_after_combine, dv_after_combine]
            m3_slz_gate = [dv_after_combine, dv_after_combine]
            m3_st_gate = ST.n
            t.wait(WG, wl_wc)
            for st_i in range(nst):
                if m3_xcb_gate is None:
                    y.wait(DV, dv_after_combine)
                else:
                    y.wait(PE, m3_xcb_gate)
                y.dma_start(out=arb(0, 8192), in_=xcb_s.ap()[st_i]).inc(LD, 16)
                for gi in range(16):
                    bank = gi % 2
                    abuf = 8192 + (gi % 2) * 512
                    bbuf = 9216 + (gi % 2) * 512
                    zbuf = 10240 + (gi % 2) * 256
                    y.wait(DV, m3_ain_gate[gi % 2])
                    y.wait(ST, m3_st_gate)
                    y.dma_start(out=arf(abuf, 512), in_=alp_s.ap()[st_i, gi]).inc(LD, 16)
                    y.dma_start(out=arf(bbuf, 512), in_=bet_s.ap()[st_i, gi]).inc(LD, 16)
                    y.wait(DV, m3_slz_gate[gi % 2])
                    y.dma_start(out=arb(zbuf, 512),
                                in_=slz_s.ap()[st_i][:, gi * 512:(gi + 1) * 512]).inc(LD, 16)
                    t.wait(LD)
                    bank_wait(t, bank)
                    for ke in range(16):
                        t.matmul(PS[bank][:, :],
                                 WB_[:, ke * E + gi * 128:ke * E + gi * 128 + 128],
                                 arb(0, 8192)[:, ke * 512:(ke + 1) * 512],
                                 start=(ke == 0), stop=(ke == 15)).inc(PE, 1 if ke == 15 else 0)
                    s.wait(PE)
                    s.activation(arf(11264, 512), PS[bank][:, :], AF.Tanh).inc(AC, 1)
                    m1_bankfree[bank] = ("AC", AC.n)
                    init = HSC[gi] if st_i == 0 else HC2[gi]
                    v.wait(LD)
                    vfence(v.tensor_tensor_scan(arf(10752, 512), arf(abuf, 512),
                                                arf(bbuf, 512), init,
                                                ALU.mult, ALU.add))
                    v.tensor_copy(HC2[gi], arf(11263, 1))
                    v.wait(AC)
                    v.tensor_tensor(arf(bbuf, 512), arf(10752, 512), arf(11264, 512),
                                    ALU.mult)
                    v.tensor_tensor(arb(4096, 8192)[:, gi * 512:(gi + 1) * 512],
                                    arf(bbuf, 512), arb(zbuf, 512), ALU.mult).inc(DV, 1)
                    m3_ain_gate[gi % 2] = DV.n
                    m3_slz_gate[gi % 2] = DV.n
                t.wait(WG, wl_wo)
                t.wait(DV)
                for j in range(4):
                    bank = 2 + (j % 2)
                    y.wait(DV, DV.n)
                    y.dma_start(out=arf(8192, D),
                                in_=slab2(y1_s, 1 + st_i * 4 + j)).inc(LD, 16)
                    for n in range(2):
                        bank_wait(t, bank)
                        for ke in range(16):
                            t.matmul(PS[bank][:, :],
                                     arb(4096, 8192)[:, ke * 512 + j * 128:ke * 512 + j * 128 + 128],
                                     WA[:, ke * 1024 + n * 512:ke * 1024 + (n + 1) * 512],
                                     start=(ke == 0), stop=(ke == 15)).inc(PE, 1 if ke == 15 else 0)
                        v.wait(PE)
                        v.wait(LD)
                        v.tensor_tensor(arf(9216 + n * 512, 512), PS[bank][:, :],
                                        arf(8192 + n * 512, 512), ALU.add).inc(DV, 1)
                        m1_bankfree[bank] = ("DV", DV.n)
                    g.wait(DV)
                    g.dma_start(out=slab2(x2_s, st_i * 4 + j), in_=arf(9216, D)).inc(ST, 16)
                m3_xcb_gate = PE.n
                m3_ain_gate = [DV.n, DV.n]
                m3_st_gate = ST.n
            y.wait(ST, ST.n)

            if dbg == "x2":
                for c in range(nch):
                    g.dma_start(out=dbg_t.ap()[c * 128:(c + 1) * 128, :],
                                in_=slab2(x2_s, c)).inc(ST, 16)
                y.wait(ST, ST.n)
                return

            pe_after_m3 = PE.n
            first = True
            for ch in range(8):
                if first:
                    y.wait(PE, pe_after_m3)
                    y.wait(WL, wl_imgs)
                    first = False
                y.dma_start(out=WA[:, ch * H:(ch + 1) * H], in_=w1q_s.ap()[ch]).inc(WL, 16)
            wl_f2a = WL.n
            for ch in range(32):
                y.dma_start(out=WB_[:, ch * D:(ch + 1) * D], in_=w2q_s.ap()[ch]).inc(WL, 16)
            wl_f2b = WL.n
            ffn_A("x2", q2_s, nch, 318, (WL, wl_f2a))
            ffn_B("x2", q2_s, nch, 318, "out", (WL, wl_f2b))
            y.wait(ST, ST.n)

        def run_walk(eng):
            def f(handle):
                P.cur = eng
                P.handles[eng] = handle
                for cc_ in chans.values():
                    cc_.n = 0
                emit(P)
            return f

        for eng in ENGINES:
            getattr(block, eng)(run_walk(eng))

    return nc


def _prep_inputs(inputs, n_cores, tpc):
    f32 = np.float32
    x = np.ascontiguousarray(np.asarray(inputs["L0_attn_x"]).reshape(-1, D).astype(f32))
    common = {
        "WiT": np.ascontiguousarray(np.asarray(inputs["L0_attn_Wi"]).T.astype(f32)),
        "WdtT": np.ascontiguousarray(np.asarray(inputs["L0_attn_Wdt"]).T.astype(f32)),
        "WBT": np.ascontiguousarray(np.asarray(inputs["L0_attn_WB"]).T.astype(f32)),
        "WCT": np.ascontiguousarray(np.asarray(inputs["L0_attn_WC"]).T.astype(f32)),
        "WoT": np.ascontiguousarray(np.asarray(inputs["L0_attn_Wo"]).T.astype(f32)),
        "w1T": np.ascontiguousarray(np.asarray(inputs["L0_ffn_w1"]).T.astype(f32)),
        "w2T": np.ascontiguousarray(np.asarray(inputs["L0_ffn_w2"]).T.astype(f32)),
        "ln1w_r": np.asarray(inputs["L0_ffn_ln1_w"]).reshape(1, D).astype(f32),
        "ln1b_r": np.asarray(inputs["L0_ffn_ln1_b"]).reshape(1, D).astype(f32),
        "b1_r": np.asarray(inputs["L0_ffn_b1"]).reshape(1, H).astype(f32),
        "ln2w_r": np.asarray(inputs["L0_ffn_ln2_w"]).reshape(1, H).astype(f32),
        "ln2b_r": np.asarray(inputs["L0_ffn_ln2_b"]).reshape(1, H).astype(f32),
        "b2_r": np.asarray(inputs["L0_ffn_b2"]).reshape(1, D).astype(f32),
        "bdt_c": np.ascontiguousarray(
            np.asarray(inputs["L0_attn_bdt"]).reshape(16, 128).T.astype(f32)),
        "alog_c": np.ascontiguousarray(
            np.asarray(inputs["L0_attn_A_log"]).reshape(16, 128).T.astype(f32)),
        "bconv_c": np.ascontiguousarray(
            np.asarray(inputs["L0_attn_bconv"]).reshape(16, 128).T.astype(f32)),
        "wconv_c": np.ascontiguousarray(
            np.asarray(inputs["L0_attn_Wconv"]).reshape(16, 128, 4)
            .transpose(1, 0, 2).reshape(128, 64).astype(f32)),
        "ones_r": np.ones((1, 128), f32),
        "ones_c": np.ones((128, 1), f32),
        "idt_b": np.eye(128, dtype=f32).astype(ml_dtypes.bfloat16),
        "idf_f": np.eye(128, dtype=f32),
    }
    ims = []
    for i in range(n_cores):
        st = i * tpc
        im = dict(common)
        im["x_own"] = x[st:st + tpc]
        im["x_halo"] = x[st - 128:st] if i > 0 else np.ascontiguousarray(x[0:128])
        im["halo_m"] = np.full((128, 1), 0.0 if i == 0 else 1.0, f32)
        pm = np.zeros((n_cores, 1), f32)
        pm[:i] = 1.0
        im["prefm"] = pm
        mm = np.zeros((n_cores, n_cores), f32)
        for j in range(i):
            for k in range(j + 1, i):
                mm[k, j] = 1.0
        im["mm8"] = mm
        ims.append(im)
    return ims


def kernel(**inputs):
    n_cores = 8
    tpc = L_FULL // n_cores
    nc = build_nc(n_cores, tpc)
    ims = _prep_inputs(inputs, n_cores, tpc)
    res = run_bass_kernel_spmd(nc, ims, core_ids=list(range(n_cores)))
    out = np.concatenate([res.results[i]["out"] for i in range(n_cores)], axis=0)
    return out.reshape(1, L_FULL, D)


# version 06
# speedup vs baseline: 1.0832x; 1.0438x over previous
import contextlib
import numpy as np
import ml_dtypes
import concourse.bass as bass
import concourse.mybir as mybir
from concourse.bass_utils import run_bass_kernel_spmd

F32 = mybir.dt.float32
BF16 = mybir.dt.bfloat16
I32 = mybir.dt.int32
AF = mybir.ActivationFunctionType
ALU = mybir.AluOpType
AX = mybir.AxisListType

L_FULL, D, E, H = 16384, 1024, 2048, 4096
EPS = 1e-5
ENGINES = ("tensor", "vector", "scalar", "gpsimd", "sync")


class Chan:
    __slots__ = ("name", "sem", "n")

    def __init__(self, name):
        self.name = name
        self.sem = None
        self.n = 0


class Inst:
    __slots__ = ("h",)

    def __init__(self, h):
        self.h = h

    def inc(self, chan, k=1):
        if k <= 0:
            return self
        if self.h is not None:
            self.h.then_inc(chan.sem, k)
        chan.n += k
        return self


class Eng:
    def __init__(self, prog, name):
        object.__setattr__(self, "_p", prog)
        object.__setattr__(self, "_n", name)

    def wait(self, chan, val=None):
        v = chan.n if val is None else val
        if self._p.cur == self._n:
            self._p.handles[self._n].wait_ge(chan.sem, v)

    def __getattr__(self, op):
        p = self._p
        nm = self._n

        def call(*a, **kw):
            if p.cur == nm:
                return Inst(getattr(p.handles[nm], op)(*a, **kw))
            return Inst(None)

        return call


class Prog:
    def __init__(self):
        self.cur = None
        self.handles = {}
        self.t = Eng(self, "tensor")
        self.v = Eng(self, "vector")
        self.s = Eng(self, "scalar")
        self.g = Eng(self, "gpsimd")
        self.y = Eng(self, "sync")


def build_nc(n_cores, tpc, dbg=None):
    nch = tpc // 128
    nst = tpc // 512
    nc = bass.Bass("TRN2", target_bir_lowering=False,
                   detect_race_conditions=False)

    def din(name, shape, dt=F32):
        return nc.dram_tensor(name, list(shape), dt, kind="ExternalInput")

    x_own = din("x_own", [tpc, D])
    x_halo = din("x_halo", [128, D])
    WiT = din("WiT", [D, 2 * E])
    WdtT = din("WdtT", [E, E])
    WBT = din("WBT", [E, E])
    WCT = din("WCT", [E, E])
    WoT = din("WoT", [E, D])
    w1T = din("w1T", [D, H])
    w2T = din("w2T", [H, D])
    ln1w_r = din("ln1w_r", [1, D])
    ln1b_r = din("ln1b_r", [1, D])
    b1_r = din("b1_r", [1, H])
    ln2w_r = din("ln2w_r", [1, H])
    ln2b_r = din("ln2b_r", [1, H])
    b2_r = din("b2_r", [1, D])
    bdt_c = din("bdt_c", [128, 16])
    alog_c = din("alog_c", [128, 16])
    bconv_c = din("bconv_c", [128, 16])
    wconv_c = din("wconv_c", [128, 64])
    ones_r = din("ones_r", [1, 128])
    ones_c = din("ones_c", [128, 1])
    idt_b = din("idt_b", [128, 128], BF16)
    idf_f = din("idf_f", [128, 128])
    halo_m = din("halo_m", [128, 1])
    prefm = din("prefm", [n_cores, 1])
    mm8 = din("mm8", [n_cores, n_cores])

    out_t = nc.dram_tensor("out", [tpc, D], F32, kind="ExternalOutput")

    y1_s = nc.dram_tensor("y1_s", [nch + 1, 128, D], F32)
    q1_s = nc.dram_tensor("q1_s", [nch + 1, 128, H], BF16)
    q2_s = nc.dram_tensor("q2_s", [nch, 128, H], BF16)
    w1q_s = nc.dram_tensor("w1q_s", [8, 128, H], BF16)
    w2q_s = nc.dram_tensor("w2q_s", [32, 128, D], BF16)
    xcb_s = nc.dram_tensor("xcb_s", [nst, 128, 16 * 512], BF16)
    slz_s = nc.dram_tensor("slz_s", [nst, 128, 16 * 512], BF16)
    alp_s = nc.dram_tensor("alp_s", [nst, 16, 128, 512], F32)
    bet_s = nc.dram_tensor("bet_s", [nst, 16, 128, 512], F32)
    x2_s = nc.dram_tensor("x2_s", [nch, 128, D], F32)
    cc_in = nc.dram_tensor("cc_in", [2, E], F32)
    cc_out = nc.dram_tensor("cc_out", [2 * n_cores, E], F32)

    dbg_t = None
    if dbg == "y1":
        dbg_t = nc.dram_tensor("dbg", [(nch + 1) * 128, D], F32, kind="ExternalOutput")
    elif dbg == "m1":
        dbg_t = nc.dram_tensor("dbg", [2 * nst * 128, 16 * 512], F32, kind="ExternalOutput")
    elif dbg == "m2":
        dbg_t = nc.dram_tensor("dbg", [2 * nst * 16 * 128 + 3 * 128, 512], F32, kind="ExternalOutput")
    elif dbg == "x2":
        dbg_t = nc.dram_tensor("dbg", [nch * 128, D], F32, kind="ExternalOutput")
    elif dbg == "q1":
        dbg_t = nc.dram_tensor("dbg", [43 * 128, H], F32, kind="ExternalOutput")
    elif dbg == "f1a":
        dbg_t = nc.dram_tensor("dbg", [8 * 128, H], F32, kind="ExternalOutput")
    elif dbg == "wdt":
        dbg_t = nc.dram_tensor("dbg", [6 * 128, 2048], F32, kind="ExternalOutput")
    elif dbg == "m2g":
        dbg_t = nc.dram_tensor("dbg", [24 * 128, 512], F32, kind="ExternalOutput")

    chans = {n: Chan(n) for n in ("LD", "ST", "PE", "DV", "AC", "CC", "WL", "WG",
                                  "QA", "QB", "QC", "QD")}

    es = contextlib.ExitStack()
    with es:
        def sbt(name, shape, dt=F32):
            return es.enter_context(nc.sbuf_tensor(name, list(shape), dt))

        WA = sbt("WA", [128, 32768], BF16)
        WB_ = sbt("WB_", [128, 32768], BF16)
        AR = sbt("AR", [128, 11776], F32)
        BCln2w = sbt("BCln2w", [128, H], BF16)
        BCln2b = sbt("BCln2b", [128, H], BF16)
        BCb1 = sbt("BCb1", [128, H], BF16)
        BCb2h = sbt("BCb2h", [128, D], BF16)
        BCln1w = sbt("BCln1w", [128, D], BF16)
        BCln1b = sbt("BCln1b", [128, D], BF16)
        SMALL = sbt("SMALL", [128, 480], F32)
        IDT = sbt("IDT", [128, 128], BF16)
        IDF = sbt("IDF", [128, 128], F32)
        PRM = sbt("PRM", [128, 128], F32)
        PS = [es.enter_context(nc.psum_tensor(f"ps{i}", [128, 512], F32))
              for i in range(4)]
        PSX = es.enter_context(nc.psum_tensor("psx", [128, 512], F32))
        PSY = es.enter_context(nc.psum_tensor("psy", [128, 512], F32))
        PT = [es.enter_context(nc.psum_tensor(f"pt{i}", [128, 1024], BF16))
              for i in range(2)]

        for cn in chans:
            chans[cn].sem = es.enter_context(nc.semaphore(cn))

        block = es.enter_context(nc.Block())
        P = Prog()

        def arf(o, w):
            return AR[:, o:o + w]

        def arb(o, w):
            return AR[:, o:o + (w + 1) // 2].bitcast(BF16)[:, :w]

        def ari(o, w):
            return AR[:, o:o + w].bitcast(I32)[:, :w]

        def smc(o, w=1):
            return SMALL[:, o:o + w]

        C = chans

        ONES_ROW = SMALL[0:1, 0:128]

        def emit(P):
            LD, ST, PE, DV, AC, CCh, WL, WG = (C["LD"], C["ST"], C["PE"], C["DV"],
                                               C["AC"], C["CC"], C["WL"], C["WG"])
            t, v, s, g, y = P.t, P.v, P.s, P.g, P.y

            def vfence(inst):
                inst.inc(DV, 1)
                v.wait(DV)

            def ln_rstd(src, width, cb):
                nj = width // 512
                last = None
                for j in range(nj):
                    last = v.bn_stats(smc(cb + j * 6, 6), src(j))
                vfence(last)
                vfence(v.bn_aggr(smc(cb + 48, 2), smc(cb, 6 * nj)))
                vfence(v.tensor_scalar(smc(cb + 50), smc(cb + 49), EPS, None, ALU.add))
                vfence(v.tensor_scalar(smc(cb + 55), smc(cb + 50), -0.5, None, ALU.mult))
                iv = smc(cb + 50).bitcast(I32)
                iy = smc(cb + 51).bitcast(I32)
                vfence(v.tensor_scalar(iy, iv, 1, None, ALU.arith_shift_right))
                vfence(v.tensor_scalar(iy, iy, -1, None, ALU.bitwise_xor))
                vfence(v.tensor_scalar(iy, iy, 0x5F3759E0, None, ALU.add))
                for _ in range(3):
                    s.wait(DV)
                    s.activation(smc(cb + 53), smc(cb + 51), AF.Square).inc(AC, 1)
                    v.wait(AC)
                    vfence(v.tensor_scalar(smc(cb + 54), smc(cb + 53), smc(cb + 55),
                                           1.5, ALU.mult, ALU.add))
                    vfence(v.tensor_scalar(smc(cb + 51), smc(cb + 54), smc(cb + 51),
                                           None, ALU.mult))
                vfence(v.tensor_scalar(smc(cb + 52), smc(cb + 51), smc(cb + 48),
                                       -1.0, ALU.mult, ALU.mult))
                return smc(cb + 51), smc(cb + 52)

            def slab2(ten, i):
                return ten.ap()[i]

            y.dma_start(out=IDT[:, :], in_=idt_b.ap()).inc(LD, 16)
            y.dma_start(out=IDF[:, :], in_=idf_f.ap()).inc(LD, 16)
            y.dma_start(out=PRM[:, 0:16], in_=bdt_c.ap()).inc(LD, 16)
            y.dma_start(out=PRM[:, 16:32], in_=alog_c.ap()).inc(LD, 16)
            y.dma_start(out=PRM[:, 32:48], in_=bconv_c.ap()).inc(LD, 16)
            y.dma_start(out=PRM[:, 48:112], in_=wconv_c.ap()).inc(LD, 16)
            y.dma_start(out=PRM[:, 112:113], in_=halo_m.ap()).inc(LD, 16)
            y.dma_start(out=PRM[:, 113:114], in_=ones_c.ap()).inc(LD, 16)
            y.dma_start(out=SMALL[0:1, 0:128], in_=ones_r.ap()).inc(LD, 16)
            ld_p0 = LD.n

            o_ln1w, o_ln1b, o_ln2w, o_ln2b, o_b2, o_b1 = 0, 1024, 2048, 6144, 10240, 11264 - 4096
            y.dma_start(out=AR[0:1, o_ln1w:o_ln1w + D], in_=ln1w_r.ap()).inc(LD, 16)
            y.dma_start(out=AR[0:1, o_ln1b:o_ln1b + D], in_=ln1b_r.ap()).inc(LD, 16)
            y.dma_start(out=AR[0:1, o_b2:o_b2 + D], in_=b2_r.ap()).inc(LD, 16)
            y.dma_start(out=AR[0:1, o_ln2w:o_ln2w + H], in_=b1_r.ap()).inc(LD, 16)
            ld_rows1 = LD.n

            def bcast_row(dst_bf, src_off, w, scale, ldv):
                for j in range(w // 512):
                    t.wait(LD, ldv)
                    t.matmul(PSX[:, :], ONES_ROW,
                             AR[0:1, src_off + j * 512:src_off + (j + 1) * 512],
                             start=True, stop=True).inc(PE, 1)
                    v.wait(PE)
                    if scale != 1.0:
                        v.tensor_scalar(dst_bf[:, j * 512:(j + 1) * 512],
                                        PSX[:, :], scale, None, ALU.mult).inc(DV, 1)
                    else:
                        v.tensor_copy(dst_bf[:, j * 512:(j + 1) * 512],
                                      PSX[:, :]).inc(DV, 1)
                    t.wait(DV)

            bcast_row(BCln1w, o_ln1w, D, 1.0, ld_rows1)
            bcast_row(BCln1b, o_ln1b, D, 1.0, ld_rows1)
            bcast_row(BCb2h, o_b2, D, 0.5, ld_rows1)
            bcast_row(BCb1, o_ln2w, H, 1.0, ld_rows1)
            y.wait(PE, C["PE"].n)
            y.dma_start(out=AR[0:1, o_ln2w:o_ln2w + H], in_=ln2w_r.ap()).inc(LD, 16)
            y.dma_start(out=AR[0:1, o_ln2b:o_ln2b + H], in_=ln2b_r.ap()).inc(LD, 16)
            ld_rows2 = LD.n
            bcast_row(BCln2w, o_ln2w, H, 1.0, ld_rows2)
            bcast_row(BCln2b, o_ln2b, H, 1.0, ld_rows2)
            pe_bcast_done = PE.n
            dv_bcast_done = DV.n

            s.wait(LD, ld_p0)
            s.activation(PRM[:, 16:32], PRM[:, 16:32], AF.Exp).inc(AC, 1)
            v.wait(AC)
            v.tensor_scalar(PRM[:, 16:32], PRM[:, 16:32], -1.0, None, ALU.mult).inc(DV, 1)

            def wbf(i):
                return WB_[:, i * 8192:(i + 1) * 8192].bitcast(F32)

            def quant_weight(wT, nchunks, wcols, img, pair_off, dst, gate0):
                if wcols == 4096:
                    bufs = [lambda: arf(0, 4096), lambda: wbf(0),
                            lambda: arf(4096, 4096), lambda: wbf(1)]
                else:
                    bufs = [lambda: arf(0, 1024), lambda: arf(1024, 1024),
                            lambda: arf(2048, 1024), lambda: arf(3072, 1024)]
                qeng = [y, g, y, g]
                qch = [C["QA"], C["QB"], C["QC"], C["QD"]]

                def issue(ch, gates):
                    b = ch % 4
                    e = qeng[b]
                    for gc, gv in gates:
                        e.wait(gc, gv)
                    e.dma_start(out=bufs[b](),
                                in_=wT.ap()[ch * 128:(ch + 1) * 128, :]).inc(qch[b], 16)
                    return qch[b].n

                ldv = {}
                dvred = []
                for ch in range(min(4, nchunks)):
                    ldv[ch] = issue(ch, gate0)
                for ch in range(nchunks):
                    v.wait(qch[ch % 4], ldv[ch])
                    v.tensor_reduce(smc(140 + ch), bufs[ch % 4](), AX.X, ALU.add,
                                    apply_absolute_value=True).inc(DV, 1)
                    dvred.append(DV.n)
                    if ch + 4 < nchunks:
                        ldv[ch + 4] = issue(ch + 4, [(DV, dvred[ch])])
                vfence(v.tensor_copy(arf(8192, 512), arf(0, 512)))
                v.tensor_reduce(smc(139), SMALL[:, 140:140 + nchunks], AX.X, ALU.add).inc(DV, 1)
                t.wait(DV)
                t.matmul(PSX[0:1, 0:1], smc(139), PRM[:, 113:114],
                         start=True, stop=True).inc(PE, 1)
                v.wait(PE)
                sc_c = SMALL[0:1, 138:139]
                sc_s = SMALL[0:1, 137:138]
                vfence(v.tensor_scalar(sc_c, PSX[0:1, 0:1],
                                       1.0 / (nchunks * 128 * wcols), None, ALU.mult))
                vfence(v.tensor_scalar(sc_c, sc_c, 1e-5, None, ALU.max))
                v.reciprocal(sc_s, sc_c).inc(DV, 1)
                t.wait(DV)
                t.matmul(PSX[:, 0:2], ONES_ROW, SMALL[0:1, 137:139],
                         start=True, stop=True).inc(PE, 1)
                v.wait(PE)
                v.tensor_copy(smc(pair_off, 2), PSX[:, 0:2]).inc(DV, 1)
                v.wait(DV)
                t.wait(DV)
                pe_comb = PE.n
                scale_bc = smc(pair_off)
                nh = max(1, wcols // 2048)
                hw = wcols // nh
                dvq = []
                for ch in range(min(4, nchunks)):
                    ldv[ch] = issue(ch, [(DV, dvred[max(ch, nchunks - 4 + ch)])])
                for ch in range(nchunks):
                    v.wait(qch[ch % 4], ldv[ch])
                    for hhh in range(nh):
                        srcw = bufs[ch % 4]()[:, hhh * hw:(hhh + 1) * hw]
                        v.tensor_scalar(ari(8192, hw), srcw, scale_bc, None, ALU.mult)
                        v.tensor_copy(srcw, ari(8192, hw))
                        dslice = dst[:, ch * wcols + hhh * hw:ch * wcols + (hhh + 1) * hw]
                        v.tensor_scalar(dslice, srcw,
                                        -1.0, 1.0, ALU.max, ALU.min).inc(DV, 1)
                        s.wait(DV)
                        s.dma_start(out=img.ap()[ch][:, hhh * hw:(hhh + 1) * hw],
                                    in_=dslice).inc(WL, 16)
                    dvq.append(DV.n)
                    if ch + 4 < nchunks:
                        ldv[ch + 4] = issue(ch + 4, [(DV, dvq[ch])])
                return DV.n, pe_comb

            dv_w1q, pe_comb1 = quant_weight(w1T, 8, H, w1q_s, 172, WA,
                                            [(PE, pe_bcast_done), (DV, dv_bcast_done)])
            if dbg == "q1":
                g.wait(DV, DV.n)
                g.dma_start(out=dbg_t.ap()[42 * 128:43 * 128, 0:64],
                            in_=SMALL[:, 128:192]).inc(ST, 16)
                v.wait(ST, ST.n)
            v.wait(PE, pe_comb1)
            dv_w2q, _ = quant_weight(w2T, 32, D, w2q_s, 174, WB_, [(DV, dv_w1q)])
            c1_col = smc(173)
            c2_col = smc(175)
            wl_imgs = WL.n

            def ffn_A(src_kind, q_img, nchunks, sout2_base, w_gate):
                gch, gval = w_gate
                xfree = [None, None]
                ptfree = [0, 0]
                bankfree = [0, 0]
                hevfree = [0, 0]
                q2free = 0
                for c in range(nchunks):
                    xbuf = (c % 2) * 1024
                    if xfree[c % 2] is not None:
                        y.wait(AC, xfree[c % 2])
                    if src_kind == "x":
                        src = x_halo.ap() if c == 0 else x_own.ap()[(c - 1) * 128:c * 128, :]
                    else:
                        src = slab2(x2_s, c)
                    y.dma_start(out=arf(xbuf, D), in_=src).inc(LD, 16)
                    v.wait(LD)
                    rstd1, nmb1 = ln_rstd(lambda j: arf(xbuf + j * 512, 512), D, 200)
                    s.wait(DV)
                    s.activation(arb(2048, D), arf(xbuf, D), AF.Identity,
                                 bias=nmb1, scale=rstd1).inc(AC, 1)
                    xfree[c % 2] = AC.n
                    v.wait(AC)
                    v.tensor_tensor(arb(2560, D), arb(2048, D), BCln1w[:, :], ALU.mult)
                    v.tensor_tensor(arb(3072, D), arb(2560, D), BCln1b[:, :], ALU.add)
                    vfence(v.tensor_reduce(smc(229), arb(3072, D), AX.X, ALU.max,
                                           apply_absolute_value=True))
                    vfence(v.tensor_scalar(smc(229), smc(229), EPS, None, ALU.max))
                    vfence(v.reciprocal(smc(230), smc(229)))
                    v.tensor_scalar(smc(231), smc(229), c1_col, 1.0 / 127.0,
                                    ALU.mult, ALU.mult).inc(DV, 1)
                    v.tensor_scalar(ari(3584, D), arb(3072, D), 127.0, smc(230),
                                    ALU.mult, ALU.mult)
                    v.tensor_copy(arb(4608, D), ari(3584, D)).inc(DV, 1)
                    for k in range(8):
                        bk = k % 2
                        t.wait(DV) if k == 0 else None
                        t.wait(DV, ptfree[bk]) if k >= 2 else None
                        t.transpose(PT[bk][:, 0:128],
                                    arb(4608, D)[:, k * 128:(k + 1) * 128],
                                    IDT[:, :]).inc(PE, 1)
                        v.wait(PE)
                        v.tensor_copy(arb(5120, D)[:, k * 128:(k + 1) * 128],
                                      PT[bk][:, 0:128]).inc(DV, 1)
                        ptfree[bk] = DV.n
                    t.wait(gch, gval)
                    s.wait(ST, q2free)
                    for n in range(8):
                        bank = n % 2
                        t.wait(DV, bankfree[bank])
                        for k in range(8):
                            t.matmul(PS[bank][:, :],
                                     arb(5120, D)[:, k * 128:(k + 1) * 128],
                                     WA[:, k * H + n * 512:k * H + (n + 1) * 512],
                                     start=(k == 0), stop=(k == 7)).inc(PE, 1 if k == 7 else 0)
                        v.wait(PE)
                        v.wait(AC, hevfree[n % 2])
                        v.scalar_tensor_tensor(arf(10752 + (n % 2) * 512, 512),
                                               PS[bank][:, :], smc(231),
                                               BCb1[:, n * 512:(n + 1) * 512],
                                               ALU.mult, ALU.add).inc(DV, 1)
                        bankfree[bank] = DV.n
                        s.wait(DV)
                        s.activation(arb(5632, H)[:, n * 512:(n + 1) * 512],
                                     arf(10752 + (n % 2) * 512, 512),
                                     AF.Gelu).inc(AC, 1)
                        hevfree[n % 2] = AC.n
                    v.wait(AC)
                    rstd2, nmb2 = ln_rstd(lambda j: arb(5632, H)[:, j * 512:(j + 1) * 512],
                                          H, 340)
                    s.wait(DV)
                    s.activation(arb(5632, H), arb(5632, H), AF.Identity,
                                 bias=nmb2, scale=rstd2).inc(AC, 1)
                    v.wait(AC)
                    v.tensor_tensor(arb(7680, H), arb(5632, H), BCln2w[:, :], ALU.mult)
                    v.tensor_tensor(arb(5632, H), arb(7680, H), BCln2b[:, :], ALU.add)
                    vfence(v.tensor_reduce(smc(298), arb(5632, H), AX.X, ALU.max,
                                           apply_absolute_value=True))
                    vfence(v.tensor_scalar(smc(298), smc(298), EPS, None, ALU.max))
                    vfence(v.reciprocal(smc(299), smc(298)))
                    v.tensor_scalar(SMALL[:, sout2_base + c:sout2_base + c + 1],
                                    smc(298), c2_col, 1.0 / 127.0,
                                    ALU.mult, ALU.mult).inc(DV, 1)
                    last = None
                    for qq in range(4):
                        v.tensor_scalar(ari(9728, 1024),
                                        arb(5632, H)[:, qq * 1024:(qq + 1) * 1024],
                                        127.0, smc(299), ALU.mult, ALU.mult)
                        last = v.tensor_copy(arb(3584, H)[:, qq * 1024:(qq + 1) * 1024],
                                             ari(9728, 1024))
                    last.inc(DV, 1)
                    for kk in range(8):
                        bk = kk % 2
                        t.wait(DV) if kk == 0 else None
                        t.wait(DV, ptfree[bk]) if kk >= 2 else None
                        for j in range(4):
                            k = kk * 4 + j
                            t.transpose(PT[bk][:, j * 128:(j + 1) * 128],
                                        arb(3584, H)[:, k * 128:(k + 1) * 128],
                                        IDT[:, :]).inc(PE, 1 if j == 3 else 0)
                        v.wait(PE)
                        v.tensor_copy(arb(5632, H)[:, kk * 512:(kk + 1) * 512],
                                      PT[bk][:, 0:512]).inc(DV, 1)
                        ptfree[bk] = DV.n
                    g.wait(DV)
                    g.dma_start(out=slab2(q_img, c), in_=arb(5632, H)).inc(ST, 16)
                    q2free = ST.n
                y.wait(ST, ST.n)

            def ffn_B(src_kind, q_img, nchunks, sout2_base, dst_kind, w_gate):
                gch, gval = w_gate
                qfree = [0, 0]
                xfree = [0, 0]
                mmfree = [0, 0]
                for c in range(nchunks):
                    qbuf = 3584 if c % 2 == 0 else 5632
                    xbuf = (c % 2) * 1024
                    y.wait(PE, qfree[c % 2])
                    y.dma_start(out=arb(qbuf, H), in_=slab2(q_img, c)).inc(LD, 16)
                    if src_kind == "x":
                        src = x_halo.ap() if c == 0 else x_own.ap()[(c - 1) * 128:c * 128, :]
                    else:
                        src = slab2(x2_s, c)
                    y.wait(DV, xfree[c % 2])
                    y.dma_start(out=arf(xbuf, D), in_=src).inc(LD, 16)
                    v.wait(LD)
                    v.tensor_scalar(smc(273), SMALL[:, sout2_base + c:sout2_base + c + 1],
                                    0.5, None, ALU.mult).inc(DV, 1)
                    t.wait(gch, gval)
                    t.wait(DV)
                    for n in range(2):
                        bank = 2 + n
                        t.wait(DV, mmfree[n])
                        for k in range(32):
                            t.matmul(PS[bank][:, :],
                                     arb(qbuf, H)[:, k * 128:(k + 1) * 128],
                                     WB_[:, k * D + n * 512:k * D + (n + 1) * 512],
                                     start=(k == 0), stop=(k == 31)).inc(PE, 1 if k == 31 else 0)
                        v.wait(PE)
                        v.scalar_tensor_tensor(arf(10752, 512), PS[bank][:, :],
                                               smc(273),
                                               BCb2h[:, n * 512:(n + 1) * 512],
                                               ALU.mult, ALU.add)
                        v.tensor_tensor(arf(2560 + n * 512, 512), arf(10752, 512),
                                        arf(xbuf + n * 512, 512), ALU.add).inc(DV, 1)
                        mmfree[n] = DV.n
                    qfree[c % 2] = PE.n
                    xfree[c % 2] = DV.n
                    g.wait(DV)
                    if dst_kind == "y1":
                        g.dma_start(out=slab2(y1_s, c), in_=arf(2560, D)).inc(ST, 16)
                    else:
                        g.dma_start(out=out_t.ap()[c * 128:(c + 1) * 128, :],
                                    in_=arf(2560, D)).inc(ST, 16)
                y.wait(ST, ST.n)

            y.wait(DV, dv_w2q)
            ffn_A("x", q1_s, 1 if dbg == "f1a" else nch + 1, 300, (DV, dv_w1q))
            if dbg == "f1a":
                g.wait(DV, DV.n)
                g.wait(AC, AC.n)
                g.dma_start(out=dbg_t.ap()[0:128, 0:D], in_=arb(2048, D)).inc(ST, 16)
                g.dma_start(out=dbg_t.ap()[128:256, 0:D], in_=arb(3072, D)).inc(ST, 16)
                g.dma_start(out=dbg_t.ap()[256:384, 0:D], in_=arb(4608, D)).inc(ST, 16)
                g.dma_start(out=dbg_t.ap()[384:512, :], in_=arb(5632, H)).inc(ST, 16)
                g.dma_start(out=dbg_t.ap()[512:640, :], in_=arb(7680, H)).inc(ST, 16)
                g.dma_start(out=dbg_t.ap()[640:768, :], in_=arb(3584, H)).inc(ST, 16)
                g.dma_start(out=dbg_t.ap()[768:896, 0:120], in_=SMALL[:, 200:320]).inc(ST, 16)
                g.dma_start(out=dbg_t.ap()[896:1024, 0:512], in_=arf(10752, 512)).inc(ST, 16)
                y.wait(ST, ST.n)
                return
            if dbg == "q1":
                for ch in range(8):
                    g.dma_start(out=dbg_t.ap()[ch * 128:(ch + 1) * 128, :],
                                in_=w1q_s.ap()[ch]).inc(ST, 16)
                for ch in range(32):
                    g.dma_start(out=dbg_t.ap()[(8 + ch) * 128:(9 + ch) * 128, 0:D],
                                in_=w2q_s.ap()[ch]).inc(ST, 16)
                g.dma_start(out=dbg_t.ap()[40 * 128:41 * 128, :],
                            in_=slab2(q1_s, 3)).inc(ST, 16)
                g.wait(DV, DV.n)
                g.dma_start(out=dbg_t.ap()[41 * 128:42 * 128, 0:33],
                            in_=SMALL[:, 300:333]).inc(ST, 16)
                y.wait(ST, ST.n)
                return
            ffn_B("x", q1_s, nch + 1, 300, "y1", (DV, dv_w2q))

            if dbg == "y1":
                for c in range(nch + 1):
                    g.dma_start(out=dbg_t.ap()[c * 128:(c + 1) * 128, :],
                                in_=slab2(y1_s, c)).inc(ST, 16)
                y.wait(ST, ST.n)
                return

            pe_after_f1 = PE.n
            g.wait(PE, pe_after_f1)
            g.wait(WL, wl_imgs)
            for ch in range(16):
                g.dma_start(out=WB_[:, ch * E:(ch + 1) * E],
                            in_=WdtT.ap()[ch * 128:(ch + 1) * 128, :]).inc(WG, 16)
            wl_wdt = WG.n
            for ch in range(8):
                g.dma_start(out=WA[:, ch * 4096:(ch + 1) * 4096],
                            in_=WiT.ap()[ch * 128:(ch + 1) * 128, :]).inc(WG, 16)
            wl_wi = WG.n

            m1_ptfree = [0, 0]
            m1_ybuf_gate = [0, 0]
            m1_xhb_gate = [0]

            def m1_ln_chunk(slab_idx, parity, lny_col):
                ybuf = 6208 + parity * 1024
                y.wait(AC, m1_ybuf_gate[parity])
                y.dma_start(out=arf(ybuf, D), in_=slab2(y1_s, slab_idx)).inc(LD, 16)
                v.wait(LD)
                rstdm, nmbm = ln_rstd(lambda j: arf(ybuf + j * 512, 512), D, 200)
                s.wait(DV)
                s.wait(PE, m1_xhb_gate[0])
                s.activation(arb(10304, D), arf(ybuf, D), AF.Identity,
                             bias=nmbm, scale=rstdm).inc(AC, 1)
                m1_ybuf_gate[parity] = AC.n
                t.wait(AC)
                for k in range(8):
                    bk = k % 2
                    if k >= 2:
                        t.wait(DV, m1_ptfree[bk])
                    t.transpose(PT[bk][:, 0:128],
                                arb(10304, D)[:, k * 128:(k + 1) * 128],
                                IDT[:, :]).inc(PE, 1)
                    v.wait(PE)
                    v.tensor_copy(arb(4160, 4096)[:, k * 512 + lny_col:k * 512 + lny_col + 128],
                                  PT[bk][:, 0:128]).inc(DV, 1)
                    m1_ptfree[bk] = DV.n
                m1_xhb_gate[0] = PE.n

            X1R = arb(0, 16 * 520)

            m1_ln_chunk(0, 0, 0)
            t.wait(WG, wl_wi)
            m1_bankfree = [("DV", 0), ("DV", 0), ("DV", 0), ("DV", 0)]

            def bank_wait(eng, bank):
                cname, val = m1_bankfree[bank]
                eng.wait(C[cname], val)

            for gi in range(16):
                bank = gi % 2
                base = gi * 520
                bank_wait(t, bank)
                for k in range(8):
                    t.matmul(PS[bank][:, 0:4],
                             WA[:, k * 4096 + gi * 128:k * 4096 + gi * 128 + 128],
                             arb(4160, 4096)[:, k * 512 + 124:k * 512 + 128],
                             start=(k == 0), stop=(k == 7)).inc(PE, 1 if k == 7 else 0)
                v.wait(PE)
                v.tensor_scalar(X1R[:, base:base + 4], PS[bank][:, 0:4],
                                PRM[:, 112:113], None, ALU.mult).inc(DV, 1)
                m1_bankfree[bank] = ("DV", DV.n)
            last = None
            for gi in range(16):
                base = gi * 520
                for k in range(4):
                    last = v.tensor_scalar(arf(10816 + gi * 4 + k, 1),
                                           X1R[:, base + k:base + k + 1],
                                           PRM[:, 48 + gi * 4 + k:49 + gi * 4 + k],
                                           None, ALU.mult)
            vfence(last)
            for gi in range(16):
                last = v.tensor_reduce(arf(10880 + gi, 1), arf(10816 + gi * 4, 4),
                                       AX.X, ALU.add)
            vfence(last)
            vfence(v.tensor_tensor(arf(10896, 16), arf(10880, 16),
                                   PRM[:, 32:48], ALU.add))
            s.wait(DV)
            s.activation(arf(10928, 16), arf(10896, 16), AF.Silu).inc(AC, 1)
            v.wait(AC)
            XCH = SMALL[:, 386:394].bitcast(BF16)[:, 0:16]
            v.tensor_scalar(XCH, arf(10928, 16), PRM[:, 112:113], None,
                            ALU.mult).inc(DV, 1)

            m1_xco_gate = [0, 0]
            m1_szo_gate = [0, 0]
            m1_cv = [0]
            for st_i in range(nst):
                for i in range(4):
                    m1_ln_chunk(1 + st_i * 4 + i, i % 2, i * 128)
                for gi in range(16):
                    bank = gi % 2
                    base = gi * 520
                    bank_wait(t, bank)
                    for k in range(8):
                        t.matmul(PS[bank][:, :],
                                 WA[:, k * 4096 + gi * 128:k * 4096 + gi * 128 + 128],
                                 arb(4160, 4096)[:, k * 512:(k + 1) * 512],
                                 start=(k == 0), stop=(k == 7)).inc(PE, 1 if k == 7 else 0)
                    v.wait(PE)
                    v.tensor_copy(X1R[:, base + 4:base + 516], PS[bank][:, :]).inc(DV, 1)
                    m1_bankfree[bank] = ("DV", DV.n)
                    v.wait(AC, m1_cv[0])
                    v.tensor_scalar(arf(9280, 512), X1R[:, base + 1:base + 513],
                                    PRM[:, 48 + gi * 4:49 + gi * 4], None, ALU.mult)
                    for k in range(1, 4):
                        dsto = 9792 if k % 2 == 1 else 9280
                        srco = 9280 if k % 2 == 1 else 9792
                        v.scalar_tensor_tensor(arf(dsto, 512),
                                               X1R[:, base + 1 + k:base + 513 + k],
                                               PRM[:, 48 + gi * 4 + k:49 + gi * 4 + k],
                                               arf(srco, 512), ALU.mult, ALU.add)
                    v.tensor_scalar(arf(9280, 512), arf(9792, 512),
                                    PRM[:, 32 + gi:33 + gi], None, ALU.add).inc(DV, 1)
                    v.tensor_copy(X1R[:, base:base + 4],
                                  X1R[:, base + 512:base + 516])
                    xco = 8256 + (gi % 2) * 256
                    s.wait(DV)
                    s.wait(ST, m1_xco_gate[gi % 2])
                    s.activation(arb(xco, 512), arf(9280, 512), AF.Silu).inc(AC, 1)
                    m1_cv[0] = AC.n
                    g.wait(AC)
                    g.dma_start(out=xcb_s.ap()[st_i][:, gi * 512:(gi + 1) * 512],
                                in_=arb(xco, 512)).inc(ST, 16)
                    m1_xco_gate[gi % 2] = ST.n
                    bank_wait(t, bank)
                    for k in range(8):
                        t.matmul(PS[bank][:, :],
                                 WA[:, k * 4096 + 2048 + gi * 128:k * 4096 + 2048 + gi * 128 + 128],
                                 arb(4160, 4096)[:, k * 512:(k + 1) * 512],
                                 start=(k == 0), stop=(k == 7)).inc(PE, 1 if k == 7 else 0)
                    szo = 8768 + (gi % 2) * 256
                    s.wait(PE)
                    s.wait(ST, m1_szo_gate[gi % 2])
                    s.activation(arb(szo, 512), PS[bank][:, :], AF.Silu).inc(AC, 1)
                    m1_bankfree[bank] = ("AC", AC.n)
                    g.wait(AC)
                    g.dma_start(out=slz_s.ap()[st_i][:, gi * 512:(gi + 1) * 512],
                                in_=arb(szo, 512)).inc(ST, 16)
                    m1_szo_gate[gi % 2] = ST.n
            y.wait(ST, ST.n)

            if dbg == "m1":
                for st_i in range(nst):
                    g.dma_start(out=dbg_t.ap()[st_i * 128:(st_i + 1) * 128, :],
                                in_=xcb_s.ap()[st_i]).inc(ST, 16)
                    g.dma_start(out=dbg_t.ap()[(nst + st_i) * 128:(nst + st_i + 1) * 128, :],
                                in_=slz_s.ap()[st_i]).inc(ST, 16)
                y.wait(ST, ST.n)
                return

            pe_after_m1 = PE.n
            g.wait(PE, pe_after_m1)
            for ch in range(16):
                g.dma_start(out=WA[:, ch * E:(ch + 1) * E],
                            in_=WBT.ap()[ch * 128:(ch + 1) * 128, :]).inc(WG, 16)
            wl_wb = WG.n
            UC = [SMALL[:, 402 + i:403 + i] for i in range(16)]
            HC = [SMALL[:, 418 + i:419 + i] for i in range(16)]

            t.wait(WG, wl_wb)
            for gi in range(16):
                bank = gi % 2
                bank_wait(t, bank)
                for ke in range(16):
                    t.matmul(PS[bank][:, 0:1],
                             WA[:, ke * E + gi * 128:ke * E + gi * 128 + 128],
                             XCH[:, ke:ke + 1],
                             start=(ke == 0), stop=(ke == 15)).inc(PE, 1 if ke == 15 else 0)
                t.matmul(PSY[0:128, 0:128], ONES_ROW, ONES_ROW,
                         start=True, stop=True).inc(PE, 1)
                s.wait(PE)
                s.activation(arf(10816 + gi, 1), PS[bank][:, 0:1], AF.Exp,
                             scale=-1.0).inc(AC, 1)
                m1_bankfree[bank] = ("AC", AC.n)
            v.wait(AC)
            vfence(v.tensor_scalar(arf(10832, 16), arf(10816, 16), 1.0, None, ALU.add))
            vfence(v.reciprocal(arf(10848, 16), arf(10832, 16)))
            vfence(v.tensor_tensor(arf(10864, 16), arf(10848, 16), XCH[:, 0:16],
                                   ALU.mult))
            v.tensor_scalar(SMALL[:, 402:418], arf(10864, 16), PRM[:, 112:113],
                            None, ALU.mult)
            v.memset(SMALL[:, 418:434], 0.0).inc(DV, 1)

            if dbg == "wdt":
                g.wait(DV, DV.n)
                g.dma_start(out=dbg_t.ap()[0:128, :], in_=WB_[:, 0:E]).inc(ST, 16)
                g.dma_start(out=dbg_t.ap()[128:256, :], in_=WB_[:, 5 * E:6 * E]).inc(ST, 16)
                g.dma_start(out=dbg_t.ap()[256:384, :], in_=WB_[:, 15 * E:16 * E]).inc(ST, 16)
                g.dma_start(out=dbg_t.ap()[384:512, :], in_=WA[:, 0:E]).inc(ST, 16)
                g.dma_start(out=dbg_t.ap()[512:640, 0:128], in_=PRM[:, :]).inc(ST, 16)
                y.wait(ST, ST.n)
                return

            m2_afree = [0, 0]
            m2_bfree = [0, 0]
            m2_xcb_gate = 0
            m2_bside_dv = [0]
            for st_i in range(nst):
                y.wait(PE, m2_xcb_gate)
                y.wait(DV, m2_bside_dv[0])
                y.dma_start(out=arb(0, 8192), in_=xcb_s.ap()[st_i]).inc(LD, 16)
                for gi in range(16):
                    bank = gi % 2
                    base_a = 8200 + (gi % 2) * 512
                    base_b = 7176 + (gi % 2) * 512
                    t.wait(LD)
                    bank_wait(t, bank)
                    for ke in range(16):
                        t.matmul(PS[bank][:, :],
                                 WB_[:, ke * E + gi * 128:ke * E + gi * 128 + 128],
                                 arb(0, 8192)[:, ke * 512:(ke + 1) * 512],
                                 start=(ke == 0), stop=(ke == 15)).inc(PE, 1 if ke == 15 else 0)
                    t.matmul(PSY[0:128, 0:128], ONES_ROW, ONES_ROW,
                             start=True, stop=True).inc(PE, 1)
                    s.wait(PE)
                    s.activation(arf(4096, 512), PS[bank][:, :], AF.Identity,
                                 bias=PRM[:, 0 + gi:1 + gi]).inc(AC, 1)
                    m1_bankfree[bank] = ("AC", AC.n)
                    s.wait(DV, m2_bside_dv[0])
                    s.activation(arf(4608, 512), arf(4096, 512), AF.Abs)
                    s.activation(arf(5120, 512), arf(4608, 512), AF.Exp, scale=-1.0)
                    s.activation(arf(4608, 512), arf(5120, 512), AF.Ln, bias=1.0)
                    s.activation(arf(5120, 512), arf(4096, 512), AF.Relu).inc(AC, 1)
                    v.wait(AC)
                    v.tensor_tensor(arf(5632, 512), arf(4608, 512), arf(5120, 512),
                                    ALU.add)
                    v.tensor_scalar(arf(4096, 512), arf(5632, 512),
                                    PRM[:, 16 + gi:17 + gi], -1e-4, ALU.mult, ALU.min)
                    v.tensor_scalar(arf(4096, 512), arf(4096, 512), -10.0, None,
                                    ALU.max)
                    v.tensor_reduce(arf(9736 + gi * 4 + st_i, 1), arf(4096, 512),
                                    AX.X, ALU.add).inc(DV, 1)
                    s.wait(DV)
                    s.wait(ST, m2_afree[gi % 2])
                    s.activation(arf(base_a, 512), arf(4096, 512), AF.Exp).inc(AC, 1)
                    if dbg == "m2g" and st_i == 0 and gi == 3:
                        v.wait(AC, AC.n)
                        v.tensor_copy(arf(9224, 512), PS[bank][:, :]).inc(DV, 1)
                        g.wait(DV, DV.n)
                        g.wait(AC, AC.n)
                        g.dma_start(out=dbg_t.ap()[0:128, :],
                                    in_=arf(9224, 512)).inc(ST, 16)
                        g.dma_start(out=dbg_t.ap()[128:256, :],
                                    in_=arf(4096, 512)).inc(ST, 16)
                        for rr in range(16):
                            g.dma_start(out=dbg_t.ap()[(2 + rr) * 128:(3 + rr) * 128, :],
                                        in_=arb(0, 8192)[:, rr * 512:(rr + 1) * 512]).inc(ST, 16)
                        g.dma_start(out=dbg_t.ap()[18 * 128:19 * 128, 0:128],
                                    in_=WB_[:, gi * 128:gi * 128 + 128]).inc(ST, 16)
                        g.dma_start(out=dbg_t.ap()[19 * 128:20 * 128, 0:128],
                                    in_=WB_[:, 15 * E + gi * 128:15 * E + gi * 128 + 128]).inc(ST, 16)
                        t.wait(ST, ST.n)
                        v.wait(ST, ST.n)
                        s.wait(ST, ST.n)
                    bank_wait(t, bank)
                    for ke in range(16):
                        t.matmul(PS[bank][:, :],
                                 WA[:, ke * E + gi * 128:ke * E + gi * 128 + 128],
                                 arb(0, 8192)[:, ke * 512:(ke + 1) * 512],
                                 start=(ke == 0), stop=(ke == 15)).inc(PE, 1 if ke == 15 else 0)
                    t.matmul(PSY[0:128, 0:128], ONES_ROW, ONES_ROW,
                             start=True, stop=True).inc(PE, 1)
                    s.wait(PE)
                    s.activation(arf(4608, 512), PS[bank][:, :], AF.Exp,
                                 scale=-1.0).inc(AC, 1)
                    m1_bankfree[bank] = ("AC", AC.n)
                    v.wait(AC)
                    v.tensor_scalar(arf(4608, 512), arf(4608, 512), 1.0, None, ALU.add)
                    v.reciprocal(arf(5120, 512), arf(4608, 512))
                    v.tensor_copy(arf(6144, 1), UC[gi])
                    vfence(v.tensor_tensor(arf(6145, 512),
                                           arb(0, 8192)[:, gi * 512:(gi + 1) * 512],
                                           arf(5120, 512), ALU.mult))
                    v.tensor_copy(UC[gi], arf(6656, 1))
                    v.tensor_tensor(arf(6664, 512), arf(6144, 512), arf(6145, 512),
                                    ALU.add)
                    s.wait(ST, m2_bfree[gi % 2]) if False else None
                    v.wait(ST, m2_bfree[gi % 2])
                    v.scalar_tensor_tensor(arf(base_b, 512), arf(6664, 512), 0.5,
                                           arf(5632, 512), ALU.mult, ALU.mult).inc(DV, 1)
                    v.wait(AC)
                    vfence(v.tensor_tensor_scan(arf(9224, 512), arf(base_a, 512),
                                                arf(base_b, 512), HC[gi],
                                                ALU.mult, ALU.add))
                    v.tensor_copy(HC[gi], arf(9735, 1)).inc(DV, 1)
                    m2_bside_dv[0] = DV.n
                    g.wait(AC)
                    g.wait(DV)
                    g.dma_start(out=alp_s.ap()[st_i, gi], in_=arf(base_a, 512)).inc(ST, 16)
                    g.dma_start(out=bet_s.ap()[st_i, gi], in_=arf(base_b, 512)).inc(ST, 16)
                    m2_afree[gi % 2] = ST.n
                    m2_bfree[gi % 2] = ST.n
                m2_xcb_gate = PE.n
            y.wait(ST, ST.n)

            last = None
            for gi in range(16):
                last = v.tensor_reduce(arf(9800 + gi, 1), arf(9736 + gi * 4, nst),
                                       AX.X, ALU.add)
            vfence(last)
            s.wait(DV)
            s.activation(arf(9816, 16), arf(9800, 16), AF.Exp).inc(AC, 1)
            last = None
            for gi in range(16):
                last = v.tensor_copy(arf(9832 + gi, 1), HC[gi])
            vfence(last)
            t.wait(DV)
            t.wait(AC)
            t.transpose(PSX[0:16, 0:128], arf(9832, 16), IDF[:, :]).inc(PE, 1)
            v.wait(PE)
            v.tensor_copy(AR[0:16, 9856:9984], PSX[0:16, 0:128]).inc(DV, 1)
            t.transpose(PSY[0:16, 0:128], arf(9816, 16), IDF[:, :]).inc(PE, 1)
            v.wait(PE)
            v.tensor_copy(AR[0:16, 9984:10112], PSY[0:16, 0:128]).inc(DV, 1)
            g.wait(DV)
            g.dma_start(out=cc_in.ap()[0:1, :].rearrange("a (r c) -> (a r) c", c=128),
                        in_=AR[0:16, 9856:9984]).inc(ST, 16)
            g.dma_start(out=cc_in.ap()[1:2, :].rearrange("a (r c) -> (a r) c", c=128),
                        in_=AR[0:16, 9984:10112]).inc(ST, 16)
            g.wait(ST, ST.n)
            g.collective_compute(
                "AllGather", ALU.bypass,
                replica_groups=[list(range(n_cores))],
                ins=[cc_in.ap().opt()],
                outs=[cc_out.ap().opt()],
            ).inc(CCh, 1)
            g.wait(CCh, 1)
            NCr = n_cores
            cco = cc_out.ap().rearrange("(r two) e -> two r e", two=2)
            g.dma_start(out=AR[0:NCr, 0:E], in_=cco[0]).inc(ST, 16)
            g.dma_start(out=AR[0:NCr, 2048:2048 + E], in_=cco[1]).inc(ST, 16)
            y.dma_start(out=AR[0:NCr, 4096:4096 + NCr], in_=mm8.ap()).inc(LD, 16)
            y.dma_start(out=AR[0:NCr, 4200:4201], in_=prefm.ap()).inc(LD, 16)
            s.wait(ST, ST.n)
            s.wait(LD, LD.n)
            s.activation(AR[0:NCr, 6144:6144 + E], AR[0:NCr, 2048:2048 + E],
                         AF.Ln).inc(AC, 1)
            v.wait(AC)
            v.tensor_scalar(AR[0:NCr, 6144:6144 + E], AR[0:NCr, 6144:6144 + E],
                            -1e30, None, ALU.max).inc(DV, 1)
            t.wait(DV)
            for j4 in range(4):
                t.matmul(PSX[0:NCr, 0:512], AR[0:NCr, 4096:4096 + NCr],
                         AR[0:NCr, 6144 + j4 * 512:6144 + (j4 + 1) * 512],
                         start=True, stop=True).inc(PE, 1)
                s.wait(PE)
                s.activation(AR[0:NCr, 8192 + j4 * 512:8192 + (j4 + 1) * 512],
                             PSX[0:NCr, 0:512], AF.Exp).inc(AC, 1)
                t.wait(AC)
            v.wait(AC)
            v.tensor_tensor(AR[0:NCr, 8192:8192 + E], AR[0:NCr, 8192:8192 + E],
                            AR[0:NCr, 0:E], ALU.mult).inc(DV, 1)
            t.wait(DV)
            for j4 in range(4):
                t.matmul(PSX[0:1, 0:512], AR[0:NCr, 4200:4201],
                         AR[0:NCr, 8192 + j4 * 512:8192 + (j4 + 1) * 512],
                         start=True, stop=True).inc(PE, 1)
                v.wait(PE)
                v.tensor_copy(AR[0:1, 6144 + j4 * 512:6144 + (j4 + 1) * 512],
                              PSX[0:1, 0:512]).inc(DV, 1)
                t.wait(DV)
            for gi in range(16):
                bank = gi % 2
                pso = PSX if bank == 0 else PSY
                t.transpose(pso[:, 0:1],
                            AR[0:1, 6144 + gi * 128:6144 + (gi + 1) * 128],
                            IDF[0:1, 0:1]).inc(PE, 1)
                v.wait(PE)
                v.tensor_copy(SMALL[:, 448 + gi:449 + gi], pso[:, 0:1]).inc(DV, 1)
                t.wait(DV) if gi >= 1 else None
            HSC = [SMALL[:, 448 + i:449 + i] for i in range(16)]
            dv_after_combine = DV.n

            if dbg == "m2":
                for st_i in range(nst):
                    for gi in range(16):
                        g.dma_start(out=dbg_t.ap()[(st_i * 16 + gi) * 128:(st_i * 16 + gi + 1) * 128, :],
                                    in_=alp_s.ap()[st_i, gi]).inc(ST, 16)
                        g.dma_start(out=dbg_t.ap()[(nst * 16 + st_i * 16 + gi) * 128:(nst * 16 + st_i * 16 + gi + 1) * 128, :],
                                    in_=bet_s.ap()[st_i, gi]).inc(ST, 16)
                base_r = 2 * nst * 16 * 128
                g.wait(DV, DV.n)
                g.dma_start(out=dbg_t.ap()[base_r:base_r + 128, 0:16], in_=arf(9832, 16)).inc(ST, 16)
                g.dma_start(out=dbg_t.ap()[base_r + 128:base_r + 256, 0:16], in_=arf(9816, 16)).inc(ST, 16)
                g.dma_start(out=dbg_t.ap()[base_r + 256:base_r + 384, 0:16], in_=SMALL[:, 448:464]).inc(ST, 16)
                y.wait(ST, ST.n)
                return

            pe_after_m2 = PE.n
            g.wait(PE, pe_after_m2)
            for ch in range(16):
                g.dma_start(out=WB_[:, ch * E:(ch + 1) * E],
                            in_=WCT.ap()[ch * 128:(ch + 1) * 128, :]).inc(WG, 16)
            wl_wc = WG.n
            for ch in range(16):
                g.dma_start(out=WA[:, ch * 1024:(ch + 1) * 1024],
                            in_=WoT.ap()[ch * 128:(ch + 1) * 128, :]).inc(WG, 16)
            wl_wo = WG.n

            HC2 = HC
            m3_xcb_gate = None
            m3_ain_gate = [dv_after_combine, dv_after_combine]
            m3_slz_gate = [dv_after_combine, dv_after_combine]
            m3_st_gate = ST.n
            t.wait(WG, wl_wc)
            for st_i in range(nst):
                if m3_xcb_gate is None:
                    y.wait(DV, dv_after_combine)
                else:
                    y.wait(PE, m3_xcb_gate)
                y.dma_start(out=arb(0, 8192), in_=xcb_s.ap()[st_i]).inc(LD, 16)
                for gi in range(16):
                    bank = gi % 2
                    abuf = 8192 + (gi % 2) * 512
                    bbuf = 9216 + (gi % 2) * 512
                    zbuf = 10240 + (gi % 2) * 256
                    y.wait(DV, m3_ain_gate[gi % 2])
                    y.wait(ST, m3_st_gate)
                    y.dma_start(out=arf(abuf, 512), in_=alp_s.ap()[st_i, gi]).inc(LD, 16)
                    y.dma_start(out=arf(bbuf, 512), in_=bet_s.ap()[st_i, gi]).inc(LD, 16)
                    y.wait(DV, m3_slz_gate[gi % 2])
                    y.dma_start(out=arb(zbuf, 512),
                                in_=slz_s.ap()[st_i][:, gi * 512:(gi + 1) * 512]).inc(LD, 16)
                    t.wait(LD)
                    bank_wait(t, bank)
                    for ke in range(16):
                        t.matmul(PS[bank][:, :],
                                 WB_[:, ke * E + gi * 128:ke * E + gi * 128 + 128],
                                 arb(0, 8192)[:, ke * 512:(ke + 1) * 512],
                                 start=(ke == 0), stop=(ke == 15)).inc(PE, 1 if ke == 15 else 0)
                    s.wait(PE)
                    s.activation(arf(11264, 512), PS[bank][:, :], AF.Tanh).inc(AC, 1)
                    m1_bankfree[bank] = ("AC", AC.n)
                    init = HSC[gi] if st_i == 0 else HC2[gi]
                    v.wait(LD)
                    vfence(v.tensor_tensor_scan(arf(10752, 512), arf(abuf, 512),
                                                arf(bbuf, 512), init,
                                                ALU.mult, ALU.add))
                    v.tensor_copy(HC2[gi], arf(11263, 1))
                    v.wait(AC)
                    v.tensor_tensor(arf(bbuf, 512), arf(10752, 512), arf(11264, 512),
                                    ALU.mult)
                    v.tensor_tensor(arb(4096, 8192)[:, gi * 512:(gi + 1) * 512],
                                    arf(bbuf, 512), arb(zbuf, 512), ALU.mult).inc(DV, 1)
                    m3_ain_gate[gi % 2] = DV.n
                    m3_slz_gate[gi % 2] = DV.n
                t.wait(WG, wl_wo)
                t.wait(DV)
                for j in range(4):
                    bank = 2 + (j % 2)
                    y.wait(DV, DV.n)
                    y.dma_start(out=arf(8192, D),
                                in_=slab2(y1_s, 1 + st_i * 4 + j)).inc(LD, 16)
                    for n in range(2):
                        bank_wait(t, bank)
                        for ke in range(16):
                            t.matmul(PS[bank][:, :],
                                     arb(4096, 8192)[:, ke * 512 + j * 128:ke * 512 + j * 128 + 128],
                                     WA[:, ke * 1024 + n * 512:ke * 1024 + (n + 1) * 512],
                                     start=(ke == 0), stop=(ke == 15)).inc(PE, 1 if ke == 15 else 0)
                        v.wait(PE)
                        v.wait(LD)
                        v.tensor_tensor(arf(9216 + n * 512, 512), PS[bank][:, :],
                                        arf(8192 + n * 512, 512), ALU.add).inc(DV, 1)
                        m1_bankfree[bank] = ("DV", DV.n)
                    g.wait(DV)
                    g.dma_start(out=slab2(x2_s, st_i * 4 + j), in_=arf(9216, D)).inc(ST, 16)
                m3_xcb_gate = PE.n
                m3_ain_gate = [DV.n, DV.n]
                m3_st_gate = ST.n
            y.wait(ST, ST.n)

            if dbg == "x2":
                for c in range(nch):
                    g.dma_start(out=dbg_t.ap()[c * 128:(c + 1) * 128, :],
                                in_=slab2(x2_s, c)).inc(ST, 16)
                y.wait(ST, ST.n)
                return

            pe_after_m3 = PE.n
            first = True
            for ch in range(8):
                if first:
                    y.wait(PE, pe_after_m3)
                    y.wait(WL, wl_imgs)
                    first = False
                y.dma_start(out=WA[:, ch * H:(ch + 1) * H], in_=w1q_s.ap()[ch]).inc(WL, 16)
            wl_f2a = WL.n
            for ch in range(32):
                y.dma_start(out=WB_[:, ch * D:(ch + 1) * D], in_=w2q_s.ap()[ch]).inc(WL, 16)
            wl_f2b = WL.n
            ffn_A("x2", q2_s, nch, 318, (WL, wl_f2a))
            ffn_B("x2", q2_s, nch, 318, "out", (WL, wl_f2b))
            y.wait(ST, ST.n)

        def run_walk(eng):
            def f(handle):
                P.cur = eng
                P.handles[eng] = handle
                for cc_ in chans.values():
                    cc_.n = 0
                emit(P)
            return f

        for eng in ENGINES:
            getattr(block, eng)(run_walk(eng))

    return nc


def _prep_inputs(inputs, n_cores, tpc):
    f32 = np.float32
    x = np.ascontiguousarray(np.asarray(inputs["L0_attn_x"]).reshape(-1, D).astype(f32))
    common = {
        "WiT": np.ascontiguousarray(np.asarray(inputs["L0_attn_Wi"]).T.astype(f32)),
        "WdtT": np.ascontiguousarray(np.asarray(inputs["L0_attn_Wdt"]).T.astype(f32)),
        "WBT": np.ascontiguousarray(np.asarray(inputs["L0_attn_WB"]).T.astype(f32)),
        "WCT": np.ascontiguousarray(np.asarray(inputs["L0_attn_WC"]).T.astype(f32)),
        "WoT": np.ascontiguousarray(np.asarray(inputs["L0_attn_Wo"]).T.astype(f32)),
        "w1T": np.ascontiguousarray(np.asarray(inputs["L0_ffn_w1"]).T.astype(f32)),
        "w2T": np.ascontiguousarray(np.asarray(inputs["L0_ffn_w2"]).T.astype(f32)),
        "ln1w_r": np.asarray(inputs["L0_ffn_ln1_w"]).reshape(1, D).astype(f32),
        "ln1b_r": np.asarray(inputs["L0_ffn_ln1_b"]).reshape(1, D).astype(f32),
        "b1_r": np.asarray(inputs["L0_ffn_b1"]).reshape(1, H).astype(f32),
        "ln2w_r": np.asarray(inputs["L0_ffn_ln2_w"]).reshape(1, H).astype(f32),
        "ln2b_r": np.asarray(inputs["L0_ffn_ln2_b"]).reshape(1, H).astype(f32),
        "b2_r": np.asarray(inputs["L0_ffn_b2"]).reshape(1, D).astype(f32),
        "bdt_c": np.ascontiguousarray(
            np.asarray(inputs["L0_attn_bdt"]).reshape(16, 128).T.astype(f32)),
        "alog_c": np.ascontiguousarray(
            np.asarray(inputs["L0_attn_A_log"]).reshape(16, 128).T.astype(f32)),
        "bconv_c": np.ascontiguousarray(
            np.asarray(inputs["L0_attn_bconv"]).reshape(16, 128).T.astype(f32)),
        "wconv_c": np.ascontiguousarray(
            np.asarray(inputs["L0_attn_Wconv"]).reshape(16, 128, 4)
            .transpose(1, 0, 2).reshape(128, 64).astype(f32)),
        "ones_r": np.ones((1, 128), f32),
        "ones_c": np.ones((128, 1), f32),
        "idt_b": np.eye(128, dtype=f32).astype(ml_dtypes.bfloat16),
        "idf_f": np.eye(128, dtype=f32),
    }
    ims = []
    for i in range(n_cores):
        st = i * tpc
        im = dict(common)
        im["x_own"] = x[st:st + tpc]
        im["x_halo"] = x[st - 128:st] if i > 0 else np.ascontiguousarray(x[0:128])
        im["halo_m"] = np.full((128, 1), 0.0 if i == 0 else 1.0, f32)
        pm = np.zeros((n_cores, 1), f32)
        pm[:i] = 1.0
        im["prefm"] = pm
        mm = np.zeros((n_cores, n_cores), f32)
        for j in range(i):
            for k in range(j + 1, i):
                mm[k, j] = 1.0
        im["mm8"] = mm
        ims.append(im)
    return ims


def kernel(**inputs):
    n_cores = 8
    tpc = L_FULL // n_cores
    nc = build_nc(n_cores, tpc)
    ims = _prep_inputs(inputs, n_cores, tpc)
    res = run_bass_kernel_spmd(nc, ims, core_ids=list(range(n_cores)))
    out = np.concatenate([res.results[i]["out"] for i in range(n_cores)], axis=0)
    return out.reshape(1, L_FULL, D)


# version 07 (final)
# speedup vs baseline: 1.0889x; 1.0053x over previous
import contextlib
import numpy as np
import ml_dtypes
import concourse.bass as bass
import concourse.mybir as mybir
from concourse.bass_utils import run_bass_kernel_spmd

F32 = mybir.dt.float32
BF16 = mybir.dt.bfloat16
I32 = mybir.dt.int32
AF = mybir.ActivationFunctionType
ALU = mybir.AluOpType
AX = mybir.AxisListType

L_FULL, D, E, H = 16384, 1024, 2048, 4096
EPS = 1e-5
ENGINES = ("tensor", "vector", "scalar", "gpsimd", "sync")


class Chan:
    __slots__ = ("name", "sem", "n")

    def __init__(self, name):
        self.name = name
        self.sem = None
        self.n = 0


class Inst:
    __slots__ = ("h",)

    def __init__(self, h):
        self.h = h

    def inc(self, chan, k=1):
        if k <= 0:
            return self
        if self.h is not None:
            self.h.then_inc(chan.sem, k)
        chan.n += k
        return self


class Eng:
    def __init__(self, prog, name):
        object.__setattr__(self, "_p", prog)
        object.__setattr__(self, "_n", name)

    def wait(self, chan, val=None):
        v = chan.n if val is None else val
        if self._p.cur == self._n:
            self._p.handles[self._n].wait_ge(chan.sem, v)

    def __getattr__(self, op):
        p = self._p
        nm = self._n

        def call(*a, **kw):
            if p.cur == nm:
                return Inst(getattr(p.handles[nm], op)(*a, **kw))
            return Inst(None)

        return call


class Prog:
    def __init__(self):
        self.cur = None
        self.handles = {}
        self.t = Eng(self, "tensor")
        self.v = Eng(self, "vector")
        self.s = Eng(self, "scalar")
        self.g = Eng(self, "gpsimd")
        self.y = Eng(self, "sync")


def build_nc(n_cores, tpc, dbg=None):
    nch = tpc // 128
    nst = tpc // 512
    nc = bass.Bass("TRN2", target_bir_lowering=False,
                   detect_race_conditions=False)

    def din(name, shape, dt=F32):
        return nc.dram_tensor(name, list(shape), dt, kind="ExternalInput")

    x_own = din("x_own", [tpc, D])
    x_halo = din("x_halo", [128, D])
    WiT = din("WiT", [D, 2 * E])
    WdtT = din("WdtT", [E, E])
    WBT = din("WBT", [E, E])
    WCT = din("WCT", [E, E])
    WoT = din("WoT", [E, D])
    w1T = din("w1T", [D, H])
    w2T = din("w2T", [H, D])
    ln1w_r = din("ln1w_r", [1, D])
    ln1b_r = din("ln1b_r", [1, D])
    b1_r = din("b1_r", [1, H])
    ln2w_r = din("ln2w_r", [1, H])
    ln2b_r = din("ln2b_r", [1, H])
    b2_r = din("b2_r", [1, D])
    bdt_c = din("bdt_c", [128, 16])
    alog_c = din("alog_c", [128, 16])
    bconv_c = din("bconv_c", [128, 16])
    wconv_c = din("wconv_c", [128, 64])
    ones_r = din("ones_r", [1, 128])
    ones_c = din("ones_c", [128, 1])
    idt_b = din("idt_b", [128, 128], BF16)
    idf_f = din("idf_f", [128, 128])
    halo_m = din("halo_m", [128, 1])
    prefm = din("prefm", [n_cores, 1])
    mm8 = din("mm8", [n_cores, n_cores])

    out_t = nc.dram_tensor("out", [tpc, D], F32, kind="ExternalOutput")

    y1_s = nc.dram_tensor("y1_s", [nch + 1, 128, D], F32)
    q1_s = nc.dram_tensor("q1_s", [nch + 1, 128, H], BF16)
    q2_s = nc.dram_tensor("q2_s", [nch, 128, H], BF16)
    w1q_s = nc.dram_tensor("w1q_s", [8, 128, H], BF16)
    w2q_s = nc.dram_tensor("w2q_s", [32, 128, D], BF16)
    xcb_s = nc.dram_tensor("xcb_s", [nst, 128, 16 * 512], BF16)
    slz_s = nc.dram_tensor("slz_s", [nst, 128, 16 * 512], BF16)
    alp_s = nc.dram_tensor("alp_s", [nst, 16, 128, 512], F32)
    bet_s = nc.dram_tensor("bet_s", [nst, 16, 128, 512], F32)
    x2_s = nc.dram_tensor("x2_s", [nch, 128, D], F32)
    cc_in = nc.dram_tensor("cc_in", [2, E], F32)
    cc_out = nc.dram_tensor("cc_out", [2 * n_cores, E], F32)

    dbg_t = None
    if dbg == "y1":
        dbg_t = nc.dram_tensor("dbg", [(nch + 1) * 128, D], F32, kind="ExternalOutput")
    elif dbg == "m1":
        dbg_t = nc.dram_tensor("dbg", [2 * nst * 128, 16 * 512], F32, kind="ExternalOutput")
    elif dbg == "m2":
        dbg_t = nc.dram_tensor("dbg", [2 * nst * 16 * 128 + 3 * 128, 512], F32, kind="ExternalOutput")
    elif dbg == "x2":
        dbg_t = nc.dram_tensor("dbg", [nch * 128, D], F32, kind="ExternalOutput")
    elif dbg == "q1":
        dbg_t = nc.dram_tensor("dbg", [43 * 128, H], F32, kind="ExternalOutput")
    elif dbg == "f1a":
        dbg_t = nc.dram_tensor("dbg", [8 * 128, H], F32, kind="ExternalOutput")
    elif dbg == "wdt":
        dbg_t = nc.dram_tensor("dbg", [6 * 128, 2048], F32, kind="ExternalOutput")
    elif dbg == "m2g":
        dbg_t = nc.dram_tensor("dbg", [24 * 128, 512], F32, kind="ExternalOutput")

    chans = {n: Chan(n) for n in ("LD", "ST", "PE", "DV", "AC", "CC", "WL", "WG",
                                  "QA", "QB", "QC", "QD")}

    es = contextlib.ExitStack()
    with es:
        def sbt(name, shape, dt=F32):
            return es.enter_context(nc.sbuf_tensor(name, list(shape), dt))

        WA = sbt("WA", [128, 32768], BF16)
        WB_ = sbt("WB_", [128, 32768], BF16)
        AR = sbt("AR", [128, 11776], F32)
        BCln2w = sbt("BCln2w", [128, H], BF16)
        BCln2b = sbt("BCln2b", [128, H], BF16)
        BCb1 = sbt("BCb1", [128, H], BF16)
        BCb2h = sbt("BCb2h", [128, D], BF16)
        BCln1w = sbt("BCln1w", [128, D], BF16)
        BCln1b = sbt("BCln1b", [128, D], BF16)
        SMALL = sbt("SMALL", [128, 480], F32)
        IDT = sbt("IDT", [128, 128], BF16)
        IDF = sbt("IDF", [128, 128], F32)
        PRM = sbt("PRM", [128, 128], F32)
        PS = [es.enter_context(nc.psum_tensor(f"ps{i}", [128, 512], F32))
              for i in range(4)]
        PSX = es.enter_context(nc.psum_tensor("psx", [128, 512], F32))
        PSY = es.enter_context(nc.psum_tensor("psy", [128, 512], F32))
        PT = [es.enter_context(nc.psum_tensor(f"pt{i}", [128, 1024], BF16))
              for i in range(2)]

        for cn in chans:
            chans[cn].sem = es.enter_context(nc.semaphore(cn))

        block = es.enter_context(nc.Block())
        P = Prog()

        def arf(o, w):
            return AR[:, o:o + w]

        def arb(o, w):
            return AR[:, o:o + (w + 1) // 2].bitcast(BF16)[:, :w]


        def ari(o, w):
            return AR[:, o:o + w].bitcast(I32)[:, :w]

        def smc(o, w=1):
            return SMALL[:, o:o + w]

        C = chans

        ONES_ROW = SMALL[0:1, 0:128]

        def emit(P):
            LD, ST, PE, DV, AC, CCh, WL, WG = (C["LD"], C["ST"], C["PE"], C["DV"],
                                               C["AC"], C["CC"], C["WL"], C["WG"])
            t, v, s, g, y = P.t, P.v, P.s, P.g, P.y

            def vfence(inst):
                inst.inc(DV, 1)
                v.wait(DV)

            def ln_rstd(src, width, cb):
                nj = width // 512
                last = None
                for j in range(nj):
                    last = v.bn_stats(smc(cb + j * 6, 6), src(j))
                vfence(last)
                vfence(v.bn_aggr(smc(cb + 48, 2), smc(cb, 6 * nj)))
                vfence(v.tensor_scalar(smc(cb + 50), smc(cb + 49), EPS, None, ALU.add))
                vfence(v.tensor_scalar(smc(cb + 55), smc(cb + 50), -0.5, None, ALU.mult))
                iv = smc(cb + 50).bitcast(I32)
                iy = smc(cb + 51).bitcast(I32)
                vfence(v.tensor_scalar(iy, iv, 1, None, ALU.arith_shift_right))
                vfence(v.tensor_scalar(iy, iy, -1, None, ALU.bitwise_xor))
                vfence(v.tensor_scalar(iy, iy, 0x5F3759E0, None, ALU.add))
                for _ in range(3):
                    s.wait(DV)
                    s.activation(smc(cb + 53), smc(cb + 51), AF.Square).inc(AC, 1)
                    v.wait(AC)
                    vfence(v.tensor_scalar(smc(cb + 54), smc(cb + 53), smc(cb + 55),
                                           1.5, ALU.mult, ALU.add))
                    vfence(v.tensor_scalar(smc(cb + 51), smc(cb + 54), smc(cb + 51),
                                           None, ALU.mult))
                vfence(v.tensor_scalar(smc(cb + 52), smc(cb + 51), smc(cb + 48),
                                       -1.0, ALU.mult, ALU.mult))
                return smc(cb + 51), smc(cb + 52)

            def slab2(ten, i):
                return ten.ap()[i]

            y.dma_start(out=IDT[:, :], in_=idt_b.ap()).inc(LD, 16)
            y.dma_start(out=IDF[:, :], in_=idf_f.ap()).inc(LD, 16)
            y.dma_start(out=PRM[:, 0:16], in_=bdt_c.ap()).inc(LD, 16)
            y.dma_start(out=PRM[:, 16:32], in_=alog_c.ap()).inc(LD, 16)
            y.dma_start(out=PRM[:, 32:48], in_=bconv_c.ap()).inc(LD, 16)
            y.dma_start(out=PRM[:, 48:112], in_=wconv_c.ap()).inc(LD, 16)
            y.dma_start(out=PRM[:, 112:113], in_=halo_m.ap()).inc(LD, 16)
            y.dma_start(out=PRM[:, 113:114], in_=ones_c.ap()).inc(LD, 16)
            y.dma_start(out=SMALL[0:1, 0:128], in_=ones_r.ap()).inc(LD, 16)
            ld_p0 = LD.n

            o_ln1w, o_ln1b, o_ln2w, o_ln2b, o_b2, o_b1 = 0, 1024, 2048, 6144, 10240, 11264 - 4096
            y.dma_start(out=AR[0:1, o_ln1w:o_ln1w + D], in_=ln1w_r.ap()).inc(LD, 16)
            y.dma_start(out=AR[0:1, o_ln1b:o_ln1b + D], in_=ln1b_r.ap()).inc(LD, 16)
            y.dma_start(out=AR[0:1, o_b2:o_b2 + D], in_=b2_r.ap()).inc(LD, 16)
            y.dma_start(out=AR[0:1, o_ln2w:o_ln2w + H], in_=b1_r.ap()).inc(LD, 16)
            ld_rows1 = LD.n

            def bcast_row(dst_bf, src_off, w, scale, ldv):
                for j in range(w // 512):
                    t.wait(LD, ldv)
                    t.matmul(PSX[:, :], ONES_ROW,
                             AR[0:1, src_off + j * 512:src_off + (j + 1) * 512],
                             start=True, stop=True).inc(PE, 1)
                    v.wait(PE)
                    if scale != 1.0:
                        v.tensor_scalar(dst_bf[:, j * 512:(j + 1) * 512],
                                        PSX[:, :], scale, None, ALU.mult).inc(DV, 1)
                    else:
                        v.tensor_copy(dst_bf[:, j * 512:(j + 1) * 512],
                                      PSX[:, :]).inc(DV, 1)
                    t.wait(DV)

            bcast_row(BCln1w, o_ln1w, D, 1.0, ld_rows1)
            bcast_row(BCln1b, o_ln1b, D, 1.0, ld_rows1)
            bcast_row(BCb2h, o_b2, D, 0.5, ld_rows1)
            bcast_row(BCb1, o_ln2w, H, 1.0, ld_rows1)
            y.wait(PE, C["PE"].n)
            y.dma_start(out=AR[0:1, o_ln2w:o_ln2w + H], in_=ln2w_r.ap()).inc(LD, 16)
            y.dma_start(out=AR[0:1, o_ln2b:o_ln2b + H], in_=ln2b_r.ap()).inc(LD, 16)
            ld_rows2 = LD.n
            bcast_row(BCln2w, o_ln2w, H, 1.0, ld_rows2)
            bcast_row(BCln2b, o_ln2b, H, 1.0, ld_rows2)
            pe_bcast_done = PE.n
            dv_bcast_done = DV.n

            s.wait(LD, ld_p0)
            s.activation(PRM[:, 16:32], PRM[:, 16:32], AF.Exp).inc(AC, 1)
            v.wait(AC)
            v.tensor_scalar(PRM[:, 16:32], PRM[:, 16:32], -1.0, None, ALU.mult).inc(DV, 1)

            def wbf(i):
                return WB_[:, i * 8192:(i + 1) * 8192].bitcast(F32)

            def quant_weight(wT, nchunks, wcols, img, pair_off, dst, gate0):
                if wcols == 4096:
                    bufs = [lambda: arf(0, 4096), lambda: wbf(0),
                            lambda: arf(4096, 4096), lambda: wbf(1)]
                else:
                    bufs = [lambda: arf(0, 1024), lambda: arf(1024, 1024),
                            lambda: arf(2048, 1024), lambda: arf(3072, 1024)]
                qeng = [y, g, y, g]
                qch = [C["QA"], C["QB"], C["QC"], C["QD"]]

                def issue(ch, gates):
                    b = ch % 4
                    e = qeng[b]
                    for gc, gv in gates:
                        e.wait(gc, gv)
                    e.dma_start(out=bufs[b](),
                                in_=wT.ap()[ch * 128:(ch + 1) * 128, :]).inc(qch[b], 16)
                    return qch[b].n

                ldv = {}
                dvred = []
                for ch in range(min(4, nchunks)):
                    ldv[ch] = issue(ch, gate0)
                for ch in range(nchunks):
                    v.wait(qch[ch % 4], ldv[ch])
                    v.tensor_reduce(smc(140 + ch), bufs[ch % 4](), AX.X, ALU.add,
                                    apply_absolute_value=True).inc(DV, 1)
                    dvred.append(DV.n)
                    if ch + 4 < nchunks:
                        ldv[ch + 4] = issue(ch + 4, [(DV, dvred[ch])])
                vfence(v.tensor_copy(arf(8192, 512), arf(0, 512)))
                v.tensor_reduce(smc(139), SMALL[:, 140:140 + nchunks], AX.X, ALU.add).inc(DV, 1)
                t.wait(DV)
                t.matmul(PSX[0:1, 0:1], smc(139), PRM[:, 113:114],
                         start=True, stop=True).inc(PE, 1)
                v.wait(PE)
                sc_c = SMALL[0:1, 138:139]
                sc_s = SMALL[0:1, 137:138]
                vfence(v.tensor_scalar(sc_c, PSX[0:1, 0:1],
                                       1.0 / (nchunks * 128 * wcols), None, ALU.mult))
                vfence(v.tensor_scalar(sc_c, sc_c, 1e-5, None, ALU.max))
                v.reciprocal(sc_s, sc_c).inc(DV, 1)
                t.wait(DV)
                t.matmul(PSX[:, 0:2], ONES_ROW, SMALL[0:1, 137:139],
                         start=True, stop=True).inc(PE, 1)
                v.wait(PE)
                v.tensor_copy(smc(pair_off, 2), PSX[:, 0:2]).inc(DV, 1)
                v.wait(DV)
                t.wait(DV)
                pe_comb = PE.n
                scale_bc = smc(pair_off)
                nh = max(1, wcols // 2048)
                hw = wcols // nh
                dvq = []
                for ch in range(min(4, nchunks)):
                    ldv[ch] = issue(ch, [(DV, dvred[max(ch, nchunks - 4 + ch)])])
                for ch in range(nchunks):
                    v.wait(qch[ch % 4], ldv[ch])
                    for hhh in range(nh):
                        srcw = bufs[ch % 4]()[:, hhh * hw:(hhh + 1) * hw]
                        v.tensor_scalar(ari(8192, hw), srcw, scale_bc, None, ALU.mult)
                        v.tensor_copy(srcw, ari(8192, hw))
                        dslice = dst[:, ch * wcols + hhh * hw:ch * wcols + (hhh + 1) * hw]
                        v.tensor_scalar(dslice, srcw,
                                        -1.0, 1.0, ALU.max, ALU.min).inc(DV, 1)
                        s.wait(DV)
                        s.dma_start(out=img.ap()[ch][:, hhh * hw:(hhh + 1) * hw],
                                    in_=dslice).inc(WL, 16)
                    dvq.append(DV.n)
                    if ch + 4 < nchunks:
                        ldv[ch + 4] = issue(ch + 4, [(DV, dvq[ch])])
                return DV.n, pe_comb

            dv_w1q, pe_comb1 = quant_weight(w1T, 8, H, w1q_s, 172, WA,
                                            [(PE, pe_bcast_done), (DV, dv_bcast_done)])
            if dbg == "q1":
                g.wait(DV, DV.n)
                g.dma_start(out=dbg_t.ap()[42 * 128:43 * 128, 0:64],
                            in_=SMALL[:, 128:192]).inc(ST, 16)
                v.wait(ST, ST.n)
            v.wait(PE, pe_comb1)
            dv_w2q, _ = quant_weight(w2T, 32, D, w2q_s, 174, WB_, [(DV, dv_w1q)])
            c1_col = smc(173)
            c2_col = smc(175)
            wl_imgs = WL.n

            def ffn_A(src_kind, q_img, nchunks, sout2_base, w_gate):
                gch, gval = w_gate
                xfree = [None, None]
                ptfree = [0, 0]
                bankfree = [0, 0]
                hevfree = [0, 0]
                q2free = [0, 0]
                mm1_ac = [0] * nchunks
                t1_pe = [0]

                def GQp(p):
                    return arb(5632, H) if p == 0 else arb(7680, H)

                def front(c):
                    xbuf = (c % 2) * 1024
                    if xfree[c % 2] is not None:
                        y.wait(AC, xfree[c % 2])
                    if src_kind == "x":
                        src = x_halo.ap() if c == 0 else x_own.ap()[(c - 1) * 128:c * 128, :]
                    else:
                        src = slab2(x2_s, c)
                    y.dma_start(out=arf(xbuf, D), in_=src).inc(LD, 16)
                    v.wait(LD)
                    rstd1, nmb1 = ln_rstd(lambda j: arf(xbuf + j * 512, 512), D, 200)
                    s.wait(DV)
                    s.activation(arb(2048, D), arf(xbuf, D), AF.Identity,
                                 bias=nmb1, scale=rstd1).inc(AC, 1)
                    xfree[c % 2] = AC.n
                    v.wait(AC)
                    v.tensor_tensor(arb(2560, D), arb(2048, D), BCln1w[:, :], ALU.mult)
                    v.tensor_tensor(arb(3072, D), arb(2560, D), BCln1b[:, :], ALU.add)
                    vfence(v.tensor_reduce(smc(229), arb(3072, D), AX.X, ALU.max,
                                           apply_absolute_value=True))
                    vfence(v.tensor_scalar(smc(229), smc(229), EPS, None, ALU.max))
                    vfence(v.reciprocal(smc(230), smc(229)))
                    v.tensor_scalar(smc(231), smc(229), c1_col, 1.0 / 127.0,
                                    ALU.mult, ALU.mult).inc(DV, 1)
                    v.wait(PE, t1_pe[0])
                    v.tensor_scalar(ari(9728, D), arb(3072, D), 127.0, smc(230),
                                    ALU.mult, ALU.mult)
                    v.tensor_copy(arb(4608, D), ari(9728, D)).inc(DV, 1)

                def t1s(c):
                    for k in range(8):
                        bk = k % 2
                        t.wait(DV) if k == 0 else None
                        t.wait(DV, ptfree[bk]) if k >= 2 else None
                        t.transpose(PT[bk][:, 0:128],
                                    arb(4608, D)[:, k * 128:(k + 1) * 128],
                                    IDT[:, :]).inc(PE, 1)
                        v.wait(PE)
                        v.tensor_copy(arb(5120, D)[:, k * 128:(k + 1) * 128],
                                      PT[bk][:, 0:128]).inc(DV, 1)
                        ptfree[bk] = DV.n
                    t1_pe[0] = PE.n

                def mm1(c):
                    gq = GQp(c % 2)
                    t.wait(gch, gval)
                    s.wait(ST, q2free[c % 2])
                    for n in range(8):
                        bank = n % 2
                        t.wait(DV, bankfree[bank])
                        for k in range(8):
                            t.matmul(PS[bank][:, :],
                                     arb(5120, D)[:, k * 128:(k + 1) * 128],
                                     WA[:, k * H + n * 512:k * H + (n + 1) * 512],
                                     start=(k == 0), stop=(k == 7)).inc(PE, 1 if k == 7 else 0)
                        v.wait(PE)
                        v.wait(AC, hevfree[n % 2])
                        v.scalar_tensor_tensor(arf(10752 + (n % 2) * 512, 512),
                                               PS[bank][:, :], smc(231),
                                               BCb1[:, n * 512:(n + 1) * 512],
                                               ALU.mult, ALU.add).inc(DV, 1)
                        bankfree[bank] = DV.n
                        s.wait(DV)
                        s.activation(gq[:, n * 512:(n + 1) * 512],
                                     arf(10752 + (n % 2) * 512, 512),
                                     AF.Gelu).inc(AC, 1)
                        hevfree[n % 2] = AC.n
                    mm1_ac[c] = AC.n

                def back(c):
                    gq = GQp(c % 2)
                    v.wait(AC, mm1_ac[c])
                    rstd2, nmb2 = ln_rstd(lambda j: gq[:, j * 512:(j + 1) * 512],
                                          H, 340)
                    s.wait(DV)
                    s.activation(gq, gq, AF.Identity,
                                 bias=nmb2, scale=rstd2).inc(AC, 1)
                    v.wait(AC)
                    for q4 in range(4):
                        sl = slice(q4 * 1024, (q4 + 1) * 1024)
                        v.tensor_tensor(arb(9728, 1024), gq[:, sl],
                                        BCln2w[:, sl], ALU.mult)
                        v.tensor_tensor(gq[:, sl], arb(9728, 1024),
                                        BCln2b[:, sl], ALU.add)
                    vfence(v.tensor_reduce(smc(298), gq, AX.X, ALU.max,
                                           apply_absolute_value=True))
                    vfence(v.tensor_scalar(smc(298), smc(298), EPS, None, ALU.max))
                    vfence(v.reciprocal(smc(299), smc(298)))
                    v.tensor_scalar(SMALL[:, sout2_base + c:sout2_base + c + 1],
                                    smc(298), c2_col, 1.0 / 127.0,
                                    ALU.mult, ALU.mult).inc(DV, 1)
                    last = None
                    for qq in range(4):
                        v.tensor_scalar(ari(9728, 1024),
                                        gq[:, qq * 1024:(qq + 1) * 1024],
                                        127.0, smc(299), ALU.mult, ALU.mult)
                        last = v.tensor_copy(gq[:, qq * 1024:(qq + 1) * 1024],
                                             ari(9728, 1024))
                    last.inc(DV, 1)
                    for kk in range(8):
                        bk = kk % 2
                        t.wait(DV) if kk == 0 else None
                        t.wait(DV, ptfree[bk]) if kk >= 2 else None
                        for j in range(4):
                            k = kk * 4 + j
                            t.transpose(PT[bk][:, j * 128:(j + 1) * 128],
                                        gq[:, k * 128:(k + 1) * 128],
                                        IDT[:, :]).inc(PE, 1 if j == 3 else 0)
                        v.wait(PE)
                        v.tensor_copy(gq[:, kk * 512:(kk + 1) * 512],
                                      PT[bk][:, 0:512]).inc(DV, 1)
                        ptfree[bk] = DV.n
                    g.wait(DV)
                    g.dma_start(out=slab2(q_img, c), in_=gq).inc(ST, 16)
                    q2free[c % 2] = ST.n

                front(0)
                t1s(0)
                mm1(0)
                for c in range(1, nchunks):
                    front(c)
                    back(c - 1)
                    t1s(c)
                    mm1(c)
                back(nchunks - 1)
                y.wait(ST, ST.n)

            def ffn_B(src_kind, q_img, nchunks, sout2_base, dst_kind, w_gate):
                gch, gval = w_gate
                qfree = [0, 0]
                xfree = [0, 0]
                mmfree = [0, 0]
                for c in range(nchunks):
                    qbuf = 3584 if c % 2 == 0 else 5632
                    xbuf = (c % 2) * 1024
                    y.wait(PE, qfree[c % 2])
                    y.dma_start(out=arb(qbuf, H), in_=slab2(q_img, c)).inc(LD, 16)
                    if src_kind == "x":
                        src = x_halo.ap() if c == 0 else x_own.ap()[(c - 1) * 128:c * 128, :]
                    else:
                        src = slab2(x2_s, c)
                    y.wait(DV, xfree[c % 2])
                    y.dma_start(out=arf(xbuf, D), in_=src).inc(LD, 16)
                    v.wait(LD)
                    v.tensor_scalar(smc(273), SMALL[:, sout2_base + c:sout2_base + c + 1],
                                    0.5, None, ALU.mult).inc(DV, 1)
                    t.wait(gch, gval)
                    t.wait(DV)
                    for n in range(2):
                        bank = 2 + n
                        t.wait(DV, mmfree[n])
                        for k in range(32):
                            t.matmul(PS[bank][:, :],
                                     arb(qbuf, H)[:, k * 128:(k + 1) * 128],
                                     WB_[:, k * D + n * 512:k * D + (n + 1) * 512],
                                     start=(k == 0), stop=(k == 31)).inc(PE, 1 if k == 31 else 0)
                        v.wait(PE)
                        v.scalar_tensor_tensor(arf(10752, 512), PS[bank][:, :],
                                               smc(273),
                                               BCb2h[:, n * 512:(n + 1) * 512],
                                               ALU.mult, ALU.add)
                        v.tensor_tensor(arf(2560 + n * 512, 512), arf(10752, 512),
                                        arf(xbuf + n * 512, 512), ALU.add).inc(DV, 1)
                        mmfree[n] = DV.n
                    qfree[c % 2] = PE.n
                    xfree[c % 2] = DV.n
                    g.wait(DV)
                    if dst_kind == "y1":
                        g.dma_start(out=slab2(y1_s, c), in_=arf(2560, D)).inc(ST, 16)
                    else:
                        g.dma_start(out=out_t.ap()[c * 128:(c + 1) * 128, :],
                                    in_=arf(2560, D)).inc(ST, 16)
                y.wait(ST, ST.n)

            y.wait(DV, dv_w2q)
            ffn_A("x", q1_s, 1 if dbg == "f1a" else nch + 1, 300, (DV, dv_w1q))
            if dbg == "f1a":
                g.wait(DV, DV.n)
                g.wait(AC, AC.n)
                g.dma_start(out=dbg_t.ap()[0:128, 0:D], in_=arb(2048, D)).inc(ST, 16)
                g.dma_start(out=dbg_t.ap()[128:256, 0:D], in_=arb(3072, D)).inc(ST, 16)
                g.dma_start(out=dbg_t.ap()[256:384, 0:D], in_=arb(4608, D)).inc(ST, 16)
                g.dma_start(out=dbg_t.ap()[384:512, :], in_=arb(5632, H)).inc(ST, 16)
                g.dma_start(out=dbg_t.ap()[512:640, :], in_=arb(7680, H)).inc(ST, 16)
                g.dma_start(out=dbg_t.ap()[640:768, :], in_=arb(3584, H)).inc(ST, 16)
                g.dma_start(out=dbg_t.ap()[768:896, 0:120], in_=SMALL[:, 200:320]).inc(ST, 16)
                g.dma_start(out=dbg_t.ap()[896:1024, 0:512], in_=arf(10752, 512)).inc(ST, 16)
                y.wait(ST, ST.n)
                return
            if dbg == "q1":
                for ch in range(8):
                    g.dma_start(out=dbg_t.ap()[ch * 128:(ch + 1) * 128, :],
                                in_=w1q_s.ap()[ch]).inc(ST, 16)
                for ch in range(32):
                    g.dma_start(out=dbg_t.ap()[(8 + ch) * 128:(9 + ch) * 128, 0:D],
                                in_=w2q_s.ap()[ch]).inc(ST, 16)
                g.dma_start(out=dbg_t.ap()[40 * 128:41 * 128, :],
                            in_=slab2(q1_s, 3)).inc(ST, 16)
                g.wait(DV, DV.n)
                g.dma_start(out=dbg_t.ap()[41 * 128:42 * 128, 0:33],
                            in_=SMALL[:, 300:333]).inc(ST, 16)
                y.wait(ST, ST.n)
                return
            ffn_B("x", q1_s, nch + 1, 300, "y1", (DV, dv_w2q))

            if dbg == "y1":
                for c in range(nch + 1):
                    g.dma_start(out=dbg_t.ap()[c * 128:(c + 1) * 128, :],
                                in_=slab2(y1_s, c)).inc(ST, 16)
                y.wait(ST, ST.n)
                return

            pe_after_f1 = PE.n
            g.wait(PE, pe_after_f1)
            g.wait(WL, wl_imgs)
            for ch in range(16):
                g.dma_start(out=WB_[:, ch * E:(ch + 1) * E],
                            in_=WdtT.ap()[ch * 128:(ch + 1) * 128, :]).inc(WG, 16)
            wl_wdt = WG.n
            for ch in range(8):
                g.dma_start(out=WA[:, ch * 4096:(ch + 1) * 4096],
                            in_=WiT.ap()[ch * 128:(ch + 1) * 128, :]).inc(WG, 16)
            wl_wi = WG.n

            m1_ptfree = [0, 0]
            m1_ybuf_gate = [0, 0]
            m1_xhb_gate = [0]

            def m1_ln_chunk(slab_idx, parity, lny_col):
                ybuf = 6208 + parity * 1024
                y.wait(AC, m1_ybuf_gate[parity])
                y.dma_start(out=arf(ybuf, D), in_=slab2(y1_s, slab_idx)).inc(LD, 16)
                v.wait(LD)
                rstdm, nmbm = ln_rstd(lambda j: arf(ybuf + j * 512, 512), D, 200)
                s.wait(DV)
                s.wait(PE, m1_xhb_gate[0])
                s.activation(arb(10304, D), arf(ybuf, D), AF.Identity,
                             bias=nmbm, scale=rstdm).inc(AC, 1)
                m1_ybuf_gate[parity] = AC.n
                t.wait(AC)
                for k in range(8):
                    bk = k % 2
                    if k >= 2:
                        t.wait(DV, m1_ptfree[bk])
                    t.transpose(PT[bk][:, 0:128],
                                arb(10304, D)[:, k * 128:(k + 1) * 128],
                                IDT[:, :]).inc(PE, 1)
                    v.wait(PE)
                    v.tensor_copy(arb(4160, 4096)[:, k * 512 + lny_col:k * 512 + lny_col + 128],
                                  PT[bk][:, 0:128]).inc(DV, 1)
                    m1_ptfree[bk] = DV.n
                m1_xhb_gate[0] = PE.n

            X1R = arb(0, 16 * 520)

            m1_ln_chunk(0, 0, 0)
            t.wait(WG, wl_wi)
            m1_bankfree = [("DV", 0), ("DV", 0), ("DV", 0), ("DV", 0)]

            def bank_wait(eng, bank):
                cname, val = m1_bankfree[bank]
                eng.wait(C[cname], val)

            for gi in range(16):
                bank = gi % 2
                base = gi * 520
                bank_wait(t, bank)
                for k in range(8):
                    t.matmul(PS[bank][:, 0:4],
                             WA[:, k * 4096 + gi * 128:k * 4096 + gi * 128 + 128],
                             arb(4160, 4096)[:, k * 512 + 124:k * 512 + 128],
                             start=(k == 0), stop=(k == 7)).inc(PE, 1 if k == 7 else 0)
                v.wait(PE)
                v.tensor_scalar(X1R[:, base:base + 4], PS[bank][:, 0:4],
                                PRM[:, 112:113], None, ALU.mult).inc(DV, 1)
                m1_bankfree[bank] = ("DV", DV.n)
            last = None
            for gi in range(16):
                base = gi * 520
                for k in range(4):
                    last = v.tensor_scalar(arf(10816 + gi * 4 + k, 1),
                                           X1R[:, base + k:base + k + 1],
                                           PRM[:, 48 + gi * 4 + k:49 + gi * 4 + k],
                                           None, ALU.mult)
            vfence(last)
            for gi in range(16):
                last = v.tensor_reduce(arf(10880 + gi, 1), arf(10816 + gi * 4, 4),
                                       AX.X, ALU.add)
            vfence(last)
            vfence(v.tensor_tensor(arf(10896, 16), arf(10880, 16),
                                   PRM[:, 32:48], ALU.add))
            s.wait(DV)
            s.activation(arf(10928, 16), arf(10896, 16), AF.Silu).inc(AC, 1)
            v.wait(AC)
            XCH = SMALL[:, 386:394].bitcast(BF16)[:, 0:16]
            v.tensor_scalar(XCH, arf(10928, 16), PRM[:, 112:113], None,
                            ALU.mult).inc(DV, 1)

            m1_xco_gate = [0, 0]
            m1_szo_gate = [0, 0]
            m1_cv = [0]
            for st_i in range(nst):
                for i in range(4):
                    m1_ln_chunk(1 + st_i * 4 + i, i % 2, i * 128)
                for gi in range(16):
                    bank = gi % 2
                    base = gi * 520
                    bank_wait(t, bank)
                    for k in range(8):
                        t.matmul(PS[bank][:, :],
                                 WA[:, k * 4096 + gi * 128:k * 4096 + gi * 128 + 128],
                                 arb(4160, 4096)[:, k * 512:(k + 1) * 512],
                                 start=(k == 0), stop=(k == 7)).inc(PE, 1 if k == 7 else 0)
                    v.wait(PE)
                    v.tensor_copy(X1R[:, base + 4:base + 516], PS[bank][:, :]).inc(DV, 1)
                    m1_bankfree[bank] = ("DV", DV.n)
                    v.wait(AC, m1_cv[0])
                    v.tensor_scalar(arf(9280, 512), X1R[:, base + 1:base + 513],
                                    PRM[:, 48 + gi * 4:49 + gi * 4], None, ALU.mult)
                    for k in range(1, 4):
                        dsto = 9792 if k % 2 == 1 else 9280
                        srco = 9280 if k % 2 == 1 else 9792
                        v.scalar_tensor_tensor(arf(dsto, 512),
                                               X1R[:, base + 1 + k:base + 513 + k],
                                               PRM[:, 48 + gi * 4 + k:49 + gi * 4 + k],
                                               arf(srco, 512), ALU.mult, ALU.add)
                    v.tensor_scalar(arf(9280, 512), arf(9792, 512),
                                    PRM[:, 32 + gi:33 + gi], None, ALU.add).inc(DV, 1)
                    v.tensor_copy(X1R[:, base:base + 4],
                                  X1R[:, base + 512:base + 516])
                    xco = 8256 + (gi % 2) * 256
                    s.wait(DV)
                    s.wait(ST, m1_xco_gate[gi % 2])
                    s.activation(arb(xco, 512), arf(9280, 512), AF.Silu).inc(AC, 1)
                    m1_cv[0] = AC.n
                    g.wait(AC)
                    g.dma_start(out=xcb_s.ap()[st_i][:, gi * 512:(gi + 1) * 512],
                                in_=arb(xco, 512)).inc(ST, 16)
                    m1_xco_gate[gi % 2] = ST.n
                    bank_wait(t, bank)
                    for k in range(8):
                        t.matmul(PS[bank][:, :],
                                 WA[:, k * 4096 + 2048 + gi * 128:k * 4096 + 2048 + gi * 128 + 128],
                                 arb(4160, 4096)[:, k * 512:(k + 1) * 512],
                                 start=(k == 0), stop=(k == 7)).inc(PE, 1 if k == 7 else 0)
                    szo = 8768 + (gi % 2) * 256
                    s.wait(PE)
                    s.wait(ST, m1_szo_gate[gi % 2])
                    s.activation(arb(szo, 512), PS[bank][:, :], AF.Silu).inc(AC, 1)
                    m1_bankfree[bank] = ("AC", AC.n)
                    g.wait(AC)
                    g.dma_start(out=slz_s.ap()[st_i][:, gi * 512:(gi + 1) * 512],
                                in_=arb(szo, 512)).inc(ST, 16)
                    m1_szo_gate[gi % 2] = ST.n
            y.wait(ST, ST.n)

            if dbg == "m1":
                for st_i in range(nst):
                    g.dma_start(out=dbg_t.ap()[st_i * 128:(st_i + 1) * 128, :],
                                in_=xcb_s.ap()[st_i]).inc(ST, 16)
                    g.dma_start(out=dbg_t.ap()[(nst + st_i) * 128:(nst + st_i + 1) * 128, :],
                                in_=slz_s.ap()[st_i]).inc(ST, 16)
                y.wait(ST, ST.n)
                return

            pe_after_m1 = PE.n
            g.wait(PE, pe_after_m1)
            for ch in range(16):
                g.dma_start(out=WA[:, ch * E:(ch + 1) * E],
                            in_=WBT.ap()[ch * 128:(ch + 1) * 128, :]).inc(WG, 16)
            wl_wb = WG.n
            UC = [SMALL[:, 402 + i:403 + i] for i in range(16)]
            HC = [SMALL[:, 418 + i:419 + i] for i in range(16)]

            t.wait(WG, wl_wb)
            for gi in range(16):
                bank = gi % 2
                bank_wait(t, bank)
                for ke in range(16):
                    t.matmul(PS[bank][:, 0:1],
                             WA[:, ke * E + gi * 128:ke * E + gi * 128 + 128],
                             XCH[:, ke:ke + 1],
                             start=(ke == 0), stop=(ke == 15)).inc(PE, 1 if ke == 15 else 0)
                t.matmul(PSY[0:128, 0:128], ONES_ROW, ONES_ROW,
                         start=True, stop=True).inc(PE, 1)
                s.wait(PE)
                s.activation(arf(10816 + gi, 1), PS[bank][:, 0:1], AF.Exp,
                             scale=-1.0).inc(AC, 1)
                m1_bankfree[bank] = ("AC", AC.n)
            v.wait(AC)
            vfence(v.tensor_scalar(arf(10832, 16), arf(10816, 16), 1.0, None, ALU.add))
            vfence(v.reciprocal(arf(10848, 16), arf(10832, 16)))
            vfence(v.tensor_tensor(arf(10864, 16), arf(10848, 16), XCH[:, 0:16],
                                   ALU.mult))
            v.tensor_scalar(SMALL[:, 402:418], arf(10864, 16), PRM[:, 112:113],
                            None, ALU.mult)
            v.memset(SMALL[:, 418:434], 0.0).inc(DV, 1)

            if dbg == "wdt":
                g.wait(DV, DV.n)
                g.dma_start(out=dbg_t.ap()[0:128, :], in_=WB_[:, 0:E]).inc(ST, 16)
                g.dma_start(out=dbg_t.ap()[128:256, :], in_=WB_[:, 5 * E:6 * E]).inc(ST, 16)
                g.dma_start(out=dbg_t.ap()[256:384, :], in_=WB_[:, 15 * E:16 * E]).inc(ST, 16)
                g.dma_start(out=dbg_t.ap()[384:512, :], in_=WA[:, 0:E]).inc(ST, 16)
                g.dma_start(out=dbg_t.ap()[512:640, 0:128], in_=PRM[:, :]).inc(ST, 16)
                y.wait(ST, ST.n)
                return

            m2_afree = [0, 0]
            m2_bfree = [0, 0]
            m2_xcb_gate = 0
            m2_bside_dv = [0]
            for st_i in range(nst):
                y.wait(PE, m2_xcb_gate)
                y.wait(DV, m2_bside_dv[0])
                y.dma_start(out=arb(0, 8192), in_=xcb_s.ap()[st_i]).inc(LD, 16)
                for gi in range(16):
                    bank = gi % 2
                    base_a = 8200 + (gi % 2) * 512
                    base_b = 7176 + (gi % 2) * 512
                    t.wait(LD)
                    bank_wait(t, bank)
                    for ke in range(16):
                        t.matmul(PS[bank][:, :],
                                 WB_[:, ke * E + gi * 128:ke * E + gi * 128 + 128],
                                 arb(0, 8192)[:, ke * 512:(ke + 1) * 512],
                                 start=(ke == 0), stop=(ke == 15)).inc(PE, 1 if ke == 15 else 0)
                    t.matmul(PSY[0:128, 0:128], ONES_ROW, ONES_ROW,
                             start=True, stop=True).inc(PE, 1)
                    s.wait(PE)
                    s.activation(arf(4096, 512), PS[bank][:, :], AF.Identity,
                                 bias=PRM[:, 0 + gi:1 + gi]).inc(AC, 1)
                    m1_bankfree[bank] = ("AC", AC.n)
                    s.wait(DV, m2_bside_dv[0])
                    s.activation(arf(4608, 512), arf(4096, 512), AF.Abs)
                    s.activation(arf(5120, 512), arf(4608, 512), AF.Exp, scale=-1.0)
                    s.activation(arf(4608, 512), arf(5120, 512), AF.Ln, bias=1.0)
                    s.activation(arf(5120, 512), arf(4096, 512), AF.Relu).inc(AC, 1)
                    v.wait(AC)
                    v.tensor_tensor(arf(5632, 512), arf(4608, 512), arf(5120, 512),
                                    ALU.add)
                    v.tensor_scalar(arf(4096, 512), arf(5632, 512),
                                    PRM[:, 16 + gi:17 + gi], -1e-4, ALU.mult, ALU.min)
                    v.tensor_scalar(arf(4096, 512), arf(4096, 512), -10.0, None,
                                    ALU.max)
                    v.tensor_reduce(arf(9736 + gi * 4 + st_i, 1), arf(4096, 512),
                                    AX.X, ALU.add).inc(DV, 1)
                    s.wait(DV)
                    s.wait(ST, m2_afree[gi % 2])
                    s.activation(arf(base_a, 512), arf(4096, 512), AF.Exp).inc(AC, 1)
                    if dbg == "m2g" and st_i == 0 and gi == 3:
                        v.wait(AC, AC.n)
                        v.tensor_copy(arf(9224, 512), PS[bank][:, :]).inc(DV, 1)
                        g.wait(DV, DV.n)
                        g.wait(AC, AC.n)
                        g.dma_start(out=dbg_t.ap()[0:128, :],
                                    in_=arf(9224, 512)).inc(ST, 16)
                        g.dma_start(out=dbg_t.ap()[128:256, :],
                                    in_=arf(4096, 512)).inc(ST, 16)
                        for rr in range(16):
                            g.dma_start(out=dbg_t.ap()[(2 + rr) * 128:(3 + rr) * 128, :],
                                        in_=arb(0, 8192)[:, rr * 512:(rr + 1) * 512]).inc(ST, 16)
                        g.dma_start(out=dbg_t.ap()[18 * 128:19 * 128, 0:128],
                                    in_=WB_[:, gi * 128:gi * 128 + 128]).inc(ST, 16)
                        g.dma_start(out=dbg_t.ap()[19 * 128:20 * 128, 0:128],
                                    in_=WB_[:, 15 * E + gi * 128:15 * E + gi * 128 + 128]).inc(ST, 16)
                        t.wait(ST, ST.n)
                        v.wait(ST, ST.n)
                        s.wait(ST, ST.n)
                    bank_wait(t, bank)
                    for ke in range(16):
                        t.matmul(PS[bank][:, :],
                                 WA[:, ke * E + gi * 128:ke * E + gi * 128 + 128],
                                 arb(0, 8192)[:, ke * 512:(ke + 1) * 512],
                                 start=(ke == 0), stop=(ke == 15)).inc(PE, 1 if ke == 15 else 0)
                    t.matmul(PSY[0:128, 0:128], ONES_ROW, ONES_ROW,
                             start=True, stop=True).inc(PE, 1)
                    s.wait(PE)
                    s.activation(arf(4608, 512), PS[bank][:, :], AF.Exp,
                                 scale=-1.0).inc(AC, 1)
                    m1_bankfree[bank] = ("AC", AC.n)
                    v.wait(AC)
                    v.tensor_scalar(arf(4608, 512), arf(4608, 512), 1.0, None, ALU.add)
                    v.reciprocal(arf(5120, 512), arf(4608, 512))
                    v.tensor_copy(arf(6144, 1), UC[gi])
                    vfence(v.tensor_tensor(arf(6145, 512),
                                           arb(0, 8192)[:, gi * 512:(gi + 1) * 512],
                                           arf(5120, 512), ALU.mult))
                    v.tensor_copy(UC[gi], arf(6656, 1))
                    v.tensor_tensor(arf(6664, 512), arf(6144, 512), arf(6145, 512),
                                    ALU.add)
                    s.wait(ST, m2_bfree[gi % 2]) if False else None
                    v.wait(ST, m2_bfree[gi % 2])
                    v.scalar_tensor_tensor(arf(base_b, 512), arf(6664, 512), 0.5,
                                           arf(5632, 512), ALU.mult, ALU.mult).inc(DV, 1)
                    v.wait(AC)
                    vfence(v.tensor_tensor_scan(arf(9224, 512), arf(base_a, 512),
                                                arf(base_b, 512), HC[gi],
                                                ALU.mult, ALU.add))
                    v.tensor_copy(HC[gi], arf(9735, 1)).inc(DV, 1)
                    m2_bside_dv[0] = DV.n
                    g.wait(AC)
                    g.wait(DV)
                    g.dma_start(out=alp_s.ap()[st_i, gi], in_=arf(base_a, 512)).inc(ST, 16)
                    g.dma_start(out=bet_s.ap()[st_i, gi], in_=arf(base_b, 512)).inc(ST, 16)
                    m2_afree[gi % 2] = ST.n
                    m2_bfree[gi % 2] = ST.n
                m2_xcb_gate = PE.n
            y.wait(ST, ST.n)

            last = None
            for gi in range(16):
                last = v.tensor_reduce(arf(9800 + gi, 1), arf(9736 + gi * 4, nst),
                                       AX.X, ALU.add)
            vfence(last)
            s.wait(DV)
            s.activation(arf(9816, 16), arf(9800, 16), AF.Exp).inc(AC, 1)
            last = None
            for gi in range(16):
                last = v.tensor_copy(arf(9832 + gi, 1), HC[gi])
            vfence(last)
            t.wait(DV)
            t.wait(AC)
            t.transpose(PSX[0:16, 0:128], arf(9832, 16), IDF[:, :]).inc(PE, 1)
            v.wait(PE)
            v.tensor_copy(AR[0:16, 9856:9984], PSX[0:16, 0:128]).inc(DV, 1)
            t.transpose(PSY[0:16, 0:128], arf(9816, 16), IDF[:, :]).inc(PE, 1)
            v.wait(PE)
            v.tensor_copy(AR[0:16, 9984:10112], PSY[0:16, 0:128]).inc(DV, 1)
            g.wait(DV)
            g.dma_start(out=cc_in.ap()[0:1, :].rearrange("a (r c) -> (a r) c", c=128),
                        in_=AR[0:16, 9856:9984]).inc(ST, 16)
            g.dma_start(out=cc_in.ap()[1:2, :].rearrange("a (r c) -> (a r) c", c=128),
                        in_=AR[0:16, 9984:10112]).inc(ST, 16)
            g.wait(ST, ST.n)
            g.collective_compute(
                "AllGather", ALU.bypass,
                replica_groups=[list(range(n_cores))],
                ins=[cc_in.ap().opt()],
                outs=[cc_out.ap().opt()],
            ).inc(CCh, 1)
            g.wait(CCh, 1)
            NCr = n_cores
            cco = cc_out.ap().rearrange("(r two) e -> two r e", two=2)
            g.dma_start(out=AR[0:NCr, 0:E], in_=cco[0]).inc(ST, 16)
            g.dma_start(out=AR[0:NCr, 2048:2048 + E], in_=cco[1]).inc(ST, 16)
            y.dma_start(out=AR[0:NCr, 4096:4096 + NCr], in_=mm8.ap()).inc(LD, 16)
            y.dma_start(out=AR[0:NCr, 4200:4201], in_=prefm.ap()).inc(LD, 16)
            s.wait(ST, ST.n)
            s.wait(LD, LD.n)
            s.activation(AR[0:NCr, 6144:6144 + E], AR[0:NCr, 2048:2048 + E],
                         AF.Ln).inc(AC, 1)
            v.wait(AC)
            v.tensor_scalar(AR[0:NCr, 6144:6144 + E], AR[0:NCr, 6144:6144 + E],
                            -1e30, None, ALU.max).inc(DV, 1)
            t.wait(DV)
            for j4 in range(4):
                t.matmul(PSX[0:NCr, 0:512], AR[0:NCr, 4096:4096 + NCr],
                         AR[0:NCr, 6144 + j4 * 512:6144 + (j4 + 1) * 512],
                         start=True, stop=True).inc(PE, 1)
                s.wait(PE)
                s.activation(AR[0:NCr, 8192 + j4 * 512:8192 + (j4 + 1) * 512],
                             PSX[0:NCr, 0:512], AF.Exp).inc(AC, 1)
                t.wait(AC)
            v.wait(AC)
            v.tensor_tensor(AR[0:NCr, 8192:8192 + E], AR[0:NCr, 8192:8192 + E],
                            AR[0:NCr, 0:E], ALU.mult).inc(DV, 1)
            t.wait(DV)
            for j4 in range(4):
                t.matmul(PSX[0:1, 0:512], AR[0:NCr, 4200:4201],
                         AR[0:NCr, 8192 + j4 * 512:8192 + (j4 + 1) * 512],
                         start=True, stop=True).inc(PE, 1)
                v.wait(PE)
                v.tensor_copy(AR[0:1, 6144 + j4 * 512:6144 + (j4 + 1) * 512],
                              PSX[0:1, 0:512]).inc(DV, 1)
                t.wait(DV)
            for gi in range(16):
                bank = gi % 2
                pso = PSX if bank == 0 else PSY
                t.transpose(pso[:, 0:1],
                            AR[0:1, 6144 + gi * 128:6144 + (gi + 1) * 128],
                            IDF[0:1, 0:1]).inc(PE, 1)
                v.wait(PE)
                v.tensor_copy(SMALL[:, 448 + gi:449 + gi], pso[:, 0:1]).inc(DV, 1)
                t.wait(DV) if gi >= 1 else None
            HSC = [SMALL[:, 448 + i:449 + i] for i in range(16)]
            dv_after_combine = DV.n

            if dbg == "m2":
                for st_i in range(nst):
                    for gi in range(16):
                        g.dma_start(out=dbg_t.ap()[(st_i * 16 + gi) * 128:(st_i * 16 + gi + 1) * 128, :],
                                    in_=alp_s.ap()[st_i, gi]).inc(ST, 16)
                        g.dma_start(out=dbg_t.ap()[(nst * 16 + st_i * 16 + gi) * 128:(nst * 16 + st_i * 16 + gi + 1) * 128, :],
                                    in_=bet_s.ap()[st_i, gi]).inc(ST, 16)
                base_r = 2 * nst * 16 * 128
                g.wait(DV, DV.n)
                g.dma_start(out=dbg_t.ap()[base_r:base_r + 128, 0:16], in_=arf(9832, 16)).inc(ST, 16)
                g.dma_start(out=dbg_t.ap()[base_r + 128:base_r + 256, 0:16], in_=arf(9816, 16)).inc(ST, 16)
                g.dma_start(out=dbg_t.ap()[base_r + 256:base_r + 384, 0:16], in_=SMALL[:, 448:464]).inc(ST, 16)
                y.wait(ST, ST.n)
                return

            pe_after_m2 = PE.n
            g.wait(PE, pe_after_m2)
            for ch in range(16):
                g.dma_start(out=WB_[:, ch * E:(ch + 1) * E],
                            in_=WCT.ap()[ch * 128:(ch + 1) * 128, :]).inc(WG, 16)
            wl_wc = WG.n
            for ch in range(16):
                g.dma_start(out=WA[:, ch * 1024:(ch + 1) * 1024],
                            in_=WoT.ap()[ch * 128:(ch + 1) * 128, :]).inc(WG, 16)
            wl_wo = WG.n

            HC2 = HC
            m3_xcb_gate = None
            m3_ain_gate = [dv_after_combine, dv_after_combine]
            m3_slz_gate = [dv_after_combine, dv_after_combine]
            m3_st_gate = ST.n
            t.wait(WG, wl_wc)
            for st_i in range(nst):
                if m3_xcb_gate is None:
                    y.wait(DV, dv_after_combine)
                else:
                    y.wait(PE, m3_xcb_gate)
                y.dma_start(out=arb(0, 8192), in_=xcb_s.ap()[st_i]).inc(LD, 16)
                for gi in range(16):
                    bank = gi % 2
                    abuf = 8192 + (gi % 2) * 512
                    bbuf = 9216 + (gi % 2) * 512
                    zbuf = 10240 + (gi % 2) * 256
                    y.wait(DV, m3_ain_gate[gi % 2])
                    y.wait(ST, m3_st_gate)
                    y.dma_start(out=arf(abuf, 512), in_=alp_s.ap()[st_i, gi]).inc(LD, 16)
                    y.dma_start(out=arf(bbuf, 512), in_=bet_s.ap()[st_i, gi]).inc(LD, 16)
                    y.wait(DV, m3_slz_gate[gi % 2])
                    y.dma_start(out=arb(zbuf, 512),
                                in_=slz_s.ap()[st_i][:, gi * 512:(gi + 1) * 512]).inc(LD, 16)
                    t.wait(LD)
                    bank_wait(t, bank)
                    for ke in range(16):
                        t.matmul(PS[bank][:, :],
                                 WB_[:, ke * E + gi * 128:ke * E + gi * 128 + 128],
                                 arb(0, 8192)[:, ke * 512:(ke + 1) * 512],
                                 start=(ke == 0), stop=(ke == 15)).inc(PE, 1 if ke == 15 else 0)
                    s.wait(PE)
                    s.activation(arf(11264, 512), PS[bank][:, :], AF.Tanh).inc(AC, 1)
                    m1_bankfree[bank] = ("AC", AC.n)
                    init = HSC[gi] if st_i == 0 else HC2[gi]
                    v.wait(LD)
                    vfence(v.tensor_tensor_scan(arf(10752, 512), arf(abuf, 512),
                                                arf(bbuf, 512), init,
                                                ALU.mult, ALU.add))
                    v.tensor_copy(HC2[gi], arf(11263, 1))
                    v.wait(AC)
                    v.tensor_tensor(arf(bbuf, 512), arf(10752, 512), arf(11264, 512),
                                    ALU.mult)
                    v.tensor_tensor(arb(4096, 8192)[:, gi * 512:(gi + 1) * 512],
                                    arf(bbuf, 512), arb(zbuf, 512), ALU.mult).inc(DV, 1)
                    m3_ain_gate[gi % 2] = DV.n
                    m3_slz_gate[gi % 2] = DV.n
                t.wait(WG, wl_wo)
                t.wait(DV)
                for j in range(4):
                    bank = 2 + (j % 2)
                    y.wait(DV, DV.n)
                    y.dma_start(out=arf(8192, D),
                                in_=slab2(y1_s, 1 + st_i * 4 + j)).inc(LD, 16)
                    for n in range(2):
                        bank_wait(t, bank)
                        for ke in range(16):
                            t.matmul(PS[bank][:, :],
                                     arb(4096, 8192)[:, ke * 512 + j * 128:ke * 512 + j * 128 + 128],
                                     WA[:, ke * 1024 + n * 512:ke * 1024 + (n + 1) * 512],
                                     start=(ke == 0), stop=(ke == 15)).inc(PE, 1 if ke == 15 else 0)
                        v.wait(PE)
                        v.wait(LD)
                        v.tensor_tensor(arf(9216 + n * 512, 512), PS[bank][:, :],
                                        arf(8192 + n * 512, 512), ALU.add).inc(DV, 1)
                        m1_bankfree[bank] = ("DV", DV.n)
                    g.wait(DV)
                    g.dma_start(out=slab2(x2_s, st_i * 4 + j), in_=arf(9216, D)).inc(ST, 16)
                m3_xcb_gate = PE.n
                m3_ain_gate = [DV.n, DV.n]
                m3_st_gate = ST.n
            y.wait(ST, ST.n)

            if dbg == "x2":
                for c in range(nch):
                    g.dma_start(out=dbg_t.ap()[c * 128:(c + 1) * 128, :],
                                in_=slab2(x2_s, c)).inc(ST, 16)
                y.wait(ST, ST.n)
                return

            pe_after_m3 = PE.n
            first = True
            for ch in range(8):
                if first:
                    y.wait(PE, pe_after_m3)
                    y.wait(WL, wl_imgs)
                    first = False
                y.dma_start(out=WA[:, ch * H:(ch + 1) * H], in_=w1q_s.ap()[ch]).inc(WL, 16)
            wl_f2a = WL.n
            for ch in range(32):
                y.dma_start(out=WB_[:, ch * D:(ch + 1) * D], in_=w2q_s.ap()[ch]).inc(WL, 16)
            wl_f2b = WL.n
            ffn_A("x2", q2_s, nch, 318, (WL, wl_f2a))
            ffn_B("x2", q2_s, nch, 318, "out", (WL, wl_f2b))
            y.wait(ST, ST.n)

        def run_walk(eng):
            def f(handle):
                P.cur = eng
                P.handles[eng] = handle
                for cc_ in chans.values():
                    cc_.n = 0
                emit(P)
            return f

        for eng in ENGINES:
            getattr(block, eng)(run_walk(eng))

    return nc


def _prep_inputs(inputs, n_cores, tpc):
    f32 = np.float32
    x = np.ascontiguousarray(np.asarray(inputs["L0_attn_x"]).reshape(-1, D).astype(f32))
    common = {
        "WiT": np.ascontiguousarray(np.asarray(inputs["L0_attn_Wi"]).T.astype(f32)),
        "WdtT": np.ascontiguousarray(np.asarray(inputs["L0_attn_Wdt"]).T.astype(f32)),
        "WBT": np.ascontiguousarray(np.asarray(inputs["L0_attn_WB"]).T.astype(f32)),
        "WCT": np.ascontiguousarray(np.asarray(inputs["L0_attn_WC"]).T.astype(f32)),
        "WoT": np.ascontiguousarray(np.asarray(inputs["L0_attn_Wo"]).T.astype(f32)),
        "w1T": np.ascontiguousarray(np.asarray(inputs["L0_ffn_w1"]).T.astype(f32)),
        "w2T": np.ascontiguousarray(np.asarray(inputs["L0_ffn_w2"]).T.astype(f32)),
        "ln1w_r": np.asarray(inputs["L0_ffn_ln1_w"]).reshape(1, D).astype(f32),
        "ln1b_r": np.asarray(inputs["L0_ffn_ln1_b"]).reshape(1, D).astype(f32),
        "b1_r": np.asarray(inputs["L0_ffn_b1"]).reshape(1, H).astype(f32),
        "ln2w_r": np.asarray(inputs["L0_ffn_ln2_w"]).reshape(1, H).astype(f32),
        "ln2b_r": np.asarray(inputs["L0_ffn_ln2_b"]).reshape(1, H).astype(f32),
        "b2_r": np.asarray(inputs["L0_ffn_b2"]).reshape(1, D).astype(f32),
        "bdt_c": np.ascontiguousarray(
            np.asarray(inputs["L0_attn_bdt"]).reshape(16, 128).T.astype(f32)),
        "alog_c": np.ascontiguousarray(
            np.asarray(inputs["L0_attn_A_log"]).reshape(16, 128).T.astype(f32)),
        "bconv_c": np.ascontiguousarray(
            np.asarray(inputs["L0_attn_bconv"]).reshape(16, 128).T.astype(f32)),
        "wconv_c": np.ascontiguousarray(
            np.asarray(inputs["L0_attn_Wconv"]).reshape(16, 128, 4)
            .transpose(1, 0, 2).reshape(128, 64).astype(f32)),
        "ones_r": np.ones((1, 128), f32),
        "ones_c": np.ones((128, 1), f32),
        "idt_b": np.eye(128, dtype=f32).astype(ml_dtypes.bfloat16),
        "idf_f": np.eye(128, dtype=f32),
    }
    ims = []
    for i in range(n_cores):
        st = i * tpc
        im = dict(common)
        im["x_own"] = x[st:st + tpc]
        im["x_halo"] = x[st - 128:st] if i > 0 else np.ascontiguousarray(x[0:128])
        im["halo_m"] = np.full((128, 1), 0.0 if i == 0 else 1.0, f32)
        pm = np.zeros((n_cores, 1), f32)
        pm[:i] = 1.0
        im["prefm"] = pm
        mm = np.zeros((n_cores, n_cores), f32)
        for j in range(i):
            for k in range(j + 1, i):
                mm[k, j] = 1.0
        im["mm8"] = mm
        ims.append(im)
    return ims


def kernel(**inputs):
    n_cores = 8
    tpc = L_FULL // n_cores
    nc = build_nc(n_cores, tpc)
    ims = _prep_inputs(inputs, n_cores, tpc)
    res = run_bass_kernel_spmd(nc, ims, core_ids=list(range(n_cores)))
    out = np.concatenate([res.results[i]["out"] for i in range(n_cores)], axis=0)
    return out.reshape(1, L_FULL, D)
